# version 05
# baseline (speedup 1.0000x reference)
import numpy as np
import ml_dtypes
import concourse.bass as bass
import concourse.mybir as mybir
from concourse.bass_utils import run_bass_kernel_spmd
from concourse.aot_env import allow_ici_all_to_all
from contextlib import ExitStack

F32 = mybir.dt.float32
BF16 = mybir.dt.bfloat16
U8 = mybir.dt.uint8
I32 = mybir.dt.int32
AF = mybir.ActivationFunctionType
ALU = mybir.AluOpType
AX = mybir.AxisListType

D = 2048
L = 8192
C = 1024
NCH = 8
HD = 512
NH = 4
HID = 5632
NHB = HID // 128
BLRI = float(0.01 + np.log(-np.expm1(-0.01)))
NS_COEFFS = [(4.0848, -6.8946, 2.927), (3.9505, -6.3029, 2.6377),
             (3.7418, -5.5913, 2.3037), (2.8769, -3.1427, 1.2046),
             (2.8366, -3.0525, 1.2012)]
S128 = 1.0 / float(np.sqrt(128.0))
BLK = 10240 * 1024
R_QFM, R_KFM, R_VFM, R_KTM, R_VTM = 0, 2048, 4096, 6144, 8192

_DEBUG = False
STAGE = 4
N_CHUNKS = NCH
NS_IT = 5

BF = ml_dtypes.bfloat16


class Sems:

    _DRAIN_ENGINES = None

    def __init__(self, nc, es):
        self.nc, self.es = nc, es
        self.h = {}
        self.cnt = {}
        self.eng_map = {
            mybir.EngineType.DVE: nc.vector,
            mybir.EngineType.Activation: nc.scalar,
            mybir.EngineType.Pool: nc.gpsimd,
        }

    def __getitem__(self, name):
        if name not in self.h:
            self.h[name] = self.es.enter_context(self.nc.semaphore(name))
            self.cnt[name] = 0
        return self.h[name]

    def inc(self, instr, name, n=1, raw=False):
        et = getattr(getattr(instr, "ins", None), "engine", None)
        if (not raw and n < 16 and et in self.eng_map
                and type(instr.ins).__name__ != "InstTriggeredCopy"):
            instr = self.eng_map[et].drain()
        instr.then_inc(self[name], n)
        self.cnt[name] += n
        return self.cnt[name]

    def val(self, name):
        _ = self[name]
        return self.cnt[name]

    def wait(self, eng, name, v):
        eng.wait_ge(self[name], v)


def build():
    nc = bass.Bass("TRN2", target_bir_lowering=False,
                   detect_race_conditions=False)

    x_in = nc.dram_tensor("x", [C, D], F32, kind="ExternalInput")
    Wqkv = nc.dram_tensor("Wqkv", [D, 3 * D], F32, kind="ExternalInput")
    WprojI = nc.dram_tensor("Wproj", [D, 24], F32, kind="ExternalInput")
    WgT = nc.dram_tensor("WgT", [HID, D], U8, kind="ExternalInput")
    WuT = nc.dram_tensor("WuT", [HID, D], U8, kind="ExternalInput")
    WdT = nc.dram_tensor("WdT", [HID, D], U8, kind="ExternalInput")
    WoI = nc.dram_tensor("Wo", [D, D], F32, kind="ExternalInput")
    wms = {}
    for nm in ("w0m", "w0mT", "w1m", "w1mT", "w2m", "w2mT"):
        wms[nm] = nc.dram_tensor(nm, [HD, HD], F32, kind="ExternalInput")
    sgI = nc.dram_tensor("sg", [128, NHB], F32, kind="ExternalInput")
    suI = nc.dram_tensor("su", [128, NHB], F32, kind="ExternalInput")
    sdBI = nc.dram_tensor("sdB", [128, D], F32, kind="ExternalInput")
    ongI = nc.dram_tensor("ong", [128, 4], F32, kind="ExternalInput")
    qkpI = nc.dram_tensor("qkp", [128, 64], F32, kind="ExternalInput")
    bmI = nc.dram_tensor("bm1", [128, 1], F32, kind="ExternalInput")
    bsI = nc.dram_tensor("bs1", [128, 1], F32, kind="ExternalInput")
    pbI = nc.dram_tensor("pb1", [128, 1], F32, kind="ExternalInput")
    dynI = nc.dram_tensor("dyn", [1, 64], I32, kind="ExternalInput")

    out_t = nc.dram_tensor("out", [C, D], F32, kind="ExternalOutput")
    dbg = {}
    if _DEBUG:
        dbg["d_xnfm"] = nc.dram_tensor("d_xnfm", [D, C], BF16, kind="ExternalOutput")
        dbg["d_ag"] = nc.dram_tensor("d_ag", [10240, 1024], BF16, kind="ExternalOutput")
        dbg["d_g"] = nc.dram_tensor("d_g", [C, D], BF16, kind="ExternalOutput")
        dbg["d_attn"] = nc.dram_tensor("d_attn", [D, C], BF16, kind="ExternalOutput")
        dbg["d_ttt"] = nc.dram_tensor("d_ttt", [8 * HD, C], BF16, kind="ExternalOutput")
        dbg["d_proj"] = nc.dram_tensor("d_proj", [C, 24], F32, kind="ExternalOutput")
        dbg["d_W0"] = nc.dram_tensor("d_W0", [128, 4 * HD], F32, kind="ExternalOutput")
        dbg["d_M0"] = nc.dram_tensor("d_M0", [128, 4 * HD], BF16, kind="ExternalOutput")
        dbg["d_X0"] = nc.dram_tensor("d_X0", [128, 4 * HD], BF16, kind="ExternalOutput")
        dbg["d_A0"] = nc.dram_tensor("d_A0", [128, 4 * HD], BF16, kind="ExternalOutput")
        dbg["d_kv"] = nc.dram_tensor("d_kv", [128, 5 * 4096], BF16, kind="ExternalOutput")
        dbg["d_tm"] = nc.dram_tensor("d_tm", [128, 16384], BF16, kind="ExternalOutput")
        dbg["d_pj"] = nc.dram_tensor("d_pj", [128, 96], F32, kind="ExternalOutput")

    ag_in = nc.dram_tensor("ag_in", [10240, 1024], BF16)
    ag_out = nc.dram_tensor("ag_out", [8 * 10240, 1024], BF16)
    pj_in = nc.dram_tensor("pj_in", [C, 24], F32)
    pj_out = nc.dram_tensor("pj_out", [8 * C, 24], F32)
    ta_in = nc.dram_tensor("ta_in", [8 * HD, C], BF16)
    ta_out = nc.dram_tensor("ta_out", [8 * HD, C], BF16)
    attnscr = nc.dram_tensor("attnscr", [D, C], BF16)
    blk_own = nc.dram_tensor("blk_own", [10240, 1024], BF16)
    blk_halo = nc.dram_tensor("blk_halo", [10240, 1024], BF16)
    qsc = nc.dram_tensor("qsc", [8 * HD, 1024], BF16)
    ksc = nc.dram_tensor("ksc", [8 * HD, 1024], BF16)
    vsc = nc.dram_tensor("vsc", [8 * HD, 1024], BF16)
    ktsc = nc.dram_tensor("ktsc", [8 * C, HD], BF16)
    vtsc = nc.dram_tensor("vtsc", [8 * C, HD], BF16)
    pjsc = nc.dram_tensor("pjsc", [8 * C, 5], F32)
    gscr = nc.dram_tensor("gscr", [C, D], BF16)
    rowscr = nc.dram_tensor("rowscr", [8, 1024], F32)

    ident_i = nc.inline_tensor(np.eye(128, dtype=BF), "identc")
    ones128_i = nc.inline_tensor(np.ones((128, 128), dtype=BF), "ones128c")
    onescol_i = nc.inline_tensor(np.ones((128, 1), dtype=BF), "onescolc")
    onesrow_i = nc.inline_tensor(np.ones((1, 128), dtype=BF), "onesrowc")
    m128r_i = nc.inline_tensor(np.full((1, 128), -128.0, dtype=BF), "m128rc")
    m128r512_i = nc.inline_tensor(np.full((1, 512), -128.0, dtype=BF), "m128r512c")
    jj = np.arange(512)[None, :]
    pp = np.arange(128)[:, None]
    mk = [(pp + d_ >= jj).astype(BF) for d_ in (0, 128, 256, 384)]
    mk += [(jj >= pp + d_).astype(BF) for d_ in (0, 128, 256, 384)]
    masks_i = nc.inline_tensor(np.concatenate(mk, axis=1), "masksc")
    dummy_i = nc.inline_tensor(np.zeros((1, 8), np.float32), "dummyc")
    CONSTS = [1e-5, 1.0, 1e-7, BLRI, 1e-10, 0.5, 2.0, 0.0]
    consts_i = nc.inline_tensor(
        np.broadcast_to(np.asarray(CONSTS, np.float32)[None, :],
                        (128, len(CONSTS))).copy(), "constsc")

    es = ExitStack()
    sb = lambda n, s, d: es.enter_context(nc.sbuf_tensor(n, s, d))
    S = Sems(nc, es)
    NENG = 5
    phase = [0]

    with es:
        s_ident = sb("s_ident", [128, 128], BF16)
        s_ones128 = sb("s_ones128", [128, 128], BF16)
        s_onescol = sb("s_onescol", [128, 1], BF16)
        s_onesrow = sb("s_onesrow", [1, 128], BF16)
        s_m128r = sb("s_m128r", [1, 128], BF16)
        s_m128r512 = sb("s_m128r512", [1, 512], BF16)
        s_sg = sb("s_sg", [128, NHB], F32)
        s_su = sb("s_su", [128, NHB], F32)
        s_sdB = sb("s_sdB", [128, D], F32)
        s_ong = sb("s_ong", [128, 4], F32)
        s_qkp = sb("s_qkp", [128, 64], F32)
        s_bm = sb("s_bm", [128, 1], F32)
        s_bs = sb("s_bs", [128, 1], F32)
        s_pb = sb("s_pb", [128, 1], F32)
        s_cs_hi = sb("s_cs_hi", [1, 1024], BF16)
        s_cs_lo = sb("s_cs_lo", [1, 1024], BF16)
        s_cs_f = sb("s_cs_f", [1, 1024], F32)
        s_consts = sb("s_consts", [128, 8], F32)
        for j, v in enumerate(CONSTS):
            nc.const_aps.register(F32, v, s_consts[:, j:j + 1])

        pb = [es.enter_context(nc.psum_tensor(f"psb{i}", [128, 512], F32))
              for i in range(8)]

        s_dummy = sb("s_dummy", [1, 8], F32)

        def eng_enter(eng):
            eng.wait_ge(S["ph"], NENG * phase[0])

        def leave(eng):
            import concourse.mybir as _mb
            nm = eng.engine if hasattr(eng, "engine") else None
            if nm == _mb.EngineType.DVE:
                eng.tensor_copy(s_dummy[:, 0:4], s_dummy[:, 4:8]).then_inc(
                    S["ph"], 1)
            elif nm == _mb.EngineType.Activation:
                eng.copy(s_dummy[:, 0:4], s_dummy[:, 4:8]).then_inc(S["ph"], 1)
            else:
                eng.nop().then_inc(S["ph"], 1)
            S.cnt["ph"] += 1

        def barrier():
            phase[0] += 1

        pxn = ExitStack()
        s_xnfm = pxn.enter_context(
            nc.sbuf_tensor("s_xnfm", [128, 16 * C], BF16, side="right"))
        p01 = ExitStack()
        psb = lambda n, s, d: p01.enter_context(nc.sbuf_tensor(n, s, d))
        s_xin = [psb(f"s_xin{i}", [128, D], F32) for i in range(2)]
        s_xntm = psb("s_xntm", [128, D], BF16)
        s_sqscr = psb("s_sqscr", [128, D], F32)
        s_nm = psb("s_nm", [128, 4], F32)
        s_ws = psb("s_ws", [128, 16 * D], BF16)
        s_wsf = [psb(f"s_wsf{i}", [128, D], F32) for i in range(2)]
        s_stg = [psb(f"s_stg{i}", [128, 1024], BF16) for i in range(2)]
        s_stg2 = [psb(f"s_stg2{i}", [128, 2048], BF16) for i in range(2)]
        s_pjw = psb("s_pjw", [128, 16 * 24], BF16)
        s_pjwf = psb("s_pjwf", [128, 16 * 24], F32)
        s_pjs = [psb(f"s_pjs{i}", [128, 24], F32) for i in range(2)]
        ptr = [pb[4 + i][:, 0:64].bitcast(BF16) for i in range(4)]

        EV = []
        for st_ in range(3):
            EV += [("fm", st_, cb_) for cb_ in range(16)]
            if st_ >= 1:
                EV += [("tm", st_ - 1, tt_) for tt_ in range(8)]
                if st_ == 2:
                    ev2 = []
                    for tt_ in range(8):
                        ev2.append(("tm", 1, tt_))
                        ev2.append(("pj", 1, tt_))
                    EV = EV[:-8] + ev2
        POS = {e: p for p, e in enumerate(EV)}
        NSTAGE = len(EV)

        _sc_p1 = nc.enter_named_scope("p1", False)[0]
        with nc.Block() as blk:
            @blk.sync
            def _(sync):
                for dst, src in ((s_ident, ident_i), (s_ones128, ones128_i),
                                 (s_onescol, onescol_i), (s_onesrow, onesrow_i),
                                 (s_m128r, m128r_i), (s_m128r512, m128r512_i),
                                 (s_sg, sgI), (s_su, suI), (s_sdB, sdBI),
                                 (s_ong, ongI), (s_qkp, qkpI), (s_bm, bmI),
                                 (s_bs, bsI), (s_pb, pbI),
                                 (s_consts, consts_i),
                                 (s_dummy, dummy_i)):
                    S.inc(sync.dma_start(out=dst[:, :], in_=src[:, :]),
                          "cst", 16)
                for kd in range(16):
                    S.inc(sync.dma_start(
                        out=s_pjwf[:, 24 * kd:24 * (kd + 1)],
                        in_=WprojI[128 * kd:128 * (kd + 1), :]), "cst", 16)
                for tt in range(8):
                    if tt >= 2:
                        S.wait(sync, "lnt", tt - 1)
                    S.inc(sync.dma_start(out=s_xin[tt % 2][:, :],
                                         in_=x_in[128 * tt:128 * (tt + 1), :]),
                          "xld", 16)
                for st in range(3):
                    for kd in range(16):
                        n = 16 * st + kd
                        if n >= 2:
                            S.wait(sync, "wcv", n - 1)
                        S.inc(sync.dma_start(
                            out=s_wsf[kd % 2][:, :],
                            in_=Wqkv[128 * kd:128 * (kd + 1),
                                     2048 * st:2048 * (st + 1)]), "wld", 16)
                sync.wait_ge(S["xld"], 8 * 16)
                sync.wait_ge(S["wld"], 48 * 16)
                leave(sync)

            @blk.vector
            def _(vector):
                vector.wait_ge(S["cst"], 32 * 16)
                vector.tensor_copy(s_pjw[:, :], s_pjwf[:, :])
                for tt in range(8):
                    vector.wait_ge(S["xld"], 16 * min(tt + 2, 8))
                    xb = s_xin[tt % 2]
                    vector.reduce_sum(s_nm[:, 0:1], xb[:, :], axis=AX.X,
                                      negate=True)
                    vector.drain()
                    vector.tensor_scalar(s_nm[:, 1:2], s_nm[:, 0:1], 1.0 / D,
                                         None, ALU.mult)
                    vector.drain()
                    i = vector.tensor_scalar(xb[:, :], xb[:, :], s_nm[:, 1:2],
                                             None, ALU.add)
                    S.inc(i, "lnc", 1)
                    S.wait(vector, "lns", tt + 1)
                    vector.reciprocal(s_nm[:, 2:3], s_nm[:, 3:4])
                    vector.drain()
                    if tt >= 1:
                        S.wait(vector, "trp", 16 * tt)
                    i = vector.tensor_scalar(s_xntm[:, :], xb[:, :],
                                             s_nm[:, 2:3], None, ALU.mult)
                    S.inc(i, "lnt", 1)
                    for kd in range(16):
                        n = 16 * tt + kd
                        S.wait(vector, "trp", n + 1)
                        i = vector.tensor_copy(
                            s_xnfm[:, C * kd + 128 * tt:C * kd + 128 * (tt + 1)],
                            ptr[n % 4])
                        S.inc(i, "tre", 1)
                for st in range(3):
                    for kd in range(16):
                        n = 16 * st + kd
                        vector.wait_ge(S["wld"], 16 * min(n + 2, 48))
                        if st >= 1 and kd == 0:
                            S.wait(vector, "fmm", 16 * st)
                            if st >= 2:
                                S.wait(vector, "tmm", 8 * (st - 1))
                        i = vector.tensor_copy(s_ws[:, D * kd:D * (kd + 1)],
                                               s_wsf[kd % 2][:, :])
                        S.inc(i, "wcv", 1)
                S.wait(vector, "cse", 1)
                vector.tensor_copy(s_cs_hi[:, :], s_cs_f[:, :])
                vector.drain()
                i = vector.tensor_tensor(s_cs_lo[:, :], s_cs_f[:, :],
                                         s_cs_hi[:, :], ALU.subtract)
                S.inc(i, "csd", 1)
                leave(vector)

            @blk.scalar
            def _(scalar):
                for tt in range(8):
                    S.wait(scalar, "lnc", tt + 1)
                    xb = s_xin[tt % 2]
                    scalar.activation(s_sqscr[:, :], xb[:, :], AF.Square,
                                      accum_out=s_nm[:, 3:4])
                    scalar.drain()
                    i = scalar.activation(s_nm[:, 3:4], s_nm[:, 3:4], AF.Sqrt,
                                          bias=1e-5, scale=1.0 / D)
                    S.inc(i, "lns", 1)
                for p, ev in enumerate(EV):
                    if p >= 2:
                        S.wait(scalar, "agw", 16 * p)
                    kind = ev[0]
                    if kind == "fm":
                        _, st, cb = ev
                        n = 16 * st + cb
                        g = (cb % 2) * 2
                        S.wait(scalar, "fmm", n + 1)
                        sg_ = s_stg[cb % 2]
                        scalar.activation(sg_[:, 0:512], pb[g][:, :], AF.Copy)
                        i = scalar.activation(sg_[:, 512:1024],
                                              pb[g + 1][:, :], AF.Copy)
                        S.inc(i, "evq", 1)
                        if st == 0 and cb == 15:
                            S.wait(scalar, "csm", 1)
                            scalar.activation(s_cs_f[:, 0:512],
                                              pb[4][0:1, 0:512], AF.Copy)
                            i = scalar.activation(s_cs_f[:, 512:1024],
                                                  pb[5][0:1, 0:512], AF.Copy)
                            S.inc(i, "cse", 1)
                    elif kind == "tm":
                        _, sti, tt = ev
                        n = 8 * sti + tt
                        S.wait(scalar, "tmm", n + 1)
                        sg_ = s_stg2[tt % 2]
                        for j in range(4):
                            i = scalar.activation(sg_[:, 512 * j:512 * (j + 1)],
                                                  pb[4 + j][:, :], AF.Copy)
                        S.inc(i, "evq", 1)
                    else:
                        _, _, tt = ev
                        i = scalar.activation(s_pjs[tt % 2][:, :],
                                              pb[3][:, 0:24], AF.Copy)
                        S.inc(i, "evq", 1)
                leave(scalar)

            @blk.tensor
            def _(tensor):
                tensor.wait_ge(S["cst"], 32 * 16)
                for tt in range(8):
                    S.wait(tensor, "lnt", tt + 1)
                    for kd in range(16):
                        n = 16 * tt + kd
                        if n >= 4:
                            S.wait(tensor, "tre", n - 3)
                        i = tensor.transpose(ptr[n % 4],
                                             s_xntm[:, 128 * kd:128 * (kd + 1)],
                                             s_ident[:, :])
                        S.inc(i, "trp", 1)
                S.wait(tensor, "tre", 128)
                for st in range(3):
                    S.wait(tensor, "wcv", 16 * (st + 1))
                    for cb in range(16):
                        n = 16 * st + cb
                        g = (cb % 2) * 2
                        if n >= 2:
                            S.wait(tensor, "evq",
                                   POS[("fm", (n - 2) // 16, (n - 2) % 16)] + 1)
                        for kd in range(16):
                            lhs = s_ws[:, D * kd + 128 * cb:D * kd + 128 * (cb + 1)]
                            for tb in range(2):
                                i = tensor.matmul(
                                    pb[g + tb][:, :], lhs,
                                    s_xnfm[:, C * kd + 512 * tb:
                                           C * kd + 512 * (tb + 1)],
                                    start=(kd == 0), stop=(kd == 15))
                        S.inc(i, "fmm", 1)
                    if st == 0:
                        for kd in range(16):
                            for tb in range(2):
                                i = tensor.matmul(
                                    pb[4 + tb][0:1, 0:512], s_onescol[:, :],
                                    s_xnfm[:, C * kd + 512 * tb:
                                           C * kd + 512 * (tb + 1)],
                                    start=(kd == 0), stop=(kd == 15))
                        S.inc(i, "csm", 1)
                    else:
                        sti = st - 1
                        if st == 1:
                            S.wait(tensor, "cse", 1)
                        if st == 2:
                            S.wait(tensor, "evq", POS[("fm", 2, 15)] + 1)
                        for tt in range(8):
                            n = 8 * sti + tt
                            if n >= 1:
                                pv = (POS[("tm", 0, tt - 1)] if sti == 0 or tt >= 1
                                      else POS[("tm", 0, 7)])
                                if sti == 1 and tt >= 1:
                                    pv = POS[("tm", 1, tt - 1)]
                                elif sti == 1 and tt == 0:
                                    pv = POS[("tm", 0, 7)]
                                S.wait(tensor, "evq", pv + 1)
                            if st == 2 and tt >= 1:
                                S.wait(tensor, "evq", POS[("pj", 1, tt - 1)] + 1)
                            for kd in range(16):
                                lhs = s_xnfm[:, C * kd + 128 * tt:
                                             C * kd + 128 * (tt + 1)]
                                for j in range(4):
                                    i = tensor.matmul(
                                        pb[4 + j][:, :], lhs,
                                        s_ws[:, D * kd + 512 * j:
                                             D * kd + 512 * (j + 1)],
                                        start=(kd == 0), stop=(kd == 15))
                                if st == 2:
                                    i = tensor.matmul(
                                        pb[3][:, 0:24], lhs,
                                        s_pjw[:, 24 * kd:24 * (kd + 1)],
                                        start=(kd == 0), stop=(kd == 15))
                            S.inc(i, "tmm", 1)
                leave(tensor)

            @blk.gpsimd
            def _(gpsimd):
                for p, ev in enumerate(EV):
                    S.wait(gpsimd, "evq", p + 1)
                    kind = ev[0]
                    if kind == "fm":
                        _, st, cb = ev
                        base = (R_QFM, R_KFM, R_VFM)[st]
                        S.inc(gpsimd.dma_start(
                            out=ag_in[base + 128 * cb:base + 128 * (cb + 1), :],
                            in_=s_stg[cb % 2][:, :]), "agw", 16)
                    elif kind == "tm":
                        _, sti, tt = ev
                        base = R_KTM if sti == 0 else R_VTM
                        dst = bass.AP(ag_in, base * 1024 + 2048 * 128 * tt,
                                      [[2048, 128], [1, 2048]])
                        S.inc(gpsimd.dma_start(out=dst,
                                               in_=s_stg2[tt % 2][:, :]),
                              "agw", 16)
                    else:
                        _, _, tt = ev
                        S.inc(gpsimd.dma_start(
                            out=pj_in[128 * tt:128 * (tt + 1), :],
                            in_=s_pjs[tt % 2][:, :]), "agw", 16)
                gpsimd.wait_ge(S["agw"], S.val("agw"))
                if _DEBUG:
                    S.inc(gpsimd.dma_start(out=dbg["d_ag"].ap(),
                                           in_=ag_in.ap()), "dbgw", 16)
                    for kd in range(16):
                        S.inc(gpsimd.dma_start(
                            out=dbg["d_xnfm"][128 * kd:128 * (kd + 1), :],
                            in_=s_xnfm[:, C * kd:C * (kd + 1)]), "dbgw", 16)
                    gpsimd.wait_ge(S["dbgw"], S.val("dbgw"))
                leave(gpsimd)

        barrier()
        p01.close()

        p2 = ExitStack()
        s_hid = p2.enter_context(nc.sbuf_tensor("s_hid", [128, NHB * C], BF16))
        p2a = ExitStack()
        s_wu8 = [p2a.enter_context(nc.sbuf_tensor(f"s_wu8_{i}", [128, 2048], U8,
                                                  side="right"))
                 for i in range(4)]
        s_wgu = p2a.enter_context(
            nc.sbuf_tensor("s_wgu", [128, 2 * 4096], BF16, side="right"))
        s_tg = p2a.enter_context(
            nc.sbuf_tensor("s_tg", [128, 2 * C], F32, side="right"))

        nc.leave_named_scope("p1", _sc_p1, False)
        _sc_p2a = nc.enter_named_scope("p2a", False)[0]
        with nc.Block() as blk:
            @blk.sync
            def _(sync):
                eng_enter(sync)
                for hb in range(NHB):
                    if hb >= 2:
                        S.wait(sync, "gcv", 2 * (hb - 1))
                    S.inc(sync.dma_start(out=s_wu8[2 * (hb % 2)][:, :],
                                         in_=WgT[128 * hb:128 * (hb + 1), :]),
                          "gld", 16)
                    S.inc(sync.dma_start(out=s_wu8[2 * (hb % 2) + 1][:, :],
                                         in_=WuT[128 * hb:128 * (hb + 1), :]),
                          "gld", 16)
                leave(sync)

            @blk.vector
            def _(vector):
                eng_enter(vector)
                for hb in range(NHB):
                    vector.wait_ge(S["gld"], 32 * min(hb + 2, NHB))
                    if hb >= 2:
                        S.wait(vector, "gmm", hb - 1)
                    o = (hb % 2) * 4096
                    vector.tensor_copy(s_wgu[:, o:o + 2048],
                                       s_wu8[2 * (hb % 2)][:, :])
                    i = vector.tensor_copy(s_wgu[:, o + 2048:o + 4096],
                                           s_wu8[2 * (hb % 2) + 1][:, :])
                    S.inc(i, "gcv", 2)
                    S.wait(vector, "ges", hb + 1)
                    g0 = (hb % 2) * 4
                    to = (hb % 2) * C
                    for tb in range(2):
                        i = vector.scalar_tensor_tensor(
                            s_hid[:, C * hb + 512 * tb:C * hb + 512 * (tb + 1)],
                            pb[g0 + 2 + tb][:, :], s_su[:, hb:hb + 1],
                            s_tg[:, to + 512 * tb:to + 512 * (tb + 1)],
                            ALU.mult, ALU.mult)
                    S.inc(i, "hev", 1)
                leave(vector)

            @blk.scalar
            def _(scalar):
                eng_enter(scalar)
                for hb in range(NHB):
                    S.wait(scalar, "gmm", hb + 1)
                    if hb >= 2:
                        S.wait(scalar, "hev", hb - 1)
                    g0 = (hb % 2) * 4
                    to = (hb % 2) * C
                    for tb in range(2):
                        i = scalar.activation(s_tg[:, to + 512 * tb:
                                                   to + 512 * (tb + 1)],
                                              pb[g0 + tb][:, :], AF.Silu,
                                              scale=s_sg[:, hb:hb + 1])
                    S.inc(i, "ges", 1)
                leave(scalar)

            @blk.tensor
            def _(tensor):
                eng_enter(tensor)
                S.wait(tensor, "csd", 1)
                for hb in range(NHB):
                    S.wait(tensor, "gcv", 2 * (hb + 1))
                    g0 = (hb % 2) * 4
                    o = (hb % 2) * 4096
                    if hb >= 2:
                        S.wait(tensor, "hev", hb - 1)
                    for kd in range(16):
                        lg = s_wgu[:, o + 128 * kd:o + 128 * (kd + 1)]
                        lu = s_wgu[:, o + 2048 + 128 * kd:o + 2048 + 128 * (kd + 1)]
                        for tb in range(2):
                            rhs = s_xnfm[:, C * kd + 512 * tb:C * kd + 512 * (tb + 1)]
                            tensor.matmul(pb[g0 + tb][:, :], lg, rhs,
                                          start=(kd == 0), stop=False)
                            tensor.matmul(pb[g0 + 2 + tb][:, :], lu, rhs,
                                          start=(kd == 0), stop=False)
                    for tb in range(2):
                        cs_h = s_cs_hi[:, 512 * tb:512 * (tb + 1)]
                        cs_l = s_cs_lo[:, 512 * tb:512 * (tb + 1)]
                        tensor.matmul(pb[g0 + tb][:, :], s_m128r[:, :], cs_h,
                                      start=False, stop=False)
                        tensor.matmul(pb[g0 + tb][:, :], s_m128r[:, :], cs_l,
                                      start=False, stop=True)
                        tensor.matmul(pb[g0 + 2 + tb][:, :], s_m128r[:, :], cs_h,
                                      start=False, stop=False)
                        i = tensor.matmul(pb[g0 + 2 + tb][:, :], s_m128r[:, :],
                                          cs_l, start=False, stop=True)
                    S.inc(i, "gmm", 1)
                leave(tensor)

            @blk.gpsimd
            def _(gpsimd):
                eng_enter(gpsimd)
                leave(gpsimd)

        barrier()
        p2a.close()
        pxn.close()

        s_wd = p2.enter_context(nc.sbuf_tensor("s_wd", [128, NHB * 1024], BF16))
        s_wd8 = [p2.enter_context(nc.sbuf_tensor(f"s_wd8_{i}", [128, 1024], U8))
                 for i in range(2)]
        s_hsum = s_cs_hi
        s_hsumlo = s_cs_lo
        s_hsf = s_cs_f
        s_dt = p2.enter_context(nc.sbuf_tensor("s_dt", [128, C], F32))
        s_gstg = [p2.enter_context(nc.sbuf_tensor(f"s_gstg{i}", [128, C], BF16))
                  for i in range(2)]

        nc.leave_named_scope("p2a", _sc_p2a, False)
        _sc_p2b = nc.enter_named_scope("p2b", False)[0]
        with nc.Block() as blk:
            @blk.sync
            def _(sync):
                eng_enter(sync)
                for half in range(2):
                    for kh in range(NHB):
                        ik = half * NHB + kh
                        if ik >= 2:
                            S.wait(sync, "dcv", ik - 1)
                        S.inc(sync.dma_start(
                            out=s_wd8[ik % 2][:, :],
                            in_=WdT[128 * kh:128 * (kh + 1),
                                    1024 * half:1024 * (half + 1)]), "dld", 16)
                leave(sync)

            @blk.vector
            def _(vector):
                eng_enter(vector)
                S.wait(vector, "hsf", 1)
                vector.tensor_copy(s_hsum[:, :], s_hsf[:, :])
                vector.drain()
                i = vector.tensor_tensor(s_hsumlo[:, :], s_hsf[:, :],
                                         s_hsum[:, :], ALU.subtract)
                S.inc(i, "hse", 1)
                for half in range(2):
                    for kh in range(NHB):
                        ik = half * NHB + kh
                        vector.wait_ge(S["dld"], 16 * min(ik + 2, 2 * NHB))
                        if half == 1 and kh == 0:
                            S.wait(vector, "dmm", 8)
                        i = vector.tensor_copy(s_wd[:, 1024 * kh:1024 * (kh + 1)],
                                               s_wd8[ik % 2][:, :])
                        S.inc(i, "dcv", 1)
                    for tt in range(8):
                        it = half * 8 + tt
                        g0 = (it % 2) * 2
                        S.wait(vector, "dmm", it + 1)
                        if it >= 1:
                            S.wait(vector, "gsp", it)
                        for nb in range(2):
                            i = vector.tensor_tensor(
                                s_dt[:, 512 * nb:512 * (nb + 1)],
                                pb[g0 + nb][:, :],
                                s_sdB[:, 1024 * half + 512 * nb:
                                      1024 * half + 512 * (nb + 1)],
                                ALU.mult)
                        S.inc(i, "dsc", 1)
                leave(vector)

            @blk.tensor
            def _(tensor):
                eng_enter(tensor)
                for hb in range(NHB):
                    for tb in range(2):
                        i = tensor.matmul(
                            pb[6 + tb][0:1, 0:512], s_onescol[:, :],
                            s_hid[:, C * hb + 512 * tb:C * hb + 512 * (tb + 1)],
                            start=(hb == 0), stop=(hb == NHB - 1))
                S.inc(i, "hsm", 1)
                for half in range(2):
                    S.wait(tensor, "dcv", NHB * (half + 1))
                    for tt in range(8):
                        it = half * 8 + tt
                        g0 = (it % 2) * 2
                        if it >= 2:
                            S.wait(tensor, "dsc", it - 1)
                        for kh in range(NHB):
                            lhs = s_hid[:, C * kh + 128 * tt:C * kh + 128 * (tt + 1)]
                            for nb in range(2):
                                tensor.matmul(
                                    pb[g0 + nb][:, :], lhs,
                                    s_wd[:, 1024 * kh + 512 * nb:
                                         1024 * kh + 512 * (nb + 1)],
                                    start=(kh == 0), stop=False)
                        S.wait(tensor, "hse", 1)
                        for nb in range(2):
                            tensor.matmul(pb[g0 + nb][:, :],
                                          s_hsum[0:1, 128 * tt:128 * (tt + 1)],
                                          s_m128r512[:, :],
                                          start=False, stop=False)
                            i = tensor.matmul(
                                pb[g0 + nb][:, :],
                                s_hsumlo[0:1, 128 * tt:128 * (tt + 1)],
                                s_m128r512[:, :], start=False, stop=True)
                        S.inc(i, "dmm", 1)
                leave(tensor)

            @blk.scalar
            def _(scalar):
                eng_enter(scalar)
                S.wait(scalar, "hsm", 1)
                scalar.activation(s_hsf[0:1, 0:512], pb[6][0:1, 0:512], AF.Copy)
                i = scalar.activation(s_hsf[0:1, 512:1024], pb[7][0:1, 0:512],
                                      AF.Copy)
                S.inc(i, "hsf", 1)
                for half in range(2):
                    for tt in range(8):
                        it = half * 8 + tt
                        S.wait(scalar, "dsc", it + 1)
                        if it >= 2:
                            S.wait(scalar, "gwr", 16 * it)
                        i = scalar.activation(s_gstg[it % 2][:, :],
                                              s_dt[:, :], AF.Silu)
                        S.inc(i, "gsp", 1)
                leave(scalar)

            @blk.gpsimd
            def _(gpsimd):
                eng_enter(gpsimd)
                i = gpsimd.collective_compute(
                    "AllGather", ALU.bypass, replica_groups=[list(range(8))],
                    ins=[ag_in.ap().opt()], outs=[ag_out.ap().opt()])
                S.inc(i, "agc", 1, raw=True)
                i = gpsimd.collective_compute(
                    "AllGather", ALU.bypass, replica_groups=[list(range(8))],
                    ins=[pj_in.ap().opt()], outs=[pj_out.ap().opt()])
                S.inc(i, "agc", 1, raw=True)
                for half in range(2):
                    for tt in range(8):
                        it = half * 8 + tt
                        S.wait(gpsimd, "gsp", it + 1)
                        dst = bass.AP(gscr, (128 * tt) * D + 1024 * half,
                                      [[D, 128], [1, 1024]])
                        S.inc(gpsimd.dma_start(out=dst, in_=s_gstg[it % 2][:, :]),
                              "gwr", 16)
                gpsimd.wait_ge(S["gwr"], S.val("gwr"))
                if _DEBUG:
                    S.inc(gpsimd.dma_start(out=dbg["d_g"].ap(), in_=gscr.ap()),
                          "dbgw", 16)
                    gpsimd.wait_ge(S["dbgw"], S.val("dbgw"))
                leave(gpsimd)

        barrier()
        p2.close()

        if STAGE >= 3:
            for nm_ in ("aex_p", "asc_p", "aex_t", "amk_t"):
                S.cnt[nm_] = 0

        p3 = ExitStack()
        q3 = lambda n, sh, dt_: p3.enter_context(nc.sbuf_tensor(n, sh, dt_))
        s_aq = [q3(f"s_aq{i}", [128, 1024], BF16) for i in range(2)]
        s_ak = [q3(f"s_ak{i}", [128, 2048], BF16) for i in range(2)]
        s_av = [q3(f"s_av{i}", [128, 2048], BF16) for i in range(2)]
        s_E = q3("s_E", [128, 16 * 512], BF16)
        s_masks = q3("s_masks", [128, 8 * 512], BF16)
        s_sq3 = q3("s_sq3", [128, 1024], BF16)
        s_r3 = q3("s_r3", [128, 8], F32)
        s_rw3 = q3("s_rw3", [1, 3072], F32)
        s_rc = q3("s_rc", [1, 8], F32)
        s_rcb = q3("s_rcb", [1, 8], BF16)
        s_rinv = q3("s_rinv", [128, 512], F32)
        s_at = [q3(f"s_at{i}", [128, 1024], BF16) for i in range(2)]

        def kt_plan(qb):
            plan = []
            qoff = 512 * qb
            for kt in range(8):
                d_ = 128 * kt - qoff
                if d_ >= 512 - 1:
                    plan.append((kt, None))
                elif d_ > -128:
                    plan.append((kt, (0, 128, 256, 384).index(d_)))
            for kt in range(8, 16):
                d_ = 128 * (kt - 8) - qoff
                if d_ <= -128:
                    plan.append((kt, None))
                elif d_ < 512:
                    plan.append((kt, 4 + (0, 128, 256, 384).index(d_)))
            return plan

        PLANS = [kt_plan(0), kt_plan(1)]
        NMSK = [sum(1 for _, mi in p if mi is not None) for p in PLANS]

        nc.leave_named_scope("p2b", _sc_p2b, False)
        _sc_p3 = nc.enter_named_scope("p3", False)[0]
        with nc.Block() as blk:
            @blk.gpsimd
            def _(gpsimd):
                eng_enter(gpsimd)
                S.inc(gpsimd.dma_start(out=s_masks[:, :], in_=masks_i[:, :]),
                      "a3m", 16)
                gpsimd.wait_ge(S["agc"], 2)
                with gpsimd.register("r_halo") as r_halo_reg:
                    gpsimd.reg_load(r_halo_reg, dynI[0:1, 1:2])
                    halo = gpsimd.snap(r_halo_reg, min_val=0,
                                       max_val=10240 * 7)
                    S.inc(gpsimd.dma_start(
                        out=blk_halo[0:2048, :],
                        in_=ag_out[bass.ds(halo + R_KFM, 2048), :],
                        bounds_check=None), "stg", 16)
                    S.inc(gpsimd.dma_start(
                        out=blk_halo[2048:4096, :],
                        in_=ag_out[bass.ds(halo + R_VTM, 2048), :],
                        bounds_check=None), "stg", 16)
                gpsimd.wait_ge(S["stg"], 32)
                for a in range(16):
                    if a >= 2:
                        S.wait(gpsimd, "apv", 2 * (a - 1))
                    S.inc(gpsimd.dma_start(
                        out=s_aq[a % 2][:, :],
                        in_=ag_in[R_QFM + 128 * a:R_QFM + 128 * (a + 1), :]),
                        "a3l", 16)
                    S.inc(gpsimd.dma_start(
                        out=s_ak[a % 2][:, 0:1024],
                        in_=blk_halo[128 * a:128 * (a + 1), :]),
                        "a3l", 16)
                    S.inc(gpsimd.dma_start(
                        out=s_ak[a % 2][:, 1024:2048],
                        in_=ag_in[R_KFM + 128 * a:R_KFM + 128 * (a + 1), :]),
                        "a3l", 16)
                    for src, ro, fo in ((blk_halo, 2048 * 1024, 0),
                                        (ag_in, R_VTM * 1024, 1024)):
                        S.inc(gpsimd.dma_start(
                            out=s_av[a % 2][:, fo:fo + 1024]
                            .rearrange("p (tk c) -> p tk c", c=128),
                            in_=bass.AP(src, ro + 128 * a,
                                        [[2048, 128], [2048 * 128, 8],
                                         [1, 128]])), "a3l", 16)
                gpsimd.wait_ge(S["a3w"], 16 * 16)
                if _DEBUG:
                    S.inc(gpsimd.dma_start(out=dbg["d_attn"].ap(),
                                           in_=attnscr.ap()), "dbgw", 16)
                    gpsimd.wait_ge(S["dbgw"], S.val("dbgw"))
                leave(gpsimd)

            @blk.vector
            def _(vector):
                eng_enter(vector)
                for a in range(16):
                    vector.wait_ge(S["a3l"], 5 * 16 * min(a + 2, 16))
                    aq, ak = s_aq[a % 2], s_ak[a % 2]
                    vector.tensor_scalar(aq[:, :], aq[:, :],
                                         s_qkp[:, a:a + 1], None, ALU.mult)
                    vector.tensor_scalar(aq[:, :], aq[:, :],
                                         s_qkp[:, 32 + a:33 + a], None, ALU.add)
                    vector.tensor_scalar(ak[:, :], ak[:, :],
                                         s_qkp[:, 16 + a:17 + a], None,
                                         ALU.mult)
                    i = vector.tensor_scalar(ak[:, :], ak[:, :],
                                             s_qkp[:, 48 + a:49 + a], None,
                                             ALU.add)
                    S.inc(i, "a3s", 1)
                    S.wait(vector, "a3r", 3 * (a + 1))
                    vector.reduce_max(s_rc[0:1, 0:1], s_rw3[0:1, 0:1024],
                                      axis=AX.X)
                    vector.reduce_max(s_rc[0:1, 1:2], s_rw3[0:1, 1024:2048],
                                      axis=AX.X)
                    vector.reduce_max(s_rc[0:1, 2:3], s_rw3[0:1, 2048:3072],
                                      axis=AX.X)
                    vector.drain()
                    vector.tensor_max(s_rc[0:1, 1:2], s_rc[0:1, 1:2],
                                      s_rc[0:1, 2:3])
                    vector.drain()
                    i = vector.tensor_mul(s_rc[0:1, 2:3], s_rc[0:1, 0:1],
                                          s_rc[0:1, 1:2])
                    S.inc(i, "a3c", 1)
                    for qb in range(2):
                        plan = PLANS[qb]
                        S.wait(vector, "aex", S.val("aex_p") + len(plan))
                        S.cnt["aex_p"] = S.val("aex_p") + len(plan)
                        for pi, (kt, mi) in enumerate(plan):
                            if mi is not None:
                                i = vector.tensor_mul(
                                    s_E[:, 512 * kt:512 * (kt + 1)],
                                    s_E[:, 512 * kt:512 * (kt + 1)],
                                    s_masks[:, 512 * mi:512 * (mi + 1)])
                                S.inc(i, "amk", 1)
                        S.wait(vector, "apv", 2 * a + qb + 1)
                        if a >= 2 and qb == 0:
                            S.wait(vector, "a3w", 16 * (a - 1))
                        vector.reciprocal(s_rinv[:, :], pb[1][:, :])
                        vector.drain()
                        i = vector.tensor_tensor(
                            s_at[a % 2][:, 512 * qb:512 * (qb + 1)],
                            pb[0][:, :], s_rinv[:, :], ALU.mult)
                        S.inc(i, "a3o", 1)
                leave(vector)

            @blk.scalar
            def _(scalar):
                eng_enter(scalar)
                for a in range(16):
                    S.wait(scalar, "a3s", a + 1)
                    scalar.activation(s_sq3[:, :], s_aq[a % 2][:, :], AF.Square)
                    scalar.copy(s_dummy[:, 0:4], s_dummy[:, 4:8]).then_inc(
                        S["a3q"], 1)
                    S.cnt["a3q"] += 1
                    for rnd in range(3):
                        S.wait(scalar, "a3t", 3 * a + rnd + 1)
                        ro = 1024 * rnd
                        scalar.activation(s_rw3[0:1, ro:ro + 512],
                                          pb[7][0:1, 0:512], AF.Copy)
                        i = scalar.activation(s_rw3[0:1, ro + 512:ro + 1024],
                                              pb[6][0:1, 0:512], AF.Copy)
                        S.inc(i, "a3r", 1)
                        if rnd < 2:
                            i = scalar.activation(
                                s_sq3[:, :],
                                s_ak[a % 2][:, 1024 * rnd:1024 * (rnd + 1)],
                                AF.Square)
                            i.then_inc(S["a3q"], 1)
                            S.cnt["a3q"] += 1
                    S.wait(scalar, "a3c", a + 1)
                    scalar.activation(s_rc[0:1, 3:4], s_rc[0:1, 2:3], AF.Sqrt)
                    scalar.drain()
                    i = scalar.activation(s_rcb[0:1, 0:1], s_rc[0:1, 3:4],
                                          AF.Copy, scale=-S128)
                    S.inc(i, "a3cs", 1)
                    S.wait(scalar, "a3bb", a + 1)
                    scalar.activation(s_r3[:, 0:1], pb[6][:, 0:1], AF.Copy)
                    i = scalar.activation(s_r3[:, 1:2], pb[6][:, 0:1],
                                          AF.Identity, bias=s_pb[:, 0:1])
                    S.inc(i, "a3b", 1)
                    for qb in range(2):
                        plan = PLANS[qb]
                        for pi, (kt, mi) in enumerate(plan):
                            S.wait(scalar, "asc", S.val("asc_p") + pi + 1)
                            bias_ap = (s_r3[:, 1:2] if kt < 8 else s_r3[:, 0:1])
                            i = scalar.activation(
                                s_E[:, 512 * kt:512 * (kt + 1)],
                                pb[4 + (pi % 2)][:, :], AF.Exp,
                                bias=bias_ap, scale=S128)
                            S.inc(i, "aex", 1)
                        S.cnt["asc_p"] = S.val("asc_p") + len(plan)
                leave(scalar)

            @blk.tensor
            def _(tensor):
                eng_enter(tensor)
                S.wait(tensor, "a3m", 16)
                for a in range(16):
                    for rnd in range(3):
                        S.wait(tensor, "a3q", 3 * a + rnd + 1)
                        if 3 * a + rnd >= 1:
                            S.wait(tensor, "a3r", 3 * a + rnd)
                        for tb in range(2):
                            i = tensor.matmul(
                                (pb[7] if tb == 0 else pb[6])[0:1, 0:512],
                                s_onescol[:, :],
                                s_sq3[:, 512 * tb:512 * (tb + 1)],
                                start=True, stop=True)
                        S.inc(i, "a3t", 1)
                    S.wait(tensor, "a3cs", a + 1)
                    S.wait(tensor, "a3r", 3 * (a + 1))
                    i = tensor.matmul(pb[6][:, 0:1], s_onesrow[:, :],
                                      s_rcb[0:1, 0:1], start=True, stop=True)
                    S.inc(i, "a3bb", 1)
                    S.wait(tensor, "a3b", a + 1)
                    for qb in range(2):
                        plan = PLANS[qb]
                        for pi, (kt, mi) in enumerate(plan):
                            if pi >= 2:
                                S.wait(tensor, "aex", S.val("aex_t") + pi - 1)
                            i = tensor.matmul(
                                pb[4 + (pi % 2)][:, :],
                                s_ak[a % 2][:, 128 * kt:128 * (kt + 1)],
                                s_aq[a % 2][:, 512 * qb:512 * (qb + 1)],
                                start=True, stop=True)
                            S.inc(i, "asc", 1)
                        S.wait(tensor, "aex", S.val("aex_t") + len(plan))
                        S.cnt["aex_t"] = S.val("aex_t") + len(plan)
                        S.wait(tensor, "amk", S.val("amk_t") + NMSK[qb])
                        S.cnt["amk_t"] = S.val("amk_t") + NMSK[qb]
                        if 2 * a + qb >= 1:
                            S.wait(tensor, "a3o", 2 * a + qb)
                        for pi, (kt, mi) in enumerate(plan):
                            tensor.matmul(
                                pb[0][:, :],
                                s_av[a % 2][:, 128 * kt:128 * (kt + 1)],
                                s_E[:, 512 * kt:512 * (kt + 1)],
                                start=(pi == 0), stop=(pi == len(plan) - 1))
                        for pi, (kt, mi) in enumerate(plan):
                            i = tensor.matmul(
                                pb[1][:, :], s_ones128[:, :],
                                s_E[:, 512 * kt:512 * (kt + 1)],
                                start=(pi == 0), stop=(pi == len(plan) - 1))
                        S.inc(i, "apv", 1)
                leave(tensor)

            @blk.sync
            def _(sync):
                eng_enter(sync)
                if STAGE >= 4:
                    sync.wait_ge(S["agc"], 2)
                    ag8g = ag_out.ap().rearrange("(i r) c -> i r c", i=8)
                    ag8bg = ag_out.ap().rearrange("(i r) (a b) -> i (r a) b",
                                                  i=8, b=512)
                    pj8g = pj_out.ap().rearrange("(i r) c -> i r c", i=8)
                    with sync.register("r_hs4") as r_hs4:
                        sync.reg_load(r_hs4, dynI[0:1, 4:5])
                        hrow = sync.snap(r_hs4, min_val=0, max_val=HD * 3)
                        for dst, rbase in ((qsc, R_QFM), (ksc, R_KFM),
                                           (vsc, R_VFM)):
                            S.inc(sync.dma_start(
                                out=dst.ap().rearrange("(i r) c -> i r c",
                                                       i=8),
                                in_=ag8g[0:8][:, bass.ds(hrow + rbase, HD), :],
                                bounds_check=None), "stg4", 16)
                        for dst, rbase in ((ktsc, R_KTM * 2),
                                           (vtsc, R_VTM * 2)):
                            sync.reg_load(r_hs4, dynI[0:1, 5:6])
                            hh = sync.snap(r_hs4, min_val=0, max_val=3)
                            S.inc(sync.dma_start(
                                out=dst.ap().rearrange("(i r) c -> i r c",
                                                       i=8),
                                in_=ag8bg[0:8][:, bass.ds(hh + rbase, 1024, 4),
                                               :],
                                bounds_check=None), "stg4", 16)
                        sync.reg_load(r_hs4, dynI[0:1, 6:7])
                        hp = sync.snap(r_hs4, min_val=0, max_val=18)
                        S.inc(sync.dma_start(
                            out=pjsc.ap().rearrange("(i r) c -> i r c", i=8),
                            in_=pj8g[0:8][:, :, bass.ds(hp, 5)],
                            bounds_check=None), "stg4", 16)
                        sync.wait_ge(S["stg4"], S.val("stg4"))
                for a in range(16):
                    S.wait(sync, "a3o", 2 * (a + 1))
                    S.inc(sync.dma_start(
                        out=attnscr[128 * a:128 * (a + 1), :],
                        in_=s_at[a % 2][:, :]), "a3w", 16)
                sync.wait_ge(S["a3w"], 16 * 16)
                leave(sync)

        barrier()
        p3.close()

        if STAGE >= 4:

            p4 = ExitStack()
            q4 = lambda n, sh, dt_: p4.enter_context(nc.sbuf_tensor(n, sh, dt_))
            s_W0 = q4("s_W0", [128, 2048], F32)
            s_W1 = q4("s_W1", [128, 2048], F32)
            s_W2 = q4("s_W2", [128, 2048], F32)
            s_M = [q4(f"s_M{c}", [128, 2048], BF16) for c in range(3)]
            s_MT = [q4(f"s_MT{c}", [128, 2048], BF16) for c in range(3)]
            s_W0Tb = q4("s_W0Tb", [128, 2048], BF16)
            s_W1Tb = q4("s_W1Tb", [128, 2048], BF16)
            s_W2Tb = q4("s_W2Tb", [128, 2048], BF16)
            s_W1b = q4("s_W1b", [128, 2048], BF16)
            s_n0 = q4("s_n0", [128, 12], F32)
            s_rn = q4("s_rn", [128, 8], F32)
            s_qf = q4("s_qf", [128, 4096], BF16)
            s_kf = q4("s_kf", [128, 4096], BF16)
            s_vf = q4("s_vf", [128, 4096], BF16)
            s_ktm = q4("s_ktm", [128, 4096], BF16)
            s_vtm = q4("s_vtm", [128, 4096], BF16)
            s_hl1 = q4("s_hl1", [128, 4096], BF16)
            s_dgl0 = q4("s_dgl0", [128, 4096], BF16)
            s_dhl2 = q4("s_dhl2", [128, 4096], BF16)
            s_gh = q4("s_gh", [128, 4096], BF16)
            s_out4 = q4("s_out4", [128, 4096], BF16)
            s_sq4 = q4("s_sq4", [128, 2048], BF16)
            s_scr4 = q4("s_scr4", [128, 1024], F32)
            s_scr4b = q4("s_scr4b", [128, 4096], BF16)
            s_t1 = [q4(f"s_t1_{i}", [128, 512], F32) for i in range(2)]
            s_t2 = [q4(f"s_t2_{i}", [128, 512], F32) for i in range(2)]
            s_sg1 = [q4(f"s_sg1_{i}", [128, 512], F32) for i in range(2)]
            s_dsg1 = [q4(f"s_dsg1_{i}", [128, 512], F32) for i in range(2)]
            s_hb1 = [q4(f"s_hb1_{i}", [128, 512], F32) for i in range(2)]
            s_dh1 = [q4(f"s_dh1_{i}", [128, 512], F32) for i in range(2)]
            s_tx = [q4("s_tx_0", [128, 512], F32)] * 2
            s_proj = q4("s_proj", [128, 40], F32)
            s_l0 = q4("s_l0", [128, 8], F32)
            s_l1 = q4("s_l1", [128, 8], F32)
            s_l2 = q4("s_l2", [128, 8], F32)
            s_mo = q4("s_mo", [128, 8], BF16)
            s_sc4 = q4("s_sc4", [128, 8], F32)
            s_mmv = q4("s_mmv", [128, 1], F32)
            s_rktm = q4("s_rktm", [128, 8], F32)
            s_c4 = q4("s_c4", [1, 16], F32)
            s_c4b = q4("s_c4b", [1, 16], BF16)
            s_srow = s_cs_f
            s_rwb = s_cs_hi
            s_rw4 = s_sdB[0:1, 0:1024]
            s_rq = s_sdB[:, 1024:1536].bitcast(BF16)
            s_fac = s_sdB[:, 1536:2048].bitcast(BF16)
            s_wca = None
            s_tts = None
            s_wca = s_out4[:, 0:2048]
            s_tts = s_scr4b[:, 2048:4096]
            s_X = [s_dgl0[:, 0:2048], s_hl1[:, 0:2048], s_dhl2[:, 0:2048]]
            s_XT = [s_dgl0[:, 2048:4096], s_hl1[:, 2048:4096],
                    s_dhl2[:, 2048:4096]]
            s_A = [s_gh[:, 0:2048], s_gh[:, 2048:4096], s_sq4[:, 0:2048]]

            SCH = []
            P4MAP = {"pv": "p4v", "ps": "p4s", "tg": "p4g", "p4d": "p4d",
                     "p4n": "p4n"}
            CNT = {"tg": 0, "pv": 0, "ps": 0, "p4d": 0}
            BANK = [None] * 8

            def t_mm(emit, deps, ee, emit_ev, evdeps=()):
                CNT["tg"] += 1
                seq = CNT["tg"]
                bank = (seq - 1) % 8
                bdep = BANK[bank]
                sem = "pv" if ee == "v" else "ps"
                CNT[sem] += 1
                ecnt = CNT[sem]
                BANK[bank] = (sem, ecnt)
                SCH.append(("mm", emit, tuple(deps), ee, emit_ev,
                            tuple(evdeps), seq, bank, ecnt, bdep))
                return ("tg", seq), (sem, ecnt)

            def t_op(eng, emit, deps=(), force=False):
                sem = "pv" if eng == "v" else "ps"
                CNT[sem] += 1
                SCH.append(("op", eng, emit, tuple(deps), CNT[sem], force))
                return (sem, CNT[sem])

            def t_dma(emit, deps=()):
                CNT["p4d"] += 16
                SCH.append(("dma", emit, tuple(deps), CNT["p4d"]))
                return ("p4d", CNT["p4d"])

            def mk_slice_dma(dst, off, src, rows):
                def f(g, dst=dst, off=off, src=src, rows=rows):
                    return g.dma_start(out=dst[:, off:off + src.shape[1]]
                                       if False else dst,
                                       in_=src[rows[0]:rows[1], :])
                return f

            NSC = NS_COEFFS

            def emit_chunk(ci):
                lastcons = CNT["tg"]
                dd = []
                for j in range(4):
                    dd.append(t_dma(lambda g, j=j, ci=ci: g.dma_start(
                        out=s_qf[:, 1024 * j:1024 * (j + 1)],
                        in_=qsc[HD * ci + 128 * j:HD * ci + 128 * (j + 1), :])))
                    dd.append(t_dma(lambda g, j=j, ci=ci: g.dma_start(
                        out=s_kf[:, 1024 * j:1024 * (j + 1)],
                        in_=ksc[HD * ci + 128 * j:HD * ci + 128 * (j + 1), :])))
                    dd.append(t_dma(lambda g, j=j, ci=ci: g.dma_start(
                        out=s_vf[:, 1024 * j:1024 * (j + 1)],
                        in_=vsc[HD * ci + 128 * j:HD * ci + 128 * (j + 1), :])))
                for ct in range(8):
                    dd.append(t_dma(lambda g, ct=ct, ci=ci: g.dma_start(
                        out=s_ktm[:, 512 * ct:512 * (ct + 1)],
                        in_=ktsc[C * ci + 128 * ct:C * ci + 128 * (ct + 1), :])))
                    dd.append(t_dma(lambda g, ct=ct, ci=ci: g.dma_start(
                        out=s_vtm[:, 512 * ct:512 * (ct + 1)],
                        in_=vtsc[C * ci + 128 * ct:C * ci + 128 * (ct + 1), :])))
                for tt in range(8):
                    dd.append(t_dma(lambda g, tt=tt, ci=ci: g.dma_start(
                        out=s_proj[:, 5 * tt:5 * (tt + 1)],
                        in_=pjsc[C * ci + 128 * tt:C * ci + 128 * (tt + 1), :])))
                ld = ("p4d", CNT["p4d"])

                wdep = [("pv", CNT["pv"]), ("ps", CNT["ps"])]
                cast1 = t_op("v", lambda v: v.tensor_copy(s_W1b[:, :], s_W1[:, :]),
                             wdep)
                tr_ev = []
                for srcb, dstb, caster in (
                        (s_wca, s_W0Tb, lambda v: v.tensor_copy(s_wca[:, :],
                                                                s_W0[:, :])),
                        (s_wca, s_W2Tb, lambda v: v.tensor_copy(s_wca[:, :],
                                                                s_W2[:, :])),
                        (s_W1b, s_W1Tb, None)):
                    if caster is not None:
                        cdep = t_op("v", caster, wdep + tr_ev[-1:])
                    else:
                        cdep = cast1
                    for jh in range(4):
                        def mmf(tensor, pbt, src=srcb, jh=jh):
                            for c2 in range(4):
                                i = tensor.transpose(
                                    pbt[:, 64 * c2:64 * (c2 + 1)].bitcast(BF16),
                                    src[:, 512 * jh + 128 * c2:
                                        512 * jh + 128 * (c2 + 1)],
                                    s_ident[:, :])
                            return i
                        def evf(v, pbt, dst=dstb, jh=jh):
                            return v.tensor_copy(
                                dst.ap().rearrange("p (a b) -> p a b", b=128)
                                [:, :, 128 * 0:128] if False else
                                bass.AP(dst, 128 * jh,
                                        [[2048, 128], [512, 4], [1, 128]]),
                                pbt[:, 0:256].bitcast(BF16))
                        tr_ev.append(t_mm(mmf, [cdep], "v", evf)[1])
                stp_done = [("pv", CNT["pv"])]

                pq = []
                for j in range(4):
                    pq.append(t_op("s", lambda sc, j=j: sc.activation(
                        s_qf[:, 1024 * j:1024 * (j + 1)],
                        s_qf[:, 1024 * j:1024 * (j + 1)], AF.Silu), [ld]))
                    pq.append(t_op("s", lambda sc, j=j: sc.activation(
                        s_vf[:, 1024 * j:1024 * (j + 1)],
                        s_vf[:, 1024 * j:1024 * (j + 1)], AF.Silu), [ld]))
                for ct in range(8):
                    pq.append(t_op("s", lambda sc, ct=ct: sc.activation(
                        s_vtm[:, 512 * ct:512 * (ct + 1)],
                        s_vtm[:, 512 * ct:512 * (ct + 1)], AF.Silu), [ld]))
                def normrows(srcfm, dep_sq, evdeps=()):
                    rows = []
                    for tb in range(2):
                        def mmf(tensor, pbt, tb=tb):
                            for j in range(4):
                                i = tensor.matmul(
                                    pbt[0:1, 0:512], s_onescol[:, :],
                                    s_scr4b[:, 1024 * j + 512 * tb:
                                            1024 * j + 512 * (tb + 1)],
                                    start=(j == 0), stop=(j == 3))
                            return i
                        def evf(sc, pbt, tb=tb):
                            return sc.activation(s_rw4[0:1, 512 * tb:512 * (tb + 1)],
                                                 pbt[0:1, 0:512], AF.Sqrt)
                        rows.append(t_mm(mmf, dep_sq, "s", evf, evdeps)[1])
                    return rows
                sqd = []
                for j in range(4):
                    sqd.append(t_op("s", lambda sc, j=j: sc.activation(
                        s_scr4b[:, 1024 * j:1024 * (j + 1)],
                        s_qf[:, 1024 * j:1024 * (j + 1)], AF.Square), [pq[-1]]))
                qrows = normrows(s_qf, sqd)
                rq1 = t_op("v", lambda v: v.tensor_scalar(
                    s_rw4[0:1, 0:1024], s_rw4[0:1, 0:1024], 1e-5, None, ALU.add),
                    qrows, force=True)
                rq2 = t_op("v", lambda v: v.reciprocal(s_rw4[0:1, 0:1024],
                                                       s_rw4[0:1, 0:1024]),
                    [rq1], force=True)
                rq3 = t_op("v", lambda v: v.tensor_copy(s_rwb[0:1, 0:1024],
                                                        s_rw4[0:1, 0:1024]), [rq2])
                qbc = []
                for tb in range(2):
                    def mmf(tensor, pbt, tb=tb):
                        return tensor.matmul(pbt[:, :], s_onesrow[:, :],
                                             s_rwb[0:1, 512 * tb:512 * (tb + 1)],
                                             start=True, stop=True)
                    def evf(sc, pbt, tb=tb):
                        return sc.activation(s_rq[:, 512 * tb:512 * (tb + 1)],
                                             pbt[:, :], AF.Copy)
                    qbc.append(t_mm(mmf, [rq3], "s", evf)[1])
                qap = []
                for j in range(4):
                    qap.append(t_op("v", lambda v, j=j: v.tensor_mul(
                        s_qf[:, 1024 * j:1024 * (j + 1)],
                        s_qf[:, 1024 * j:1024 * (j + 1)], s_rq[:, :]), qbc))
                pk = []
                for j in range(4):
                    pk.append(t_op("s", lambda sc, j=j: sc.activation(
                        s_kf[:, 1024 * j:1024 * (j + 1)],
                        s_kf[:, 1024 * j:1024 * (j + 1)], AF.Silu),
                        [ld] + qap[-1:]))
                sqk = []
                for j in range(4):
                    sqk.append(t_op("s", lambda sc, j=j: sc.activation(
                        s_scr4b[:, 1024 * j:1024 * (j + 1)],
                        s_kf[:, 1024 * j:1024 * (j + 1)], AF.Square), [pk[-1]]))
                krows = normrows(s_kf, sqk, (rq3,))
                rk1 = t_op("v", lambda v: v.tensor_scalar(
                    s_rw4[0:1, 0:1024], s_rw4[0:1, 0:1024], 1e-5, None, ALU.add),
                    krows + qap, force=True)
                rk2 = t_op("v", lambda v: v.reciprocal(s_rw4[0:1, 0:1024],
                                                       s_rw4[0:1, 0:1024]),
                    [rk1], force=True)
                rk3 = t_op("v", lambda v: v.tensor_copy(s_rwb[0:1, 0:1024],
                                                        s_rw4[0:1, 0:1024]), [rk2])
                kbc = []
                for tb in range(2):
                    def mmf(tensor, pbt, tb=tb):
                        return tensor.matmul(pbt[:, :], s_onesrow[:, :],
                                             s_rwb[0:1, 512 * tb:512 * (tb + 1)],
                                             start=True, stop=True)
                    def evf(sc, pbt, tb=tb):
                        return sc.activation(s_rq[:, 512 * tb:512 * (tb + 1)],
                                             pbt[:, :], AF.Copy)
                    kbc.append(t_mm(mmf, [rk3], "s", evf, tuple(qap))[1])
                kap = []
                for j in range(4):
                    kap.append(t_op("v", lambda v, j=j: v.tensor_mul(
                        s_kf[:, 1024 * j:1024 * (j + 1)],
                        s_kf[:, 1024 * j:1024 * (j + 1)], s_rq[:, :]), kbc))
                bo1 = t_dma(lambda g, ci=ci: g.dma_start(
                    out=rowscr[0:1, :], in_=s_rw4[0:1, 0:1024]), [rk2])
                bo2 = t_dma(lambda g: g.dma_start(
                    out=s_rktm[:, :],
                    in_=bass.AP(rowscr, 0, [[1, 128], [128, 8]]),
                    allow_slow_non_contiguous=True), [bo1])
                ktp = []
                for ct in range(8):
                    d1 = t_op("s", lambda sc, ct=ct: sc.activation(
                        s_ktm[:, 512 * ct:512 * (ct + 1)],
                        s_ktm[:, 512 * ct:512 * (ct + 1)], AF.Silu), [ld])
                    ktp.append(t_op("v", lambda v, ct=ct: v.tensor_scalar(
                        s_ktm[:, 512 * ct:512 * (ct + 1)],
                        s_ktm[:, 512 * ct:512 * (ct + 1)],
                        s_rktm[:, ct:ct + 1], None, ALU.mult), [d1, bo2]))
                lrd = []
                for dst, col in ((s_l0, 0), (s_l1, 1), (s_l2, 2)):
                    a1 = t_op("s", lambda sc, dst=dst, col=col: sc.activation(
                        dst[:, :], s_proj[:, col:40:5], AF.Abs,
                        bias=BLRI), [ld])
                    a2 = t_op("s", lambda sc, dst=dst: sc.activation(
                        dst[:, :], dst[:, :], AF.Exp, scale=-1.0), [a1])
                    a3 = t_op("s", lambda sc, dst=dst: sc.activation(
                        dst[:, :], dst[:, :], AF.Ln, bias=1.0), [a2])
                    a4 = t_op("v", lambda v, dst=dst, col=col: v.tensor_scalar(
                        s_rn[:, 4:5] if False else s_scr4[:, 0:8],
                        s_proj[:, col:40:5], BLRI, None, ALU.add),
                        [a3], force=True)
                    a5 = t_op("v", lambda v: v.tensor_scalar(
                        s_scr4[:, 0:8], s_scr4[:, 0:8], 0.0, None, ALU.max),
                        [a4], force=True)
                    lrd.append(t_op("v", lambda v, dst=dst: v.tensor_add(
                        dst[:, :], dst[:, :], s_scr4[:, 0:8]), [a5],
                        force=True))
                mo1 = t_op("s", lambda sc: sc.activation(
                    s_mo[:, :], s_proj[:, 3:40:5], AF.Sigmoid,
                    bias=s_bm[:, 0:1]), [ld])
                sc1 = t_op("s", lambda sc: sc.activation(
                    s_sc4[:, :], s_proj[:, 4:40:5], AF.Silu,
                    bias=s_bs[:, 0:1]), [ld])
                def mmf(tensor, pbt):
                    return tensor.matmul(pbt[0:1, 0:8], s_onescol[:, :],
                                         s_mo[:, :], start=True, stop=True)
                def evf(sc, pbt):
                    return sc.activation(s_c4[0:1, 0:8], pbt[0:1, 0:8], AF.Copy)
                mrow = t_mm(mmf, [mo1], "s", evf)[1]
                m1 = t_op("v", lambda v: v.reduce_sum(s_c4[0:1, 8:9],
                                                      s_c4[0:1, 0:8], axis=AX.X),
                          [mrow], force=True)
                m2 = t_op("v", lambda v: v.tensor_scalar(
                    s_c4b[0:1, 0:1], s_c4[0:1, 8:9], 1.0 / C, None, ALU.mult),
                    [m1])
                def mmf2(tensor, pbt):
                    return tensor.matmul(pbt[:, 0:1], s_onesrow[:, :],
                                         s_c4b[0:1, 0:1], start=True, stop=True)
                def evf2(sc, pbt):
                    return sc.activation(s_mmv[:, :], pbt[:, 0:1], AF.Copy)
                mmvd = t_mm(mmf2, [m2], "s", evf2)[1]
                sb1 = t_dma(lambda g: g.dma_start(
                    out=bass.AP(rowscr, 1024, [[1, 128], [128, 8]]),
                    in_=s_sc4[:, :], allow_slow_non_contiguous=True), [sc1])
                sb2 = t_dma(lambda g: g.dma_start(
                    out=s_srow[0:1, :], in_=rowscr[1:2, :]), [sb1])

                ghd = {}
                ghorder = []
                for m in range(4):
                    for tb in range(2):
                        pi = (m * 2 + tb) % 2
                        ghguard = [ghorder[-2]] if len(ghorder) >= 2 else []
                        def mmh(tensor, pbt, m=m, tb=tb):
                            for kd in range(4):
                                i = tensor.matmul(
                                    pbt[:, :],
                                    s_W2Tb[:, 512 * kd + 128 * m:
                                           512 * kd + 128 * (m + 1)],
                                    s_qf[:, 1024 * kd + 512 * tb:
                                         1024 * kd + 512 * (tb + 1)],
                                    start=(kd == 0), stop=(kd == 3))
                            return i
                        def evh(sc, pbt, pi=pi):
                            return sc.activation(s_t1[pi][:, :], pbt[:, :], AF.Copy)
                        hdep = t_mm(mmh, stp_done + qap, "s", evh, ghguard)[1]
                        def mmg(tensor, pbt, m=m, tb=tb):
                            for kd in range(4):
                                i = tensor.matmul(
                                    pbt[:, :],
                                    s_W0Tb[:, 512 * kd + 128 * m:
                                           512 * kd + 128 * (m + 1)],
                                    s_qf[:, 1024 * kd + 512 * tb:
                                         1024 * kd + 512 * (tb + 1)],
                                    start=(kd == 0), stop=(kd == 3))
                            return i
                        def evg(sc, pbt, pi=pi):
                            return sc.activation(s_t2[pi][:, :], pbt[:, :], AF.Silu)
                        gdep = t_mm(mmg, stp_done + qap, "s", evg, ghguard)[1]
                        ghd[(m, tb)] = t_op(
                            "v", lambda v, m=m, tb=tb, pi=pi: v.tensor_mul(
                                s_gh[:, 1024 * m + 512 * tb:
                                     1024 * m + 512 * (tb + 1)],
                                s_t1[pi][:, :], s_t2[pi][:, :]), [hdep, gdep])
                        ghorder.append(ghd[(m, tb)])
                msum_sq = {0: [], 1: []}
                for tb in range(2):
                    ghdeps = [ghd[(kh, tb)] for kh in range(4)]
                    for m in range(4):
                        def mmo(tensor, pbt, m=m, tb=tb):
                            for kh in range(4):
                                i = tensor.matmul(
                                    pbt[:, :],
                                    s_W1Tb[:, 512 * kh + 128 * m:
                                           512 * kh + 128 * (m + 1)],
                                    s_gh[:, 1024 * kh + 512 * tb:
                                         1024 * kh + 512 * (tb + 1)],
                                    start=(kh == 0), stop=(kh == 3))
                            return i
                        def evo(sc, pbt, m=m, tb=tb):
                            sc.activation(s_sq4[:, 512 * m:512 * (m + 1)],
                                          pbt[:, :], AF.Square)
                            return sc.activation(
                                s_out4[:, 2048 * tb + 512 * m:
                                       2048 * tb + 512 * (m + 1)],
                                pbt[:, :], AF.Copy)
                        msum_sq[tb].append(t_mm(mmo, ghdeps, "s", evo)[1])
                    def mmr(tensor, pbt, tb=tb):
                        for m in range(4):
                            i = tensor.matmul(pbt[0:1, 0:512], s_onescol[:, :],
                                              s_sq4[:, 512 * m:512 * (m + 1)],
                                              start=(m == 0), stop=(m == 3))
                        return i
                    def evr(sc, pbt, tb=tb):
                        sc.activation(s_rw4[0:1, 512 * tb:512 * (tb + 1)],
                                      pbt[0:1, 0:512], AF.Ln, bias=1e-5,
                                      scale=1.0 / HD)
                        sc.drain()
                        return sc.activation(s_rw4[0:1, 512 * tb:512 * (tb + 1)],
                                             s_rw4[0:1, 512 * tb:512 * (tb + 1)],
                                             AF.Exp, scale=-0.5)
                    msum_sq[tb].append(t_mm(mmr, msum_sq[tb], "s", evr,
                                            (rk3,))[1])

                kvops = []
                for ct in range(8):
                    pi = ct % 2
                    def mmgb(tensor, pbt, ct=ct):
                        for kd in range(4):
                            i = tensor.matmul(
                                pbt[:, :],
                                s_kf[:, 1024 * kd + 128 * ct:
                                     1024 * kd + 128 * (ct + 1)]
                                if False else
                                s_kf[:, 1024 * kd + 128 * ct:1024 * kd + 128 * ct + 128],
                                s_W0Tb[:, 512 * kd:512 * (kd + 1)],
                                start=(kd == 0), stop=(kd == 3))
                        return i
                    ghlast = ([ghd[(3, 1)]] if ct < 2 else
                              [kvops[ct - 2]])
                    def evgb(sc, pbt, pi=pi):
                        sc.activation(s_sg1[pi][:, :], pbt[:, :], AF.Silu)
                        sc.activation(s_dsg1[pi][:, :], pbt[:, :], AF.Sigmoid)
                        return sc.activation(s_t1[pi][:, :], pbt[:, :],
                                             AF.Copy)
                    d1 = t_mm(mmgb, kap, "s", evgb, ghlast)[1]
                    w1 = t_op("v", lambda v, pi=pi: v.tensor_mul(
                        s_t2[pi][:, :], s_dsg1[pi][:, :],
                        s_t1[pi][:, :]), [d1])
                    w3 = t_op("v", lambda v, pi=pi: v.scalar_tensor_tensor(
                        s_t1[pi][:, :], s_t1[pi][:, :], 1.0,
                        s_t2[pi][:, :], ALU.add, ALU.subtract), [w1])
                    w4 = t_op("v", lambda v, pi=pi: v.tensor_mul(
                        s_dsg1[pi][:, :], s_dsg1[pi][:, :],
                        s_t1[pi][:, :]), [w3])
                    def mmhb(tensor, pbt, ct=ct):
                        for kd in range(4):
                            i = tensor.matmul(
                                pbt[:, :],
                                s_kf[:, 1024 * kd + 128 * ct:1024 * kd + 128 * ct + 128],
                                s_W2Tb[:, 512 * kd:512 * (kd + 1)],
                                start=(kd == 0), stop=(kd == 3))
                        return i
                    def evhb(sc, pbt, pi=pi):
                        return sc.activation(s_hb1[pi][:, :], pbt[:, :], AF.Copy)
                    d2 = t_mm(mmhb, kap, "s", evhb, ghlast)[1]
                    def mmdh(tensor, pbt, ct=ct):
                        for ko in range(4):
                            i = tensor.matmul(
                                pbt[:, :],
                                s_vf[:, 1024 * ko + 128 * ct:1024 * ko + 128 * ct + 128],
                                s_W1b[:, 512 * ko:512 * (ko + 1)],
                                start=(ko == 0), stop=(ko == 3))
                        return i
                    def evdh(sc, pbt, pi=pi):
                        return sc.activation(s_dh1[pi][:, :], pbt[:, :], AF.Copy)
                    d3 = t_mm(mmdh, stp_done + pq[-1:], "s", evdh, ghlast)[1]
                    o2 = t_op("v", lambda v, ct=ct, pi=pi:
                              v.scalar_tensor_tensor(
                                  s_hl1[:, 512 * ct:512 * (ct + 1)],
                                  s_sg1[pi][:, :], s_l1[:, ct:ct + 1],
                                  s_hb1[pi][:, :], ALU.mult, ALU.mult),
                              [d1, d2, lrd[1]])
                    o3 = t_op("v", lambda v, pi=pi: v.tensor_mul(
                        s_tx[pi][:, :], s_dh1[pi][:, :], s_hb1[pi][:, :]),
                        [d3, o2])
                    o5 = t_op("v", lambda v, ct=ct, pi=pi:
                              v.scalar_tensor_tensor(
                                  s_dgl0[:, 512 * ct:512 * (ct + 1)],
                                  s_tx[pi][:, :], s_l0[:, ct:ct + 1],
                                  s_dsg1[pi][:, :], ALU.mult, ALU.mult),
                              [o3, w4, lrd[0]])
                    o7 = t_op("v", lambda v, ct=ct, pi=pi:
                              v.scalar_tensor_tensor(
                                  s_dhl2[:, 512 * ct:512 * (ct + 1)],
                                  s_dh1[pi][:, :], s_l2[:, ct:ct + 1],
                                  s_sg1[pi][:, :], ALU.mult, ALU.mult),
                              [o5, lrd[2]])
                    kvops.append(o7)

                frob_bc = []
                for c in range(3):
                    L_, R_ = ((s_dgl0, s_ktm), (s_vtm, s_hl1),
                              (s_dhl2, s_ktm))[c]
                    Ms, MTs = s_M[c], s_MT[c]
                    evs = []
                    for t_ in range(2):
                        A_, B_ = (L_, R_) if t_ == 0 else (R_, L_)
                        dst = Ms if t_ == 0 else MTs
                        for m in range(4):
                            def mmd(tensor, pbt, A_=A_, B_=B_, m=m):
                                for ct in range(8):
                                    i = tensor.matmul(
                                        pbt[:, :],
                                        A_[:, 512 * ct + 128 * m:
                                           512 * ct + 128 * (m + 1)],
                                        B_[:, 512 * ct:512 * (ct + 1)],
                                        start=(ct == 0), stop=(ct == 7))
                                return i
                            def evd(v, pbt, dst=dst, m=m):
                                return v.scalar_tensor_tensor(
                                    dst[:, 512 * m:512 * (m + 1)],
                                    dst[:, 512 * m:512 * (m + 1)],
                                    s_mmv[:, 0:1], pbt[:, :],
                                    ALU.mult, ALU.add)
                            evs.append(t_mm(mmd, kvops + [mmvd], "v", evd)[1])
                    if not NS_IT:
                        continue
                    f1 = t_op("s", lambda sc, Ms=Ms: sc.activation(
                        s_scr4b[:, 0:2048], Ms[:, :], AF.Square,
                        accum_out=s_rn[:, 4:5]), evs)
                    f2 = t_op("v", lambda v: v.tensor_copy(
                        s_rn[:, 5:6].bitcast(F32) if False else s_rn[:, 5:6],
                        s_rn[:, 4:5]), [f1])
                    f2b = t_op("v", lambda v: v.tensor_copy(
                        s_c4b[0:1, 1:2], s_c4[0:1, 1:2]), [f2]) if False else None
                    f3 = t_op("v", lambda v: v.tensor_copy(
                        s_scr4b[:, 0:1], s_rn[:, 4:5]), [f1])
                    def mmfb(tensor, pbt):
                        return tensor.matmul(pbt[0:1, 0:1], s_onescol[:, :],
                                             s_scr4b[:, 0:1], start=True,
                                             stop=True)
                    def evfb(sc, pbt):
                        return sc.activation(s_c4[0:1, 2:3], pbt[0:1, 0:1],
                                             AF.Sqrt)
                    fr = t_mm(mmfb, [f3], "s", evfb)[1]
                    g1 = t_op("v", lambda v: v.tensor_scalar(
                        s_c4[0:1, 2:3], s_c4[0:1, 2:3], 1e-7, None, ALU.add),
                        [fr], force=True)
                    g2 = t_op("v", lambda v: v.reciprocal(
                        s_c4[0:1, 2:3], s_c4[0:1, 2:3]), [g1], force=True)
                    g3 = t_op("v", lambda v: v.tensor_copy(
                        s_c4b[0:1, 2:3], s_c4[0:1, 2:3]), [g2])
                    def mmfc(tensor, pbt):
                        return tensor.matmul(pbt[:, 0:1], s_onesrow[:, :],
                                             s_c4b[0:1, 2:3], start=True,
                                             stop=True)
                    def evfc(sc, pbt):
                        return sc.activation(s_rn[:, 6:7], pbt[:, 0:1], AF.Copy)
                    fb = t_mm(mmfc, [g3], "s", evfc)[1]
                    x1 = t_op("v", lambda v, c=c, Ms=Ms: v.tensor_scalar(
                        s_X[c][:, :], Ms[:, :], s_rn[:, 6:7], None, ALU.mult),
                        [fb])
                    x2 = t_op("v", lambda v, c=c, MTs=MTs: v.tensor_scalar(
                        s_XT[c][:, :], MTs[:, :], s_rn[:, 6:7], None, ALU.mult),
                        [fb, x1])
                    frob_bc.append((x1, x2))

                lastX = [frob_bc[c][0] for c in range(3)] if NS_IT else []
                lastXT = [frob_bc[c][1] for c in range(3)] if NS_IT else []
                for it in range(NS_IT):
                    a_, b_, c_ = NSC[it]
                    Aev = [[], [], []]
                    for c in range(3):
                        for m in range(4):
                            def mmA(tensor, pbt, c=c, m=m):
                                for k in range(4):
                                    i = tensor.matmul(
                                        pbt[:, :],
                                        s_XT[c][:, 512 * k + 128 * m:
                                                512 * k + 128 * (m + 1)],
                                        s_XT[c][:, 512 * k:512 * (k + 1)],
                                        start=(k == 0), stop=(k == 3))
                                return i
                            def evA(sc, pbt, c=c, m=m):
                                return sc.activation(
                                    s_A[c][:, 512 * m:512 * (m + 1)],
                                    pbt[:, :], AF.Copy)
                            Aev[c].append(t_mm(mmA, [lastX[c], lastXT[c]],
                                               "s", evA)[1])
                    Bops = [[], [], []]
                    prevB = {}
                    for c in range(3):
                        a2ev = []
                        for m in range(4):
                            def mmA2(tensor, pbt, c=c, m=m):
                                for k in range(4):
                                    i = tensor.matmul(
                                        pbt[:, :],
                                        s_A[c][:, 512 * k + 128 * m:
                                               512 * k + 128 * (m + 1)],
                                        s_A[c][:, 512 * k:512 * (k + 1)],
                                        start=(k == 0), stop=(k == 3))
                                return i
                            def evA2(sc, pbt, c=c, m=m, c_=c_):
                                return sc.activation(
                                    s_scr4b[:, 512 * m:512 * (m + 1)],
                                    pbt[:, :], AF.Copy, scale=c_)
                            evd = [prevB[m]] if m in prevB else []
                            a2ev.append(t_mm(mmA2, Aev[c], "s", evA2, evd)[1])
                        for m in range(4):
                            bop = t_op(
                                "v", lambda v, c=c, m=m, b_=b_:
                                v.scalar_tensor_tensor(
                                    s_A[c][:, 512 * m:512 * (m + 1)],
                                    s_A[c][:, 512 * m:512 * (m + 1)], b_,
                                    s_scr4b[:, 512 * m:512 * (m + 1)],
                                    ALU.mult, ALU.add), a2ev)
                            Bops[c].append(bop)
                            prevB[m] = bop
                    newX, newXT = [], []
                    for c in range(3):
                        bx = []
                        for m in range(4):
                            def mmBX(tensor, pbt, c=c, m=m):
                                for k in range(4):
                                    i = tensor.matmul(
                                        pbt[:, :],
                                        s_A[c][:, 512 * k + 128 * m:
                                               512 * k + 128 * (m + 1)],
                                        s_X[c][:, 512 * k:512 * (k + 1)],
                                        start=(k == 0), stop=(k == 3))
                                return i
                            bx.append((mmBX, m))
                        xtb = []
                        for m in range(4):
                            def mmXTB(tensor, pbt, c=c, m=m):
                                for k in range(4):
                                    i = tensor.matmul(
                                        pbt[:, :],
                                        s_X[c][:, 512 * k + 128 * m:
                                               512 * k + 128 * (m + 1)],
                                        s_A[c][:, 512 * k:512 * (k + 1)],
                                        start=(k == 0), stop=(k == 3))
                                return i
                            xtb.append((mmXTB, m))
                        bxe = []
                        xtbseq = []
                        for mmf_, m in bx:
                            def evBX(v, pbt, c=c, m=m, a_=a_):
                                return v.scalar_tensor_tensor(
                                    s_X[c][:, 512 * m:512 * (m + 1)],
                                    s_X[c][:, 512 * m:512 * (m + 1)], a_,
                                    pbt[:, :], ALU.mult, ALU.add)
                            bxe.append((mmf_, evBX, m))
                        xe = []
                        for mmf_, m in xtb:
                            def evXTB(v, pbt, c=c, m=m, a_=a_):
                                return v.scalar_tensor_tensor(
                                    s_XT[c][:, 512 * m:512 * (m + 1)],
                                    s_XT[c][:, 512 * m:512 * (m + 1)], a_,
                                    pbt[:, :], ALU.mult, ALU.add)
                            xe.append(t_mm(mmf_, Bops[c], "v", evXTB)[1])
                        last_bx_seq = CNT["tg"] + 4
                        for mmf_, evf_, m in bxe:
                            bxe_dep = t_mm(mmf_, Bops[c],
                                           "v", evf_,
                                           [("tg", last_bx_seq)])[1]
                            xe.append(bxe_dep)
                        newX.append(xe[-1])
                        newXT.append(xe[3])
                    lastX, lastXT = newX, newXT

                wupd = []
                for c, W in enumerate((s_W0, s_W1, s_W2) if NS_IT else ()):
                    u1 = t_op("v", lambda v, W=W, c=c: v.tensor_add(
                        W[:, :], W[:, :], s_X[c][:, :]), [lastX[c], lastXT[c]])
                    sqs = []
                    for jb in range(4):
                        sqs.append(t_op("s", lambda sc, W=W, jb=jb: sc.activation(
                            s_scr4b[:, 512 * jb:512 * (jb + 1)],
                            W[:, 512 * jb:512 * (jb + 1)], AF.Square,
                            accum_out=s_rn[:, jb:jb + 1]), [u1]))
                    s1 = t_op("s", lambda sc: sc.activation(
                        s_rn[:, 0:4], s_rn[:, 0:4], AF.Sqrt), sqs)
                    v1 = t_op("v", lambda v: v.tensor_scalar(
                        s_rn[:, 0:4], s_rn[:, 0:4], 1e-5, None, ALU.add),
                        [s1], force=True)
                    v2 = t_op("v", lambda v: v.reciprocal(s_rn[:, 0:4],
                                                          s_rn[:, 0:4]),
                        [v1], force=True)
                    v3 = t_op("v", lambda v, c=c: v.tensor_mul(
                        s_rn[:, 0:4], s_rn[:, 0:4], s_n0[:, 4 * c:4 * (c + 1)]),
                        [v2], force=True)
                    for jb in range(4):
                        wupd.append(t_op("v", lambda v, W=W, jb=jb: v.tensor_scalar(
                            W[:, 512 * jb:512 * (jb + 1)],
                            W[:, 512 * jb:512 * (jb + 1)],
                            s_rn[:, jb:jb + 1], None, ALU.mult), [v3]))

                fac_ev = []
                for tb in range(2):
                    r1 = t_op("v", lambda v, tb=tb: v.tensor_mul(
                        s_rw4[0:1, 512 * tb:512 * (tb + 1)],
                        s_rw4[0:1, 512 * tb:512 * (tb + 1)],
                        s_srow[0:1, 512 * tb:512 * (tb + 1)]),
                        [msum_sq[tb][-1], sb2])
                    r2 = t_op("v", lambda v, tb=tb: v.tensor_copy(
                        s_rwb[0:1, 512 * tb:512 * (tb + 1)],
                        s_rw4[0:1, 512 * tb:512 * (tb + 1)]), [r1])
                    def mmf3(tensor, pbt, tb=tb):
                        return tensor.matmul(pbt[:, :], s_onesrow[:, :],
                                             s_rwb[0:1, 512 * tb:512 * (tb + 1)],
                                             start=True, stop=True)
                    def evf3(sc, pbt, tb=tb):
                        return sc.activation(s_fac[:, 512 * tb:512 * (tb + 1)],
                                             pbt[:, :], AF.Copy)
                    fac_ev.append(t_mm(mmf3, [r2], "s", evf3)[1])
                tts_ops = []
                for m in range(4):
                    for tb in range(2):
                        tts_ops.append(t_op(
                            "v", lambda v, m=m, tb=tb: v.scalar_tensor_tensor(
                                s_tts[:, 1024 * (m % 2) + 512 * tb:
                                      1024 * (m % 2) + 512 * (tb + 1)],
                                s_out4[:, 2048 * tb + 512 * m:
                                       2048 * tb + 512 * (m + 1)],
                                s_ong[:, m:m + 1],
                                s_fac[:, 512 * tb:512 * (tb + 1)],
                                ALU.mult, ALU.mult), fac_ev +
                            ([("p4d", CNT["p4d"])] if m >= 2 else [])))
                    t_dma(lambda g, m=m, ci=ci: g.dma_start(
                        out=ta_in[HD * ci + 128 * m:HD * ci + 128 * (m + 1),
                                  0:512],
                        in_=s_tts[:, 1024 * (m % 2):1024 * (m % 2) + 512]),
                        [tts_ops[-1]])
                    t_dma(lambda g, m=m, ci=ci: g.dma_start(
                        out=ta_in[HD * ci + 128 * m:HD * ci + 128 * (m + 1),
                                  512:1024],
                        in_=s_tts[:, 1024 * (m % 2) + 512:1024 * (m % 2) + 1024]),
                        [tts_ops[-1]])
                return ("tg", CNT["tg"])

            for c_, W_ in enumerate((s_W0, s_W1, s_W2)):
                for jb_ in range(4):
                    t_op("s", lambda sc, W_=W_, c_=c_, jb_=jb_: sc.activation(
                        s_scr4b[:, 512 * jb_:512 * (jb_ + 1)],
                        W_[:, 512 * jb_:512 * (jb_ + 1)], AF.Square,
                        accum_out=s_n0[:, 4 * c_ + jb_:4 * c_ + jb_ + 1]),
                        [("p4n", 1)])
                t_op("s", lambda sc, c_=c_: sc.activation(
                    s_n0[:, 4 * c_:4 * (c_ + 1)],
                    s_n0[:, 4 * c_:4 * (c_ + 1)], AF.Sqrt),
                    [("ps", CNT["ps"])])

            chunk_fences = []
            for ci in range(N_CHUNKS):
                chunk_fences.append(emit_chunk(ci))

            DRAIN_MARK = set()

            def _mark(deps, cons):
                for sem, peng in (("pv", "v"), ("ps", "s")):
                    if peng == cons:
                        continue
                    vals = [v for (s2, v) in deps if s2 == sem]
                    if vals:
                        DRAIN_MARK.add((sem, max(vals)))

            for e in SCH:
                if e[0] == "mm":
                    _mark(e[2], "t")
                    _mark(e[5], e[3])
                elif e[0] == "op":
                    _mark(e[3], e[1])
                    if e[5]:
                        DRAIN_MARK.add(("pv" if e[1] == "v" else "ps", e[4]))
                else:
                    _mark(e[2], "d")

            nc.leave_named_scope("p3", _sc_p3, False)
            _sc_p4 = nc.enter_named_scope("p4", False)[0]
            with nc.Block() as blk:
                @blk.tensor
                def _(tensor):
                    eng_enter(tensor)
                    for e in SCH:
                        if e[0] != "mm":
                            continue
                        (_, emit, deps, ee, emit_ev, evdeps, seq, bank,
                         ecnt, bdep) = e
                        for (snm, val) in deps:
                            S.wait(tensor, P4MAP[snm], val)
                        if bdep is not None:
                            S.wait(tensor, P4MAP[bdep[0]], bdep[1])
                        i = emit(tensor, pb[bank])
                        i.then_inc(S["p4g"], 1)
                    leave(tensor)

                def emit_cq(eng, enm, sem):
                    for e in SCH:
                        if e[0] == "mm" and e[3] == enm:
                            (_, emit, deps, ee, emit_ev, evdeps, seq, bank,
                             ecnt, bdep) = e
                            S.wait(eng, "p4g", seq)
                            for (snm, val) in evdeps:
                                S.wait(eng, P4MAP[snm], val)
                            i = emit_ev(eng, pb[bank])
                            if (sem, ecnt) in DRAIN_MARK:
                                i = eng.drain()
                            i.then_inc(S[sem], 1)
                        elif e[0] == "op" and e[1] == enm:
                            _, _, emit, deps, cnum, force = e
                            for (snm, val) in deps:
                                S.wait(eng, P4MAP[snm], val)
                            i = emit(eng)
                            if (sem, cnum) in DRAIN_MARK:
                                i = eng.drain()
                            i.then_inc(S[sem], 1)

                @blk.vector
                def _(vector):
                    eng_enter(vector)
                    emit_cq(vector, "v", "p4v")
                    leave(vector)

                @blk.scalar
                def _(scalar):
                    eng_enter(scalar)
                    emit_cq(scalar, "s", "p4s")
                    leave(scalar)

                @blk.gpsimd
                def _(gpsimd):
                    eng_enter(gpsimd)
                    for nm_, dst in (("w0m", s_W0), ("w1m", s_W1), ("w2m", s_W2)):
                        for jb in range(4):
                            S.inc(gpsimd.dma_start(
                                out=dst[:, 512 * jb:512 * (jb + 1)],
                                in_=wms[nm_][128 * jb:128 * (jb + 1), :]),
                                "p4i", 16)
                    for c in range(3):
                        S.inc(gpsimd.memset(s_M[c][:, :], 0), "p4im", 1)
                        S.inc(gpsimd.memset(s_MT[c][:, :], 0), "p4im", 1)
                    gpsimd.wait_ge(S["p4i"], S.val("p4i"))
                    gpsimd.wait_ge(S["p4im"], S.val("p4im"))
                    gpsimd.nop().then_inc(S["p4n"], 1)
                    S.cnt["p4n"] += 1
                    for e in SCH:
                        if e[0] != "dma":
                            continue
                        _, emit, deps, cnum = e
                        for (snm, val) in deps:
                            S.wait(gpsimd, P4MAP[snm], val)
                        i = emit(gpsimd)
                        i.then_inc(S["p4d"], 16)
                    gpsimd.wait_ge(S["p4d"], CNT["p4d"])
                    S.cnt["p4d"] = CNT["p4d"]
                    if N_CHUNKS == NCH:
                      with allow_ici_all_to_all():
                        i = gpsimd.collective_compute(
                            "AllToAll", ALU.bypass,
                            replica_groups=[list(range(8))],
                            ins=[ta_in.ap().opt()], outs=[ta_out.ap().opt()])
                      S.inc(i, "tac", 1, raw=True)
                    else:
                        S.inc(gpsimd.nop(), "tac", 1)
                    if _DEBUG:
                        gpsimd.wait_ge(S["p4v"], CNT["pv"])
                        gpsimd.wait_ge(S["p4s"], CNT["ps"])
                        if N_CHUNKS == NCH:
                            S.inc(gpsimd.dma_start(out=dbg["d_ttt"].ap(),
                                                   in_=ta_in.ap()), "dbgw", 16)
                        S.inc(gpsimd.dma_start(out=dbg["d_W0"].ap(),
                                               in_=s_W0[:, :]), "dbgw", 16)
                        S.inc(gpsimd.dma_start(out=dbg["d_M0"].ap(),
                                               in_=s_M[0][:, :]), "dbgw", 16)
                        S.inc(gpsimd.dma_start(out=dbg["d_X0"].ap(),
                                               in_=s_X[0]), "dbgw", 16)
                        S.inc(gpsimd.dma_start(out=dbg["d_A0"].ap(),
                                               in_=s_A[0]), "dbgw", 16)
                        if N_CHUNKS == 1:
                            S.inc(gpsimd.dma_start(
                                out=dbg["d_tm"][:, 0:4096],
                                in_=s_dgl0[:, :]), "dbgw", 16)
                            S.inc(gpsimd.dma_start(
                                out=dbg["d_tm"][:, 4096:8192],
                                in_=s_hl1[:, :]), "dbgw", 16)
                            for bi_, bs_ in enumerate((s_sg1[0], s_sg1[1],
                                                       s_hb1[0], s_hb1[1],
                                                       s_dh1[0], s_dh1[1])):
                                S.inc(gpsimd.dma_start(
                                    out=dbg["d_tm"][:, 8192 + 1024 * bi_:
                                                    8192 + 1024 * (bi_ + 1)]
                                    .bitcast(F32),
                                    in_=bs_[:, :]), "dbgw", 16)

                            S.inc(gpsimd.dma_start(
                                out=dbg["d_tm"][:, 14336:14352].bitcast(F32),
                                in_=s_l0[:, :]), "dbgw", 16)
                        S.inc(gpsimd.dma_start(
                            out=dbg["d_pj"][:, 0:24],
                            in_=pj_in[0:128, :]), "dbgw", 16)
                        S.inc(gpsimd.dma_start(
                            out=dbg["d_pj"][:, 24:48],
                            in_=pj_out[0:128, :]), "dbgw", 16)
                        S.inc(gpsimd.dma_start(
                            out=dbg["d_pj"][:, 48:53],
                            in_=pjsc[0:128, :]), "dbgw", 16)
                        S.inc(gpsimd.dma_start(
                            out=dbg["d_pj"][:, 53:58],
                            in_=pjsc[1024:1152, :]), "dbgw", 16)
                        for bi, bsrc in enumerate((s_qf, s_kf, s_vf, s_ktm,
                                                   s_vtm)):
                            S.inc(gpsimd.dma_start(
                                out=dbg["d_kv"][:, 4096 * bi:4096 * (bi + 1)],
                                in_=bsrc[:, :]), "dbgw", 16)
                        gpsimd.wait_ge(S["dbgw"], S.val("dbgw"))
                    leave(gpsimd)

                @blk.sync
                def _(sync):
                    eng_enter(sync)
                    leave(sync)

            barrier()
            p4.close()

        p5 = ExitStack()
        s_y = p5.enter_context(nc.sbuf_tensor("s_y", [128, 16 * C], BF16))
        s_tt = [p5.enter_context(nc.sbuf_tensor(f"s_tt{i}", [128, C], BF16))
                for i in range(2)]
        s_wo = p5.enter_context(nc.sbuf_tensor("s_wo", [128, 16 * 2048], BF16))
        s_wof = [p5.enter_context(nc.sbuf_tensor(f"s_wof{i}", [128, 2048], F32))
                 for i in range(2)]
        s_gl = [p5.enter_context(nc.sbuf_tensor(f"s_gl{i}", [128, D], BF16))
                for i in range(2)]
        s_xf = [p5.enter_context(nc.sbuf_tensor(f"s_xf{i}", [128, D], F32))
                for i in range(2)]
        s_of = [p5.enter_context(nc.sbuf_tensor(f"s_of{i}", [128, D], F32))
                for i in range(2)]

        if STAGE >= 4:
            nc.leave_named_scope("p4", _sc_p4, False)
        else:
            nc.leave_named_scope("p3", _sc_p3, False)
        _sc_p5 = nc.enter_named_scope("p5", False)[0]
        with nc.Block() as blk:
            @blk.gpsimd
            def _(gpsimd):
                eng_enter(gpsimd)
                P5TT = STAGE >= 4 and N_CHUNKS == NCH
                if STAGE >= 4:
                    S.wait(gpsimd, "tac", 1)
                for kd in range(16):
                    if STAGE >= 3:
                        S.inc(gpsimd.dma_start(
                            out=s_y[:, C * kd:C * (kd + 1)],
                            in_=attnscr[128 * kd:128 * (kd + 1), :]), "yld", 16)
                    else:
                        S.inc(gpsimd.memset(s_y[:, C * kd:C * (kd + 1)], 0),
                              "ymem", 1)
                    if P5TT:
                        hh, j = kd // 4, kd % 4
                        if kd >= 2:
                            S.wait(gpsimd, "ycb", kd - 1)
                        S.inc(gpsimd.dma_start(
                            out=s_tt[kd % 2][:, :],
                            in_=ta_out[512 * hh + 128 * j:512 * hh + 128 * (j + 1), :]),
                            "yld", 16)
                if S.val("yld"):
                    gpsimd.wait_ge(S["yld"], S.val("yld"))
                if S.val("ymem"):
                    gpsimd.wait_ge(S["ymem"], S.val("ymem"))
                gpsimd.nop().then_inc(S["ymd"], 1)
                S.cnt["ymd"] += 1
                for tt in range(8):
                    S.wait(gpsimd, "oev2", tt + 1)
                    S.inc(gpsimd.dma_start(
                        out=out_t[128 * tt:128 * (tt + 1), :],
                        in_=s_of[tt % 2][:, :]), "ost", 16)
                gpsimd.wait_ge(S["ost"], 8 * 16)
                leave(gpsimd)

            @blk.vector
            def _(vector):
                eng_enter(vector)
                if STAGE >= 4 and N_CHUNKS == NCH:
                    for kd in range(16):
                        vector.wait_ge(S["yld"], 32 * min(kd + 2, 16))
                        i = vector.tensor_add(s_y[:, C * kd:C * (kd + 1)],
                                              s_y[:, C * kd:C * (kd + 1)],
                                              s_tt[kd % 2][:, :])
                        S.inc(i, "ycb", 1)
                for kd in range(16):
                    vector.wait_ge(S["wold"], 16 * min(kd + 2, 16))
                    i = vector.tensor_copy(s_wo[:, 2048 * kd:2048 * (kd + 1)],
                                           s_wof[kd % 2][:, :])
                    S.inc(i, "wocv", 1)
                for tt in range(8):
                    vector.wait_ge(S["xrl"], 32 * min(tt + 2, 8))
                    S.wait(vector, "smm", tt + 1)
                    if tt >= 2:
                        S.wait(vector, "ost", 16 * tt)
                    g0 = (tt % 2) * 4
                    ob = s_of[tt % 2]
                    for nb in range(4):
                        i = vector.tensor_tensor(
                            ob[:, 512 * nb:512 * (nb + 1)], pb[g0 + nb][:, :],
                            s_gl[tt % 2][:, 512 * nb:512 * (nb + 1)], ALU.mult)
                    S.inc(i, "oev", 1)
                    for nb in range(4):
                        i = vector.tensor_add(
                            ob[:, 512 * nb:512 * (nb + 1)],
                            ob[:, 512 * nb:512 * (nb + 1)],
                            s_xf[tt % 2][:, 512 * nb:512 * (nb + 1)])
                    S.inc(i, "oev2", 1)
                leave(vector)

            @blk.sync
            def _(sync):
                eng_enter(sync)
                for kd in range(16):
                    if kd >= 2:
                        S.wait(sync, "wocv", kd - 1)
                    S.inc(sync.dma_start(out=s_wof[kd % 2][:, :],
                                         in_=WoI[128 * kd:128 * (kd + 1), :]),
                          "wold", 16)
                for tt in range(8):
                    if tt >= 2:
                        S.wait(sync, "oev2", tt - 1)
                    S.inc(sync.dma_start(out=s_xf[tt % 2][:, :],
                                         in_=x_in[128 * tt:128 * (tt + 1), :]),
                          "xrl", 16)
                    S.inc(sync.dma_start(out=s_gl[tt % 2][:, :],
                                         in_=gscr[128 * tt:128 * (tt + 1), :]),
                          "xrl", 16)
                leave(sync)

            @blk.tensor
            def _(tensor):
                eng_enter(tensor)
                S.wait(tensor, "ymd", 1)
                if STAGE >= 4 and N_CHUNKS == NCH:
                    S.wait(tensor, "ycb", 16)
                for tt in range(8):
                    g0 = (tt % 2) * 4
                    if tt >= 2:
                        S.wait(tensor, "oev", tt - 1)
                    for kd in range(16):
                        if tt == 0:
                            S.wait(tensor, "wocv", kd + 1)
                        lhs = s_y[:, C * kd + 128 * tt:C * kd + 128 * (tt + 1)]
                        for nb in range(4):
                            i = tensor.matmul(
                                pb[g0 + nb][:, :], lhs,
                                s_wo[:, 2048 * kd + 512 * nb:
                                     2048 * kd + 512 * (nb + 1)],
                                start=(kd == 0), stop=(kd == 15))
                    S.inc(i, "smm", 1)
                leave(tensor)

            @blk.scalar
            def _(scalar):
                eng_enter(scalar)
                leave(scalar)

        barrier()
        p5.close()
        nc.leave_named_scope("p5", _sc_p5, False)
    return nc


def _host_stage(inputs, core):
    h = core % 4
    x = np.ascontiguousarray(
        np.asarray(inputs["L0_attn_x"], dtype=np.float32).reshape(L, D)
        [C * core:C * (core + 1)])

    Wlr = np.asarray(inputs["L0_attn_Wlr"], dtype=np.float32)
    Wm = np.asarray(inputs["L0_attn_Wm"], dtype=np.float32)
    Ws = np.asarray(inputs["L0_attn_Ws"], dtype=np.float32)
    Wp = np.zeros((D, 24), np.float32)
    for hh in range(4):
        Wp[:, 6 * hh + 0:6 * hh + 3] = Wlr[:, 3 * hh:3 * hh + 3]
        Wp[:, 6 * hh + 3] = Wm[:, hh]
        Wp[:, 6 * hh + 4] = Ws[:, hh]

    def tile_gu(w):
        a = np.asarray(w).astype(np.uint8).reshape(NHB, 128, 16, 128)
        return np.ascontiguousarray(a.transpose(0, 3, 2, 1)).reshape(HID, D)

    wg = tile_gu(inputs["L0_ffn_wq_gate"])
    wu = tile_gu(inputs["L0_ffn_wq_up"])
    wd = np.ascontiguousarray(
        np.asarray(inputs["L0_ffn_wq_down"]).astype(np.uint8).T)

    sg = np.ascontiguousarray(np.asarray(inputs["L0_ffn_s_gate"],
                                         dtype=np.float32).reshape(NHB, 128).T)
    su = np.ascontiguousarray(np.asarray(inputs["L0_ffn_s_up"],
                                         dtype=np.float32).reshape(NHB, 128).T)
    sdB = np.ascontiguousarray(np.broadcast_to(
        np.asarray(inputs["L0_ffn_s_down"], dtype=np.float32)[None, :],
        (128, D)))
    ong = np.ascontiguousarray(np.asarray(inputs["L0_attn_o_norm_g"],
                                          dtype=np.float32).reshape(4, 128).T)
    qkp = np.zeros((128, 64), np.float32)
    qs = np.asarray(inputs["L0_attn_qk_scale"], dtype=np.float32)
    qo = np.asarray(inputs["L0_attn_qk_offset"], dtype=np.float32)
    qkp[:, 0:16] = qs[:, 0].reshape(16, 128).T
    qkp[:, 16:32] = qs[:, 1].reshape(16, 128).T
    qkp[:, 32:48] = qo[:, 0].reshape(16, 128).T
    qkp[:, 48:64] = qo[:, 1].reshape(16, 128).T

    w0 = np.asarray(inputs["L0_attn_w0"], dtype=np.float32)[h]
    w1 = np.asarray(inputs["L0_attn_w1"], dtype=np.float32)[h]
    w2 = np.asarray(inputs["L0_attn_w2"], dtype=np.float32)[h]

    dyn = np.zeros((1, 64), np.int32)
    dyn[0, 0] = 10240 * core
    dyn[0, 1] = 10240 * ((core + 7) % 8)
    dyn[0, 2] = 20480 * core
    dyn[0, 3] = 20480 * ((core + 7) % 8)
    dyn[0, 4] = HD * h
    dyn[0, 5] = h
    dyn[0, 6] = 6 * h

    return {
        "x": x, "Wqkv": np.asarray(inputs["L0_attn_Wqkv"], dtype=np.float32),
        "Wproj": Wp, "WgT": wg, "WuT": wu, "WdT": wd,
        "Wo": np.asarray(inputs["L0_attn_Wo"], dtype=np.float32),
        "w0m": np.ascontiguousarray(w0), "w0mT": np.ascontiguousarray(w0.T),
        "w1m": np.ascontiguousarray(w1), "w1mT": np.ascontiguousarray(w1.T),
        "w2m": np.ascontiguousarray(w2), "w2mT": np.ascontiguousarray(w2.T),
        "sg": sg, "su": su, "sdB": sdB, "ong": ong, "qkp": qkp,
        "bm1": np.full((128, 1), float(np.asarray(inputs["L0_attn_bm"])[h]),
                       np.float32),
        "bs1": np.full((128, 1), float(np.asarray(inputs["L0_attn_bs"])[h]),
                       np.float32),
        "pb1": np.full((128, 1), -200.0 if core == 0 else 0.0, np.float32),
        "dyn": dyn,
    }


def kernel(**inputs):
    nc = build()
    in_maps = [_host_stage(inputs, c) for c in range(8)]
    res = run_bass_kernel_spmd(nc, in_maps, core_ids=list(range(8)))
    out = np.concatenate([res.results[c]["out"] for c in range(8)], axis=0)
    return out.reshape(1, L, D).astype(np.float32)



# version 07 (final)
# speedup vs baseline: 1.0188x; 1.0188x over previous
import numpy as np
import ml_dtypes
import concourse.bass as bass
import concourse.mybir as mybir
from concourse.bass_utils import run_bass_kernel_spmd
from concourse.aot_env import allow_ici_all_to_all
from contextlib import ExitStack

F32 = mybir.dt.float32
BF16 = mybir.dt.bfloat16
U8 = mybir.dt.uint8
I32 = mybir.dt.int32
AF = mybir.ActivationFunctionType
ALU = mybir.AluOpType
AX = mybir.AxisListType

D = 2048
L = 8192
C = 1024
NCH = 8
HD = 512
NH = 4
HID = 5632
NHB = HID // 128
BLRI = float(0.01 + np.log(-np.expm1(-0.01)))
NS_COEFFS = [(4.0848, -6.8946, 2.927), (3.9505, -6.3029, 2.6377),
             (3.7418, -5.5913, 2.3037), (2.8769, -3.1427, 1.2046),
             (2.8366, -3.0525, 1.2012)]
S128 = 1.0 / float(np.sqrt(128.0))
BLK = 10240 * 1024
R_QFM, R_KFM, R_VFM, R_KTM, R_VTM = 0, 2048, 4096, 6144, 8192

_DEBUG = False
STAGE = 4
N_CHUNKS = NCH
NS_IT = 5

BF = ml_dtypes.bfloat16


class Sems:

    _DRAIN_ENGINES = None

    def __init__(self, nc, es):
        self.nc, self.es = nc, es
        self.h = {}
        self.cnt = {}
        self.eng_map = {
            mybir.EngineType.DVE: nc.vector,
            mybir.EngineType.Activation: nc.scalar,
            mybir.EngineType.Pool: nc.gpsimd,
        }

    def __getitem__(self, name):
        if name not in self.h:
            self.h[name] = self.es.enter_context(self.nc.semaphore(name))
            self.cnt[name] = 0
        return self.h[name]

    def inc(self, instr, name, n=1, raw=False):
        et = getattr(getattr(instr, "ins", None), "engine", None)
        if (not raw and n < 16 and et in self.eng_map
                and type(instr.ins).__name__ != "InstTriggeredCopy"):
            instr = self.eng_map[et].drain()
        instr.then_inc(self[name], n)
        self.cnt[name] += n
        return self.cnt[name]

    def val(self, name):
        _ = self[name]
        return self.cnt[name]

    def wait(self, eng, name, v):
        eng.wait_ge(self[name], v)


def build():
    nc = bass.Bass("TRN2", target_bir_lowering=False,
                   detect_race_conditions=False)

    x_in = nc.dram_tensor("x", [C, D], F32, kind="ExternalInput")
    Wqkv = nc.dram_tensor("Wqkv", [D, 3 * D], F32, kind="ExternalInput")
    WprojI = nc.dram_tensor("Wproj", [D, 24], F32, kind="ExternalInput")
    WgT = nc.dram_tensor("WgT", [HID, D], U8, kind="ExternalInput")
    WuT = nc.dram_tensor("WuT", [HID, D], U8, kind="ExternalInput")
    WdT = nc.dram_tensor("WdT", [HID, D], U8, kind="ExternalInput")
    WoI = nc.dram_tensor("Wo", [D, D], F32, kind="ExternalInput")
    wms = {}
    for nm in ("w0m", "w0mT", "w1m", "w1mT", "w2m", "w2mT"):
        wms[nm] = nc.dram_tensor(nm, [HD, HD], F32, kind="ExternalInput")
    sgI = nc.dram_tensor("sg", [128, NHB], F32, kind="ExternalInput")
    suI = nc.dram_tensor("su", [128, NHB], F32, kind="ExternalInput")
    sdBI = nc.dram_tensor("sdB", [128, D], F32, kind="ExternalInput")
    ongI = nc.dram_tensor("ong", [128, 4], F32, kind="ExternalInput")
    qkpI = nc.dram_tensor("qkp", [128, 64], F32, kind="ExternalInput")
    bmI = nc.dram_tensor("bm1", [128, 1], F32, kind="ExternalInput")
    bsI = nc.dram_tensor("bs1", [128, 1], F32, kind="ExternalInput")
    pbI = nc.dram_tensor("pb1", [128, 1], F32, kind="ExternalInput")
    dynI = nc.dram_tensor("dyn", [1, 64], I32, kind="ExternalInput")

    out_t = nc.dram_tensor("out", [C, D], F32, kind="ExternalOutput")
    dbg = {}
    if _DEBUG:
        dbg["d_xnfm"] = nc.dram_tensor("d_xnfm", [D, C], BF16, kind="ExternalOutput")
        dbg["d_ag"] = nc.dram_tensor("d_ag", [10240, 1024], BF16, kind="ExternalOutput")
        dbg["d_g"] = nc.dram_tensor("d_g", [C, D], BF16, kind="ExternalOutput")
        dbg["d_attn"] = nc.dram_tensor("d_attn", [D, C], BF16, kind="ExternalOutput")
        dbg["d_ttt"] = nc.dram_tensor("d_ttt", [8 * HD, C], BF16, kind="ExternalOutput")
        dbg["d_proj"] = nc.dram_tensor("d_proj", [C, 24], F32, kind="ExternalOutput")
        dbg["d_W0"] = nc.dram_tensor("d_W0", [128, 4 * HD], F32, kind="ExternalOutput")
        dbg["d_M0"] = nc.dram_tensor("d_M0", [128, 4 * HD], BF16, kind="ExternalOutput")
        dbg["d_X0"] = nc.dram_tensor("d_X0", [128, 4 * HD], BF16, kind="ExternalOutput")
        dbg["d_A0"] = nc.dram_tensor("d_A0", [128, 4 * HD], BF16, kind="ExternalOutput")
        dbg["d_kv"] = nc.dram_tensor("d_kv", [128, 5 * 4096], BF16, kind="ExternalOutput")
        dbg["d_tm"] = nc.dram_tensor("d_tm", [128, 16384], BF16, kind="ExternalOutput")
        dbg["d_pj"] = nc.dram_tensor("d_pj", [128, 96], F32, kind="ExternalOutput")

    ag_in = nc.dram_tensor("ag_in", [10240, 1024], BF16)
    ag_out = nc.dram_tensor("ag_out", [8 * 10240, 1024], BF16)
    pj_in = nc.dram_tensor("pj_in", [C, 24], F32)
    pj_out = nc.dram_tensor("pj_out", [8 * C, 24], F32)
    ta_in = nc.dram_tensor("ta_in", [8 * HD, C], BF16)
    ta_out = nc.dram_tensor("ta_out", [8 * HD, C], BF16)
    attnscr = nc.dram_tensor("attnscr", [D, C], BF16)
    blk_own = nc.dram_tensor("blk_own", [10240, 1024], BF16)
    blk_halo = nc.dram_tensor("blk_halo", [10240, 1024], BF16)
    qsc = nc.dram_tensor("qsc", [8 * HD, 1024], BF16)
    ksc = nc.dram_tensor("ksc", [8 * HD, 1024], BF16)
    vsc = nc.dram_tensor("vsc", [8 * HD, 1024], BF16)
    ktsc = nc.dram_tensor("ktsc", [8 * C, HD], BF16)
    vtsc = nc.dram_tensor("vtsc", [8 * C, HD], BF16)
    pjsc = nc.dram_tensor("pjsc", [8 * C, 5], F32)
    gscr = nc.dram_tensor("gscr", [C, D], BF16)
    rowscr = nc.dram_tensor("rowscr", [8, 1024], F32)

    ident_i = nc.inline_tensor(np.eye(128, dtype=BF), "identc")
    ones128_i = nc.inline_tensor(np.ones((128, 128), dtype=BF), "ones128c")
    onescol_i = nc.inline_tensor(np.ones((128, 1), dtype=BF), "onescolc")
    onesrow_i = nc.inline_tensor(np.ones((1, 128), dtype=BF), "onesrowc")
    m128r_i = nc.inline_tensor(np.full((1, 128), -128.0, dtype=BF), "m128rc")
    m128r512_i = nc.inline_tensor(np.full((1, 512), -128.0, dtype=BF), "m128r512c")
    jj = np.arange(512)[None, :]
    pp = np.arange(128)[:, None]
    mk = [(pp + d_ >= jj).astype(BF) for d_ in (0, 128, 256, 384)]
    mk += [(jj >= pp + d_).astype(BF) for d_ in (0, 128, 256, 384)]
    masks_i = nc.inline_tensor(np.concatenate(mk, axis=1), "masksc")
    dummy_i = nc.inline_tensor(np.zeros((1, 8), np.float32), "dummyc")
    CONSTS = [1e-5, 1.0, 1e-7, BLRI, 1e-10, 0.5, 2.0, 0.0]
    consts_i = nc.inline_tensor(
        np.broadcast_to(np.asarray(CONSTS, np.float32)[None, :],
                        (128, len(CONSTS))).copy(), "constsc")

    es = ExitStack()
    sb = lambda n, s, d: es.enter_context(nc.sbuf_tensor(n, s, d))
    S = Sems(nc, es)
    NENG = 5
    phase = [0]

    with es:
        s_ident = sb("s_ident", [128, 128], BF16)
        s_ones128 = sb("s_ones128", [128, 128], BF16)
        s_onescol = sb("s_onescol", [128, 1], BF16)
        s_onesrow = sb("s_onesrow", [1, 128], BF16)
        s_m128r = sb("s_m128r", [1, 128], BF16)
        s_m128r512 = sb("s_m128r512", [1, 512], BF16)
        s_sg = sb("s_sg", [128, NHB], F32)
        s_su = sb("s_su", [128, NHB], F32)
        s_sdB = sb("s_sdB", [128, D], F32)
        s_ong = sb("s_ong", [128, 4], F32)
        s_qkp = sb("s_qkp", [128, 64], F32)
        s_bm = sb("s_bm", [128, 1], F32)
        s_bs = sb("s_bs", [128, 1], F32)
        s_pb = sb("s_pb", [128, 1], F32)
        s_cs_hi = sb("s_cs_hi", [1, 1024], BF16)
        s_cs_lo = sb("s_cs_lo", [1, 1024], BF16)
        s_cs_f = sb("s_cs_f", [1, 1024], F32)
        s_consts = sb("s_consts", [128, 8], F32)
        for j, v in enumerate(CONSTS):
            nc.const_aps.register(F32, v, s_consts[:, j:j + 1])

        pb = [es.enter_context(nc.psum_tensor(f"psb{i}", [128, 512], F32))
              for i in range(8)]

        s_dummy = sb("s_dummy", [1, 8], F32)

        def eng_enter(eng):
            eng.wait_ge(S["ph"], NENG * phase[0])

        def leave(eng):
            import concourse.mybir as _mb
            nm = eng.engine if hasattr(eng, "engine") else None
            if nm == _mb.EngineType.DVE:
                eng.tensor_copy(s_dummy[:, 0:4], s_dummy[:, 4:8]).then_inc(
                    S["ph"], 1)
            elif nm == _mb.EngineType.Activation:
                eng.copy(s_dummy[:, 0:4], s_dummy[:, 4:8]).then_inc(S["ph"], 1)
            else:
                eng.nop().then_inc(S["ph"], 1)
            S.cnt["ph"] += 1

        def barrier():
            phase[0] += 1

        pxn = ExitStack()
        s_xnfm = pxn.enter_context(
            nc.sbuf_tensor("s_xnfm", [128, 16 * C], BF16, side="right"))
        p01 = ExitStack()
        psb = lambda n, s, d: p01.enter_context(nc.sbuf_tensor(n, s, d))
        s_xin = [psb(f"s_xin{i}", [128, D], F32) for i in range(2)]
        s_xntm = psb("s_xntm", [128, D], BF16)
        s_sqscr = psb("s_sqscr", [128, D], F32)
        s_nm = psb("s_nm", [128, 4], F32)
        s_ws = psb("s_ws", [128, 16 * D], BF16)
        s_wsf = [psb(f"s_wsf{i}", [128, D], F32) for i in range(2)]
        s_stg = [psb(f"s_stg{i}", [128, 1024], BF16) for i in range(2)]
        s_stg2 = [psb(f"s_stg2{i}", [128, 2048], BF16) for i in range(2)]
        s_pjw = psb("s_pjw", [128, 16 * 24], BF16)
        s_pjwf = psb("s_pjwf", [128, 16 * 24], F32)
        s_pjs = [psb(f"s_pjs{i}", [128, 24], F32) for i in range(2)]
        ptr = [pb[4 + i][:, 0:64].bitcast(BF16) for i in range(4)]

        EV = []
        for st_ in range(3):
            EV += [("fm", st_, cb_) for cb_ in range(16)]
            if st_ >= 1:
                EV += [("tm", st_ - 1, tt_) for tt_ in range(8)]
                if st_ == 2:
                    ev2 = []
                    for tt_ in range(8):
                        ev2.append(("tm", 1, tt_))
                        ev2.append(("pj", 1, tt_))
                    EV = EV[:-8] + ev2
        POS = {e: p for p, e in enumerate(EV)}
        NSTAGE = len(EV)

        _sc_p1 = nc.enter_named_scope("p1", False)[0]
        with nc.Block() as blk:
            @blk.sync
            def _(sync):
                for dst, src in ((s_ident, ident_i), (s_ones128, ones128_i),
                                 (s_onescol, onescol_i), (s_onesrow, onesrow_i),
                                 (s_m128r, m128r_i), (s_m128r512, m128r512_i),
                                 (s_sg, sgI), (s_su, suI), (s_sdB, sdBI),
                                 (s_ong, ongI), (s_qkp, qkpI), (s_bm, bmI),
                                 (s_bs, bsI), (s_pb, pbI),
                                 (s_consts, consts_i),
                                 (s_dummy, dummy_i)):
                    S.inc(sync.dma_start(out=dst[:, :], in_=src[:, :]),
                          "cst", 16)
                for kd in range(16):
                    S.inc(sync.dma_start(
                        out=s_pjwf[:, 24 * kd:24 * (kd + 1)],
                        in_=WprojI[128 * kd:128 * (kd + 1), :]), "cst", 16)
                for tt in range(8):
                    if tt >= 2:
                        S.wait(sync, "lnt", tt - 1)
                    S.inc(sync.dma_start(out=s_xin[tt % 2][:, :],
                                         in_=x_in[128 * tt:128 * (tt + 1), :]),
                          "xld", 16)
                for st in range(3):
                    for kd in range(16):
                        n = 16 * st + kd
                        if n >= 2:
                            S.wait(sync, "wcv", n - 1)
                        S.inc(sync.dma_start(
                            out=s_wsf[kd % 2][:, :],
                            in_=Wqkv[128 * kd:128 * (kd + 1),
                                     2048 * st:2048 * (st + 1)]), "wld", 16)
                sync.wait_ge(S["xld"], 8 * 16)
                sync.wait_ge(S["wld"], 48 * 16)
                leave(sync)

            @blk.vector
            def _(vector):
                vector.wait_ge(S["cst"], 32 * 16)
                vector.tensor_copy(s_pjw[:, :], s_pjwf[:, :])
                for tt in range(8):
                    vector.wait_ge(S["xld"], 16 * min(tt + 2, 8))
                    xb = s_xin[tt % 2]
                    vector.reduce_sum(s_nm[:, 0:1], xb[:, :], axis=AX.X,
                                      negate=True)
                    vector.drain()
                    vector.tensor_scalar(s_nm[:, 1:2], s_nm[:, 0:1], 1.0 / D,
                                         None, ALU.mult)
                    vector.drain()
                    i = vector.tensor_scalar(xb[:, :], xb[:, :], s_nm[:, 1:2],
                                             None, ALU.add)
                    S.inc(i, "lnc", 1)
                    S.wait(vector, "lns", tt + 1)
                    vector.reciprocal(s_nm[:, 2:3], s_nm[:, 3:4])
                    vector.drain()
                    if tt >= 1:
                        S.wait(vector, "trp", 16 * tt)
                    i = vector.tensor_scalar(s_xntm[:, :], xb[:, :],
                                             s_nm[:, 2:3], None, ALU.mult)
                    S.inc(i, "lnt", 1)
                    for kd in range(16):
                        n = 16 * tt + kd
                        S.wait(vector, "trp", n + 1)
                        i = vector.tensor_copy(
                            s_xnfm[:, C * kd + 128 * tt:C * kd + 128 * (tt + 1)],
                            ptr[n % 4])
                        S.inc(i, "tre", 1)
                for st in range(3):
                    for kd in range(16):
                        n = 16 * st + kd
                        vector.wait_ge(S["wld"], 16 * min(n + 2, 48))
                        if st >= 1 and kd == 0:
                            S.wait(vector, "fmm", 16 * st)
                            if st >= 2:
                                S.wait(vector, "tmm", 8 * (st - 1))
                        i = vector.tensor_copy(s_ws[:, D * kd:D * (kd + 1)],
                                               s_wsf[kd % 2][:, :])
                        S.inc(i, "wcv", 1)
                S.wait(vector, "cse", 1)
                vector.tensor_copy(s_cs_hi[:, :], s_cs_f[:, :])
                vector.drain()
                i = vector.tensor_tensor(s_cs_lo[:, :], s_cs_f[:, :],
                                         s_cs_hi[:, :], ALU.subtract)
                S.inc(i, "csd", 1)
                leave(vector)

            @blk.scalar
            def _(scalar):
                for tt in range(8):
                    S.wait(scalar, "lnc", tt + 1)
                    xb = s_xin[tt % 2]
                    scalar.activation(s_sqscr[:, :], xb[:, :], AF.Square,
                                      accum_out=s_nm[:, 3:4])
                    scalar.drain()
                    i = scalar.activation(s_nm[:, 3:4], s_nm[:, 3:4], AF.Sqrt,
                                          bias=1e-5, scale=1.0 / D)
                    S.inc(i, "lns", 1)
                for p, ev in enumerate(EV):
                    if p >= 2:
                        S.wait(scalar, "agw", 16 * p)
                    kind = ev[0]
                    if kind == "fm":
                        _, st, cb = ev
                        n = 16 * st + cb
                        g = (cb % 2) * 2
                        S.wait(scalar, "fmm", n + 1)
                        sg_ = s_stg[cb % 2]
                        scalar.activation(sg_[:, 0:512], pb[g][:, :], AF.Copy)
                        i = scalar.activation(sg_[:, 512:1024],
                                              pb[g + 1][:, :], AF.Copy)
                        S.inc(i, "evq", 1)
                        if st == 0 and cb == 15:
                            S.wait(scalar, "csm", 1)
                            scalar.activation(s_cs_f[:, 0:512],
                                              pb[4][0:1, 0:512], AF.Copy)
                            i = scalar.activation(s_cs_f[:, 512:1024],
                                                  pb[5][0:1, 0:512], AF.Copy)
                            S.inc(i, "cse", 1)
                    elif kind == "tm":
                        _, sti, tt = ev
                        n = 8 * sti + tt
                        S.wait(scalar, "tmm", n + 1)
                        sg_ = s_stg2[tt % 2]
                        for j in range(4):
                            i = scalar.activation(sg_[:, 512 * j:512 * (j + 1)],
                                                  pb[4 + j][:, :], AF.Copy)
                        S.inc(i, "evq", 1)
                    else:
                        _, _, tt = ev
                        i = scalar.activation(s_pjs[tt % 2][:, :],
                                              pb[3][:, 0:24], AF.Copy)
                        S.inc(i, "evq", 1)
                leave(scalar)

            @blk.tensor
            def _(tensor):
                tensor.wait_ge(S["cst"], 32 * 16)
                for tt in range(8):
                    S.wait(tensor, "lnt", tt + 1)
                    for kd in range(16):
                        n = 16 * tt + kd
                        if n >= 4:
                            S.wait(tensor, "tre", n - 3)
                        i = tensor.transpose(ptr[n % 4],
                                             s_xntm[:, 128 * kd:128 * (kd + 1)],
                                             s_ident[:, :])
                        S.inc(i, "trp", 1)
                S.wait(tensor, "tre", 128)
                for st in range(3):
                    S.wait(tensor, "wcv", 16 * (st + 1))
                    for cb in range(16):
                        n = 16 * st + cb
                        g = (cb % 2) * 2
                        if n >= 2:
                            S.wait(tensor, "evq",
                                   POS[("fm", (n - 2) // 16, (n - 2) % 16)] + 1)
                        for kd in range(16):
                            lhs = s_ws[:, D * kd + 128 * cb:D * kd + 128 * (cb + 1)]
                            for tb in range(2):
                                i = tensor.matmul(
                                    pb[g + tb][:, :], lhs,
                                    s_xnfm[:, C * kd + 512 * tb:
                                           C * kd + 512 * (tb + 1)],
                                    start=(kd == 0), stop=(kd == 15))
                        S.inc(i, "fmm", 1)
                    if st == 0:
                        for kd in range(16):
                            for tb in range(2):
                                i = tensor.matmul(
                                    pb[4 + tb][0:1, 0:512], s_onescol[:, :],
                                    s_xnfm[:, C * kd + 512 * tb:
                                           C * kd + 512 * (tb + 1)],
                                    start=(kd == 0), stop=(kd == 15))
                        S.inc(i, "csm", 1)
                    else:
                        sti = st - 1
                        if st == 1:
                            S.wait(tensor, "cse", 1)
                        if st == 2:
                            S.wait(tensor, "evq", POS[("fm", 2, 15)] + 1)
                        for tt in range(8):
                            n = 8 * sti + tt
                            if n >= 1:
                                pv = (POS[("tm", 0, tt - 1)] if sti == 0 or tt >= 1
                                      else POS[("tm", 0, 7)])
                                if sti == 1 and tt >= 1:
                                    pv = POS[("tm", 1, tt - 1)]
                                elif sti == 1 and tt == 0:
                                    pv = POS[("tm", 0, 7)]
                                S.wait(tensor, "evq", pv + 1)
                            if st == 2 and tt >= 1:
                                S.wait(tensor, "evq", POS[("pj", 1, tt - 1)] + 1)
                            for kd in range(16):
                                lhs = s_xnfm[:, C * kd + 128 * tt:
                                             C * kd + 128 * (tt + 1)]
                                for j in range(4):
                                    i = tensor.matmul(
                                        pb[4 + j][:, :], lhs,
                                        s_ws[:, D * kd + 512 * j:
                                             D * kd + 512 * (j + 1)],
                                        start=(kd == 0), stop=(kd == 15))
                                if st == 2:
                                    i = tensor.matmul(
                                        pb[3][:, 0:24], lhs,
                                        s_pjw[:, 24 * kd:24 * (kd + 1)],
                                        start=(kd == 0), stop=(kd == 15))
                            S.inc(i, "tmm", 1)
                leave(tensor)

            @blk.gpsimd
            def _(gpsimd):
                for p, ev in enumerate(EV):
                    S.wait(gpsimd, "evq", p + 1)
                    kind = ev[0]
                    if kind == "fm":
                        _, st, cb = ev
                        base = (R_QFM, R_KFM, R_VFM)[st]
                        S.inc(gpsimd.dma_start(
                            out=ag_in[base + 128 * cb:base + 128 * (cb + 1), :],
                            in_=s_stg[cb % 2][:, :]), "agw", 16)
                    elif kind == "tm":
                        _, sti, tt = ev
                        base = R_KTM if sti == 0 else R_VTM
                        dst = bass.AP(ag_in, base * 1024 + 2048 * 128 * tt,
                                      [[2048, 128], [1, 2048]])
                        S.inc(gpsimd.dma_start(out=dst,
                                               in_=s_stg2[tt % 2][:, :]),
                              "agw", 16)
                    else:
                        _, _, tt = ev
                        S.inc(gpsimd.dma_start(
                            out=pj_in[128 * tt:128 * (tt + 1), :],
                            in_=s_pjs[tt % 2][:, :]), "agw", 16)
                gpsimd.wait_ge(S["agw"], S.val("agw"))
                if _DEBUG:
                    S.inc(gpsimd.dma_start(out=dbg["d_ag"].ap(),
                                           in_=ag_in.ap()), "dbgw", 16)
                    for kd in range(16):
                        S.inc(gpsimd.dma_start(
                            out=dbg["d_xnfm"][128 * kd:128 * (kd + 1), :],
                            in_=s_xnfm[:, C * kd:C * (kd + 1)]), "dbgw", 16)
                    gpsimd.wait_ge(S["dbgw"], S.val("dbgw"))
                leave(gpsimd)

        barrier()
        p01.close()

        p2 = ExitStack()
        s_hid = p2.enter_context(nc.sbuf_tensor("s_hid", [128, NHB * C], BF16))
        p2a = ExitStack()
        s_wu8 = [p2a.enter_context(nc.sbuf_tensor(f"s_wu8_{i}", [128, 2048], U8,
                                                  side="right"))
                 for i in range(4)]
        s_wgu = p2a.enter_context(
            nc.sbuf_tensor("s_wgu", [128, 2 * 4096], BF16, side="right"))
        s_tg = p2a.enter_context(
            nc.sbuf_tensor("s_tg", [128, 2 * C], F32, side="right"))

        nc.leave_named_scope("p1", _sc_p1, False)
        _sc_p2a = nc.enter_named_scope("p2a", False)[0]
        with nc.Block() as blk:
            @blk.sync
            def _(sync):
                eng_enter(sync)
                for hb in range(NHB):
                    if hb >= 2:
                        S.wait(sync, "gcv", 2 * (hb - 1))
                    S.inc(sync.dma_start(out=s_wu8[2 * (hb % 2)][:, :],
                                         in_=WgT[128 * hb:128 * (hb + 1), :]),
                          "gld", 16)
                    S.inc(sync.dma_start(out=s_wu8[2 * (hb % 2) + 1][:, :],
                                         in_=WuT[128 * hb:128 * (hb + 1), :]),
                          "gld", 16)
                leave(sync)

            @blk.vector
            def _(vector):
                eng_enter(vector)
                for hb in range(NHB):
                    vector.wait_ge(S["gld"], 32 * min(hb + 2, NHB))
                    if hb >= 2:
                        S.wait(vector, "gmm", hb - 1)
                    o = (hb % 2) * 4096
                    vector.tensor_copy(s_wgu[:, o:o + 2048],
                                       s_wu8[2 * (hb % 2)][:, :])
                    i = vector.tensor_copy(s_wgu[:, o + 2048:o + 4096],
                                           s_wu8[2 * (hb % 2) + 1][:, :])
                    S.inc(i, "gcv", 2)
                    S.wait(vector, "ges", hb + 1)
                    g0 = (hb % 2) * 4
                    to = (hb % 2) * C
                    for tb in range(2):
                        i = vector.scalar_tensor_tensor(
                            s_hid[:, C * hb + 512 * tb:C * hb + 512 * (tb + 1)],
                            pb[g0 + 2 + tb][:, :], s_su[:, hb:hb + 1],
                            s_tg[:, to + 512 * tb:to + 512 * (tb + 1)],
                            ALU.mult, ALU.mult)
                    S.inc(i, "hev", 1)
                leave(vector)

            @blk.scalar
            def _(scalar):
                eng_enter(scalar)
                for hb in range(NHB):
                    S.wait(scalar, "gmm", hb + 1)
                    if hb >= 2:
                        S.wait(scalar, "hev", hb - 1)
                    g0 = (hb % 2) * 4
                    to = (hb % 2) * C
                    for tb in range(2):
                        i = scalar.activation(s_tg[:, to + 512 * tb:
                                                   to + 512 * (tb + 1)],
                                              pb[g0 + tb][:, :], AF.Silu,
                                              scale=s_sg[:, hb:hb + 1])
                    S.inc(i, "ges", 1)
                leave(scalar)

            @blk.tensor
            def _(tensor):
                eng_enter(tensor)
                S.wait(tensor, "csd", 1)
                for hb in range(NHB):
                    S.wait(tensor, "gcv", 2 * (hb + 1))
                    g0 = (hb % 2) * 4
                    o = (hb % 2) * 4096
                    if hb >= 2:
                        S.wait(tensor, "hev", hb - 1)
                    for kd in range(16):
                        lg = s_wgu[:, o + 128 * kd:o + 128 * (kd + 1)]
                        lu = s_wgu[:, o + 2048 + 128 * kd:o + 2048 + 128 * (kd + 1)]
                        for tb in range(2):
                            rhs = s_xnfm[:, C * kd + 512 * tb:C * kd + 512 * (tb + 1)]
                            tensor.matmul(pb[g0 + tb][:, :], lg, rhs,
                                          start=(kd == 0), stop=False)
                            tensor.matmul(pb[g0 + 2 + tb][:, :], lu, rhs,
                                          start=(kd == 0), stop=False)
                    for tb in range(2):
                        cs_h = s_cs_hi[:, 512 * tb:512 * (tb + 1)]
                        cs_l = s_cs_lo[:, 512 * tb:512 * (tb + 1)]
                        tensor.matmul(pb[g0 + tb][:, :], s_m128r[:, :], cs_h,
                                      start=False, stop=False)
                        tensor.matmul(pb[g0 + tb][:, :], s_m128r[:, :], cs_l,
                                      start=False, stop=True)
                        tensor.matmul(pb[g0 + 2 + tb][:, :], s_m128r[:, :], cs_h,
                                      start=False, stop=False)
                        i = tensor.matmul(pb[g0 + 2 + tb][:, :], s_m128r[:, :],
                                          cs_l, start=False, stop=True)
                    S.inc(i, "gmm", 1)
                leave(tensor)

            @blk.gpsimd
            def _(gpsimd):
                eng_enter(gpsimd)
                leave(gpsimd)

        barrier()
        p2a.close()
        pxn.close()

        s_wd = p2.enter_context(nc.sbuf_tensor("s_wd", [128, NHB * 1024], BF16))
        s_wd8 = [p2.enter_context(nc.sbuf_tensor(f"s_wd8_{i}", [128, 1024], U8))
                 for i in range(2)]
        s_hsum = s_cs_hi
        s_hsumlo = s_cs_lo
        s_hsf = s_cs_f
        s_dt = p2.enter_context(nc.sbuf_tensor("s_dt", [128, C], F32))
        s_gstg = [p2.enter_context(nc.sbuf_tensor(f"s_gstg{i}", [128, C], BF16))
                  for i in range(2)]

        nc.leave_named_scope("p2a", _sc_p2a, False)
        _sc_p2b = nc.enter_named_scope("p2b", False)[0]
        with nc.Block() as blk:
            @blk.sync
            def _(sync):
                eng_enter(sync)
                for half in range(2):
                    for kh in range(NHB):
                        ik = half * NHB + kh
                        if ik >= 2:
                            S.wait(sync, "dcv", ik - 1)
                        S.inc(sync.dma_start(
                            out=s_wd8[ik % 2][:, :],
                            in_=WdT[128 * kh:128 * (kh + 1),
                                    1024 * half:1024 * (half + 1)]), "dld", 16)
                leave(sync)

            @blk.vector
            def _(vector):
                eng_enter(vector)
                S.wait(vector, "hsf", 1)
                vector.tensor_copy(s_hsum[:, :], s_hsf[:, :])
                vector.drain()
                i = vector.tensor_tensor(s_hsumlo[:, :], s_hsf[:, :],
                                         s_hsum[:, :], ALU.subtract)
                S.inc(i, "hse", 1)
                for half in range(2):
                    for kh in range(NHB):
                        ik = half * NHB + kh
                        vector.wait_ge(S["dld"], 16 * min(ik + 2, 2 * NHB))
                        if half == 1 and kh == 0:
                            S.wait(vector, "dmm", 8)
                        i = vector.tensor_copy(s_wd[:, 1024 * kh:1024 * (kh + 1)],
                                               s_wd8[ik % 2][:, :])
                        S.inc(i, "dcv", 1)
                    for tt in range(8):
                        it = half * 8 + tt
                        g0 = (it % 2) * 2
                        S.wait(vector, "dmm", it + 1)
                        if it >= 1:
                            S.wait(vector, "gsp", it)
                        for nb in range(2):
                            i = vector.tensor_tensor(
                                s_dt[:, 512 * nb:512 * (nb + 1)],
                                pb[g0 + nb][:, :],
                                s_sdB[:, 1024 * half + 512 * nb:
                                      1024 * half + 512 * (nb + 1)],
                                ALU.mult)
                        S.inc(i, "dsc", 1)
                leave(vector)

            @blk.tensor
            def _(tensor):
                eng_enter(tensor)
                for hb in range(NHB):
                    for tb in range(2):
                        i = tensor.matmul(
                            pb[6 + tb][0:1, 0:512], s_onescol[:, :],
                            s_hid[:, C * hb + 512 * tb:C * hb + 512 * (tb + 1)],
                            start=(hb == 0), stop=(hb == NHB - 1))
                S.inc(i, "hsm", 1)
                for half in range(2):
                    S.wait(tensor, "dcv", NHB * (half + 1))
                    for tt in range(8):
                        it = half * 8 + tt
                        g0 = (it % 2) * 2
                        if it >= 2:
                            S.wait(tensor, "dsc", it - 1)
                        for kh in range(NHB):
                            lhs = s_hid[:, C * kh + 128 * tt:C * kh + 128 * (tt + 1)]
                            for nb in range(2):
                                tensor.matmul(
                                    pb[g0 + nb][:, :], lhs,
                                    s_wd[:, 1024 * kh + 512 * nb:
                                         1024 * kh + 512 * (nb + 1)],
                                    start=(kh == 0), stop=False)
                        S.wait(tensor, "hse", 1)
                        for nb in range(2):
                            tensor.matmul(pb[g0 + nb][:, :],
                                          s_hsum[0:1, 128 * tt:128 * (tt + 1)],
                                          s_m128r512[:, :],
                                          start=False, stop=False)
                            i = tensor.matmul(
                                pb[g0 + nb][:, :],
                                s_hsumlo[0:1, 128 * tt:128 * (tt + 1)],
                                s_m128r512[:, :], start=False, stop=True)
                        S.inc(i, "dmm", 1)
                leave(tensor)

            @blk.scalar
            def _(scalar):
                eng_enter(scalar)
                S.wait(scalar, "hsm", 1)
                scalar.activation(s_hsf[0:1, 0:512], pb[6][0:1, 0:512], AF.Copy)
                i = scalar.activation(s_hsf[0:1, 512:1024], pb[7][0:1, 0:512],
                                      AF.Copy)
                S.inc(i, "hsf", 1)
                for half in range(2):
                    for tt in range(8):
                        it = half * 8 + tt
                        S.wait(scalar, "dsc", it + 1)
                        if it >= 2:
                            S.wait(scalar, "gwr", 16 * it)
                        i = scalar.activation(s_gstg[it % 2][:, :],
                                              s_dt[:, :], AF.Silu)
                        S.inc(i, "gsp", 1)
                leave(scalar)

            @blk.gpsimd
            def _(gpsimd):
                eng_enter(gpsimd)
                i = gpsimd.collective_compute(
                    "AllGather", ALU.bypass, replica_groups=[list(range(8))],
                    ins=[ag_in.ap().opt()], outs=[ag_out.ap().opt()])
                S.inc(i, "agc", 1, raw=True)
                i = gpsimd.collective_compute(
                    "AllGather", ALU.bypass, replica_groups=[list(range(8))],
                    ins=[pj_in.ap().opt()], outs=[pj_out.ap().opt()])
                S.inc(i, "agc", 1, raw=True)
                for half in range(2):
                    for tt in range(8):
                        it = half * 8 + tt
                        S.wait(gpsimd, "gsp", it + 1)
                        dst = bass.AP(gscr, (128 * tt) * D + 1024 * half,
                                      [[D, 128], [1, 1024]])
                        S.inc(gpsimd.dma_start(out=dst, in_=s_gstg[it % 2][:, :]),
                              "gwr", 16)
                gpsimd.wait_ge(S["gwr"], S.val("gwr"))
                if _DEBUG:
                    S.inc(gpsimd.dma_start(out=dbg["d_g"].ap(), in_=gscr.ap()),
                          "dbgw", 16)
                    gpsimd.wait_ge(S["dbgw"], S.val("dbgw"))
                leave(gpsimd)

        barrier()
        p2.close()

        if STAGE >= 3:
            for nm_ in ("aex_p", "asc_p", "aex_t", "amk_t"):
                S.cnt[nm_] = 0

        p3 = ExitStack()
        q3 = lambda n, sh, dt_: p3.enter_context(nc.sbuf_tensor(n, sh, dt_))
        s_aq = [q3(f"s_aq{i}", [128, 1024], BF16) for i in range(2)]
        s_ak = [q3(f"s_ak{i}", [128, 2048], BF16) for i in range(2)]
        s_av = [q3(f"s_av{i}", [128, 2048], BF16) for i in range(2)]
        s_E = q3("s_E", [128, 16 * 512], BF16)
        s_masks = q3("s_masks", [128, 8 * 512], BF16)
        s_sq3 = q3("s_sq3", [128, 1024], BF16)
        s_r3 = q3("s_r3", [128, 8], F32)
        s_rw3 = q3("s_rw3", [1, 3072], F32)
        s_rc = q3("s_rc", [1, 8], F32)
        s_rcb = q3("s_rcb", [1, 8], BF16)
        s_rinv = q3("s_rinv", [128, 512], F32)
        s_at = [q3(f"s_at{i}", [128, 1024], BF16) for i in range(2)]

        def kt_plan(qb):
            plan = []
            qoff = 512 * qb
            for kt in range(8):
                d_ = 128 * kt - qoff
                if d_ >= 512 - 1:
                    plan.append((kt, None))
                elif d_ > -128:
                    plan.append((kt, (0, 128, 256, 384).index(d_)))
            for kt in range(8, 16):
                d_ = 128 * (kt - 8) - qoff
                if d_ <= -128:
                    plan.append((kt, None))
                elif d_ < 512:
                    plan.append((kt, 4 + (0, 128, 256, 384).index(d_)))
            return plan

        PLANS = [kt_plan(0), kt_plan(1)]
        NMSK = [sum(1 for _, mi in p if mi is not None) for p in PLANS]

        nc.leave_named_scope("p2b", _sc_p2b, False)
        _sc_p3 = nc.enter_named_scope("p3", False)[0]
        with nc.Block() as blk:
            @blk.gpsimd
            def _(gpsimd):
                eng_enter(gpsimd)
                S.inc(gpsimd.dma_start(out=s_masks[:, :], in_=masks_i[:, :]),
                      "a3m", 16)
                gpsimd.wait_ge(S["agc"], 2)
                with gpsimd.register("r_halo") as r_halo_reg:
                    gpsimd.reg_load(r_halo_reg, dynI[0:1, 1:2])
                    halo = gpsimd.snap(r_halo_reg, min_val=0,
                                       max_val=10240 * 7)
                    S.inc(gpsimd.dma_start(
                        out=blk_halo[0:2048, :],
                        in_=ag_out[bass.ds(halo + R_KFM, 2048), :],
                        bounds_check=None), "stg", 16)
                    S.inc(gpsimd.dma_start(
                        out=blk_halo[2048:4096, :],
                        in_=ag_out[bass.ds(halo + R_VTM, 2048), :],
                        bounds_check=None), "stg", 16)
                gpsimd.wait_ge(S["stg"], 32)
                for a in range(16):
                    if a >= 2:
                        S.wait(gpsimd, "apv", 2 * (a - 1))
                    S.inc(gpsimd.dma_start(
                        out=s_aq[a % 2][:, :],
                        in_=ag_in[R_QFM + 128 * a:R_QFM + 128 * (a + 1), :]),
                        "a3l", 16)
                    S.inc(gpsimd.dma_start(
                        out=s_ak[a % 2][:, 0:1024],
                        in_=blk_halo[128 * a:128 * (a + 1), :]),
                        "a3l", 16)
                    S.inc(gpsimd.dma_start(
                        out=s_ak[a % 2][:, 1024:2048],
                        in_=ag_in[R_KFM + 128 * a:R_KFM + 128 * (a + 1), :]),
                        "a3l", 16)
                    for src, ro, fo in ((blk_halo, 2048 * 1024, 0),
                                        (ag_in, R_VTM * 1024, 1024)):
                        S.inc(gpsimd.dma_start(
                            out=s_av[a % 2][:, fo:fo + 1024]
                            .rearrange("p (tk c) -> p tk c", c=128),
                            in_=bass.AP(src, ro + 128 * a,
                                        [[2048, 128], [2048 * 128, 8],
                                         [1, 128]])), "a3l", 16)
                gpsimd.wait_ge(S["a3w"], 16 * 16)
                if _DEBUG:
                    S.inc(gpsimd.dma_start(out=dbg["d_attn"].ap(),
                                           in_=attnscr.ap()), "dbgw", 16)
                    gpsimd.wait_ge(S["dbgw"], S.val("dbgw"))
                leave(gpsimd)

            @blk.vector
            def _(vector):
                eng_enter(vector)
                for a in range(16):
                    vector.wait_ge(S["a3l"], 5 * 16 * min(a + 2, 16))
                    aq, ak = s_aq[a % 2], s_ak[a % 2]
                    vector.tensor_scalar(aq[:, :], aq[:, :],
                                         s_qkp[:, a:a + 1], None, ALU.mult)
                    vector.tensor_scalar(aq[:, :], aq[:, :],
                                         s_qkp[:, 32 + a:33 + a], None, ALU.add)
                    vector.tensor_scalar(ak[:, :], ak[:, :],
                                         s_qkp[:, 16 + a:17 + a], None,
                                         ALU.mult)
                    i = vector.tensor_scalar(ak[:, :], ak[:, :],
                                             s_qkp[:, 48 + a:49 + a], None,
                                             ALU.add)
                    S.inc(i, "a3s", 1)
                    S.wait(vector, "a3r", 3 * (a + 1))
                    vector.reduce_max(s_rc[0:1, 0:1], s_rw3[0:1, 0:1024],
                                      axis=AX.X)
                    vector.reduce_max(s_rc[0:1, 1:2], s_rw3[0:1, 1024:2048],
                                      axis=AX.X)
                    vector.reduce_max(s_rc[0:1, 2:3], s_rw3[0:1, 2048:3072],
                                      axis=AX.X)
                    vector.drain()
                    vector.tensor_max(s_rc[0:1, 1:2], s_rc[0:1, 1:2],
                                      s_rc[0:1, 2:3])
                    vector.drain()
                    i = vector.tensor_mul(s_rc[0:1, 2:3], s_rc[0:1, 0:1],
                                          s_rc[0:1, 1:2])
                    S.inc(i, "a3c", 1)
                    for qb in range(2):
                        plan = PLANS[qb]
                        S.wait(vector, "aex", S.val("aex_p") + len(plan))
                        S.cnt["aex_p"] = S.val("aex_p") + len(plan)
                        for pi, (kt, mi) in enumerate(plan):
                            if mi is not None:
                                i = vector.tensor_mul(
                                    s_E[:, 512 * kt:512 * (kt + 1)],
                                    s_E[:, 512 * kt:512 * (kt + 1)],
                                    s_masks[:, 512 * mi:512 * (mi + 1)])
                                S.inc(i, "amk", 1)
                        S.wait(vector, "apv", 2 * a + qb + 1)
                        if a >= 2 and qb == 0:
                            S.wait(vector, "a3w", 16 * (a - 1))
                        vector.reciprocal(s_rinv[:, :], pb[1][:, :])
                        vector.drain()
                        i = vector.tensor_tensor(
                            s_at[a % 2][:, 512 * qb:512 * (qb + 1)],
                            pb[0][:, :], s_rinv[:, :], ALU.mult)
                        S.inc(i, "a3o", 1)
                leave(vector)

            @blk.scalar
            def _(scalar):
                eng_enter(scalar)
                for a in range(16):
                    S.wait(scalar, "a3s", a + 1)
                    scalar.activation(s_sq3[:, :], s_aq[a % 2][:, :], AF.Square)
                    scalar.copy(s_dummy[:, 0:4], s_dummy[:, 4:8]).then_inc(
                        S["a3q"], 1)
                    S.cnt["a3q"] += 1
                    for rnd in range(3):
                        S.wait(scalar, "a3t", 3 * a + rnd + 1)
                        ro = 1024 * rnd
                        scalar.activation(s_rw3[0:1, ro:ro + 512],
                                          pb[7][0:1, 0:512], AF.Copy)
                        i = scalar.activation(s_rw3[0:1, ro + 512:ro + 1024],
                                              pb[6][0:1, 0:512], AF.Copy)
                        S.inc(i, "a3r", 1)
                        if rnd < 2:
                            i = scalar.activation(
                                s_sq3[:, :],
                                s_ak[a % 2][:, 1024 * rnd:1024 * (rnd + 1)],
                                AF.Square)
                            i.then_inc(S["a3q"], 1)
                            S.cnt["a3q"] += 1
                    S.wait(scalar, "a3c", a + 1)
                    scalar.activation(s_rc[0:1, 3:4], s_rc[0:1, 2:3], AF.Sqrt)
                    scalar.drain()
                    i = scalar.activation(s_rcb[0:1, 0:1], s_rc[0:1, 3:4],
                                          AF.Copy, scale=-S128)
                    S.inc(i, "a3cs", 1)
                    S.wait(scalar, "a3bb", a + 1)
                    scalar.activation(s_r3[:, 0:1], pb[6][:, 0:1], AF.Copy)
                    i = scalar.activation(s_r3[:, 1:2], pb[6][:, 0:1],
                                          AF.Identity, bias=s_pb[:, 0:1])
                    S.inc(i, "a3b", 1)
                    for qb in range(2):
                        plan = PLANS[qb]
                        for pi, (kt, mi) in enumerate(plan):
                            S.wait(scalar, "asc", S.val("asc_p") + pi + 1)
                            bias_ap = (s_r3[:, 1:2] if kt < 8 else s_r3[:, 0:1])
                            i = scalar.activation(
                                s_E[:, 512 * kt:512 * (kt + 1)],
                                pb[4 + (pi % 2)][:, :], AF.Exp,
                                bias=bias_ap, scale=S128)
                            S.inc(i, "aex", 1)
                        S.cnt["asc_p"] = S.val("asc_p") + len(plan)
                leave(scalar)

            @blk.tensor
            def _(tensor):
                eng_enter(tensor)
                S.wait(tensor, "a3m", 16)
                for a in range(16):
                    for rnd in range(3):
                        S.wait(tensor, "a3q", 3 * a + rnd + 1)
                        if 3 * a + rnd >= 1:
                            S.wait(tensor, "a3r", 3 * a + rnd)
                        for tb in range(2):
                            i = tensor.matmul(
                                (pb[7] if tb == 0 else pb[6])[0:1, 0:512],
                                s_onescol[:, :],
                                s_sq3[:, 512 * tb:512 * (tb + 1)],
                                start=True, stop=True)
                        S.inc(i, "a3t", 1)
                    S.wait(tensor, "a3cs", a + 1)
                    S.wait(tensor, "a3r", 3 * (a + 1))
                    i = tensor.matmul(pb[6][:, 0:1], s_onesrow[:, :],
                                      s_rcb[0:1, 0:1], start=True, stop=True)
                    S.inc(i, "a3bb", 1)
                    S.wait(tensor, "a3b", a + 1)
                    for qb in range(2):
                        plan = PLANS[qb]
                        for pi, (kt, mi) in enumerate(plan):
                            if pi >= 2:
                                S.wait(tensor, "aex", S.val("aex_t") + pi - 1)
                            i = tensor.matmul(
                                pb[4 + (pi % 2)][:, :],
                                s_ak[a % 2][:, 128 * kt:128 * (kt + 1)],
                                s_aq[a % 2][:, 512 * qb:512 * (qb + 1)],
                                start=True, stop=True)
                            S.inc(i, "asc", 1)
                        S.wait(tensor, "aex", S.val("aex_t") + len(plan))
                        S.cnt["aex_t"] = S.val("aex_t") + len(plan)
                        S.wait(tensor, "amk", S.val("amk_t") + NMSK[qb])
                        S.cnt["amk_t"] = S.val("amk_t") + NMSK[qb]
                        if 2 * a + qb >= 1:
                            S.wait(tensor, "a3o", 2 * a + qb)
                        for pi, (kt, mi) in enumerate(plan):
                            tensor.matmul(
                                pb[0][:, :],
                                s_av[a % 2][:, 128 * kt:128 * (kt + 1)],
                                s_E[:, 512 * kt:512 * (kt + 1)],
                                start=(pi == 0), stop=(pi == len(plan) - 1))
                        for pi, (kt, mi) in enumerate(plan):
                            i = tensor.matmul(
                                pb[1][:, :], s_ones128[:, :],
                                s_E[:, 512 * kt:512 * (kt + 1)],
                                start=(pi == 0), stop=(pi == len(plan) - 1))
                        S.inc(i, "apv", 1)
                leave(tensor)

            @blk.sync
            def _(sync):
                eng_enter(sync)
                if STAGE >= 4:
                    sync.wait_ge(S["agc"], 2)
                    ag8g = ag_out.ap().rearrange("(i r) c -> i r c", i=8)
                    ag8bg = ag_out.ap().rearrange("(i r) (a b) -> i (r a) b",
                                                  i=8, b=512)
                    pj8g = pj_out.ap().rearrange("(i r) c -> i r c", i=8)
                    with sync.register("r_hs4") as r_hs4:
                        sync.reg_load(r_hs4, dynI[0:1, 4:5])
                        hrow = sync.snap(r_hs4, min_val=0, max_val=HD * 3)
                        for dst, rbase in ((qsc, R_QFM), (ksc, R_KFM),
                                           (vsc, R_VFM)):
                            S.inc(sync.dma_start(
                                out=dst.ap().rearrange("(i r) c -> i r c",
                                                       i=8),
                                in_=ag8g[0:8][:, bass.ds(hrow + rbase, HD), :],
                                bounds_check=None), "stg4", 16)
                        for dst, rbase in ((ktsc, R_KTM * 2),
                                           (vtsc, R_VTM * 2)):
                            sync.reg_load(r_hs4, dynI[0:1, 5:6])
                            hh = sync.snap(r_hs4, min_val=0, max_val=3)
                            S.inc(sync.dma_start(
                                out=dst.ap().rearrange("(i r) c -> i r c",
                                                       i=8),
                                in_=ag8bg[0:8][:, bass.ds(hh + rbase, 1024, 4),
                                               :],
                                bounds_check=None), "stg4", 16)
                        sync.reg_load(r_hs4, dynI[0:1, 6:7])
                        hp = sync.snap(r_hs4, min_val=0, max_val=18)
                        S.inc(sync.dma_start(
                            out=pjsc.ap().rearrange("(i r) c -> i r c", i=8),
                            in_=pj8g[0:8][:, :, bass.ds(hp, 5)],
                            bounds_check=None), "stg4", 16)
                        sync.wait_ge(S["stg4"], S.val("stg4"))
                for a in range(16):
                    S.wait(sync, "a3o", 2 * (a + 1))
                    S.inc(sync.dma_start(
                        out=attnscr[128 * a:128 * (a + 1), :],
                        in_=s_at[a % 2][:, :]), "a3w", 16)
                sync.wait_ge(S["a3w"], 16 * 16)
                leave(sync)

        barrier()
        p3.close()

        if STAGE >= 4:

            p4 = ExitStack()
            q4 = lambda n, sh, dt_: p4.enter_context(nc.sbuf_tensor(n, sh, dt_))
            s_W0 = q4("s_W0", [128, 2048], F32)
            s_W1 = q4("s_W1", [128, 2048], F32)
            s_W2 = q4("s_W2", [128, 2048], F32)
            s_M = [q4(f"s_M{c}", [128, 2048], BF16) for c in range(3)]
            s_MT = [q4(f"s_MT{c}", [128, 2048], BF16) for c in range(3)]
            s_W0Tb = q4("s_W0Tb", [128, 2048], BF16)
            s_W1Tb = q4("s_W1Tb", [128, 2048], BF16)
            s_W2Tb = q4("s_W2Tb", [128, 2048], BF16)
            s_W1b = q4("s_W1b", [128, 2048], BF16)
            s_n0 = q4("s_n0", [128, 12], F32)
            s_rn = q4("s_rn", [128, 8], F32)
            s_qf = q4("s_qf", [128, 4096], BF16)
            s_kf = q4("s_kf", [128, 4096], BF16)
            s_vf = q4("s_vf", [128, 4096], BF16)
            s_ktm = q4("s_ktm", [128, 4096], BF16)
            s_vtm = q4("s_vtm", [128, 4096], BF16)
            s_hl1 = q4("s_hl1", [128, 4096], BF16)
            s_dgl0 = q4("s_dgl0", [128, 4096], BF16)
            s_dhl2 = q4("s_dhl2", [128, 4096], BF16)
            s_gh = q4("s_gh", [128, 4096], BF16)
            s_out4 = q4("s_out4", [128, 4096], BF16)
            s_sq4 = q4("s_sq4", [128, 2048], BF16)
            s_scr4 = q4("s_scr4", [128, 1024], F32)
            s_scr4b = q4("s_scr4b", [128, 4096], BF16)
            s_t1 = [q4(f"s_t1_{i}", [128, 512], F32) for i in range(2)]
            s_t2 = [q4(f"s_t2_{i}", [128, 512], F32) for i in range(2)]
            s_sg1 = [q4(f"s_sg1_{i}", [128, 512], F32) for i in range(2)]
            s_dsg1 = [q4(f"s_dsg1_{i}", [128, 512], F32) for i in range(2)]
            s_hb1 = [q4(f"s_hb1_{i}", [128, 512], F32) for i in range(2)]
            s_dh1 = [q4(f"s_dh1_{i}", [128, 512], F32) for i in range(2)]
            s_tx = [q4("s_tx_0", [128, 512], F32)] * 2
            s_proj = q4("s_proj", [128, 40], F32)
            s_l0 = q4("s_l0", [128, 8], F32)
            s_l1 = q4("s_l1", [128, 8], F32)
            s_l2 = q4("s_l2", [128, 8], F32)
            s_mo = q4("s_mo", [128, 8], BF16)
            s_sc4 = q4("s_sc4", [128, 8], F32)
            s_mmv = q4("s_mmv", [128, 1], F32)
            s_rktm = q4("s_rktm", [128, 8], F32)
            s_c4 = q4("s_c4", [1, 16], F32)
            s_c4b = q4("s_c4b", [1, 16], BF16)
            s_srow = s_cs_f
            s_rwb = s_cs_hi
            s_rw4 = s_sdB[0:1, 0:1024]
            s_rq = s_sdB[:, 1024:1536].bitcast(BF16)
            s_fac = s_sdB[:, 1536:2048].bitcast(BF16)
            s_wca = None
            s_tts = None
            s_wca = s_gh[:, 0:2048]
            s_tts = s_scr4b[:, 2048:4096]
            s_X = [s_dgl0[:, 0:2048], s_hl1[:, 0:2048], s_dhl2[:, 0:2048]]
            s_XT = [s_dgl0[:, 2048:4096], s_hl1[:, 2048:4096],
                    s_dhl2[:, 2048:4096]]
            s_A = [s_gh[:, 0:2048], s_gh[:, 2048:4096], s_sq4[:, 0:2048]]

            SCH = []
            P4MAP = {"pv": "p4v", "ps": "p4s", "tg": "p4g", "p4d": "p4d",
                     "p4n": "p4n"}
            CNT = {"tg": 0, "pv": 0, "ps": 0, "p4d": 0}
            BANK = [None] * 8

            def t_mm(emit, deps, ee, emit_ev, evdeps=()):
                CNT["tg"] += 1
                seq = CNT["tg"]
                bank = (seq - 1) % 8
                bdep = BANK[bank]
                sem = "pv" if ee == "v" else "ps"
                CNT[sem] += 1
                ecnt = CNT[sem]
                BANK[bank] = (sem, ecnt)
                SCH.append(("mm", emit, tuple(deps), ee, emit_ev,
                            tuple(evdeps), seq, bank, ecnt, bdep))
                return ("tg", seq), (sem, ecnt)

            def t_op(eng, emit, deps=(), force=False):
                sem = "pv" if eng == "v" else "ps"
                CNT[sem] += 1
                SCH.append(("op", eng, emit, tuple(deps), CNT[sem], force))
                return (sem, CNT[sem])

            def t_dma(emit, deps=()):
                CNT["p4d"] += 16
                SCH.append(("dma", emit, tuple(deps), CNT["p4d"]))
                return ("p4d", CNT["p4d"])

            def mk_slice_dma(dst, off, src, rows):
                def f(g, dst=dst, off=off, src=src, rows=rows):
                    return g.dma_start(out=dst[:, off:off + src.shape[1]]
                                       if False else dst,
                                       in_=src[rows[0]:rows[1], :])
                return f

            NSC = NS_COEFFS

            def emit_chunk(ci, prev_post=None):
                lastcons = CNT["tg"]
                lc = [("tg", lastcons)] if ci else []
                dd = []
                for j in range(4):
                    dd.append(t_dma(lambda g, j=j, ci=ci: g.dma_start(
                        out=s_qf[:, 1024 * j:1024 * (j + 1)],
                        in_=qsc[HD * ci + 128 * j:HD * ci + 128 * (j + 1), :]),
                        lc))
                    dd.append(t_dma(lambda g, j=j, ci=ci: g.dma_start(
                        out=s_kf[:, 1024 * j:1024 * (j + 1)],
                        in_=ksc[HD * ci + 128 * j:HD * ci + 128 * (j + 1), :]),
                        lc))
                    dd.append(t_dma(lambda g, j=j, ci=ci: g.dma_start(
                        out=s_vf[:, 1024 * j:1024 * (j + 1)],
                        in_=vsc[HD * ci + 128 * j:HD * ci + 128 * (j + 1), :]),
                        lc))
                for ct in range(8):
                    dd.append(t_dma(lambda g, ct=ct, ci=ci: g.dma_start(
                        out=s_ktm[:, 512 * ct:512 * (ct + 1)],
                        in_=ktsc[C * ci + 128 * ct:C * ci + 128 * (ct + 1), :]),
                        lc))
                    dd.append(t_dma(lambda g, ct=ct, ci=ci: g.dma_start(
                        out=s_vtm[:, 512 * ct:512 * (ct + 1)],
                        in_=vtsc[C * ci + 128 * ct:C * ci + 128 * (ct + 1), :]),
                        lc))
                for tt in range(8):
                    dd.append(t_dma(lambda g, tt=tt, ci=ci: g.dma_start(
                        out=s_proj[:, 5 * tt:5 * (tt + 1)],
                        in_=pjsc[C * ci + 128 * tt:C * ci + 128 * (tt + 1), :]),
                        lc))
                ld = ("p4d", CNT["p4d"])

                wdep = [("pv", CNT["pv"]), ("ps", CNT["ps"])]
                cast1 = t_op("v", lambda v: v.tensor_copy(s_W1b[:, :], s_W1[:, :]),
                             wdep)
                tr_ev = []
                for srcb, dstb, caster in (
                        (s_wca, s_W0Tb, lambda v: v.tensor_copy(s_wca[:, :],
                                                                s_W0[:, :])),
                        (s_wca, s_W2Tb, lambda v: v.tensor_copy(s_wca[:, :],
                                                                s_W2[:, :])),
                        (s_W1b, s_W1Tb, None)):
                    if caster is not None:
                        cdep = t_op("v", caster, wdep + tr_ev[-1:])
                    else:
                        cdep = cast1
                    for jh in range(4):
                        def mmf(tensor, pbt, src=srcb, jh=jh):
                            for c2 in range(4):
                                i = tensor.transpose(
                                    pbt[:, 64 * c2:64 * (c2 + 1)].bitcast(BF16),
                                    src[:, 512 * jh + 128 * c2:
                                        512 * jh + 128 * (c2 + 1)],
                                    s_ident[:, :])
                            return i
                        def evf(v, pbt, dst=dstb, jh=jh):
                            return v.tensor_copy(
                                dst.ap().rearrange("p (a b) -> p a b", b=128)
                                [:, :, 128 * 0:128] if False else
                                bass.AP(dst, 128 * jh,
                                        [[2048, 128], [512, 4], [1, 128]]),
                                pbt[:, 0:256].bitcast(BF16))
                        tr_ev.append(t_mm(mmf, [cdep], "v", evf)[1])
                stp_done = [("pv", CNT["pv"])]
                if prev_post is not None:
                    prev_post()
                post_d = [("p4d", CNT["p4d"])] if prev_post is not None else []

                pq = []
                for j in range(4):
                    pq.append(t_op("s", lambda sc, j=j: sc.activation(
                        s_qf[:, 1024 * j:1024 * (j + 1)],
                        s_qf[:, 1024 * j:1024 * (j + 1)], AF.Silu), [ld]))
                    pq.append(t_op("s", lambda sc, j=j: sc.activation(
                        s_vf[:, 1024 * j:1024 * (j + 1)],
                        s_vf[:, 1024 * j:1024 * (j + 1)], AF.Silu), [ld]))
                for ct in range(8):
                    pq.append(t_op("s", lambda sc, ct=ct: sc.activation(
                        s_vtm[:, 512 * ct:512 * (ct + 1)],
                        s_vtm[:, 512 * ct:512 * (ct + 1)], AF.Silu), [ld]))
                def normrows(srcfm, dep_sq, evdeps=()):
                    rows = []
                    for tb in range(2):
                        def mmf(tensor, pbt, tb=tb):
                            for j in range(4):
                                i = tensor.matmul(
                                    pbt[0:1, 0:512], s_onescol[:, :],
                                    s_scr4b[:, 1024 * j + 512 * tb:
                                            1024 * j + 512 * (tb + 1)],
                                    start=(j == 0), stop=(j == 3))
                            return i
                        def evf(sc, pbt, tb=tb):
                            return sc.activation(s_rw4[0:1, 512 * tb:512 * (tb + 1)],
                                                 pbt[0:1, 0:512], AF.Sqrt)
                        rows.append(t_mm(mmf, dep_sq, "s", evf, evdeps)[1])
                    return rows
                sqd = []
                for j in range(4):
                    sqd.append(t_op("s", lambda sc, j=j: sc.activation(
                        s_scr4b[:, 1024 * j:1024 * (j + 1)],
                        s_qf[:, 1024 * j:1024 * (j + 1)], AF.Square),
                        [pq[-1]] + post_d))
                qrows = normrows(s_qf, sqd)
                rq1 = t_op("v", lambda v: v.tensor_scalar(
                    s_rw4[0:1, 0:1024], s_rw4[0:1, 0:1024], 1e-5, None, ALU.add),
                    qrows, force=True)
                rq2 = t_op("v", lambda v: v.reciprocal(s_rw4[0:1, 0:1024],
                                                       s_rw4[0:1, 0:1024]),
                    [rq1], force=True)
                rq3 = t_op("v", lambda v: v.tensor_copy(s_rwb[0:1, 0:1024],
                                                        s_rw4[0:1, 0:1024]), [rq2])
                qbc = []
                for tb in range(2):
                    def mmf(tensor, pbt, tb=tb):
                        return tensor.matmul(pbt[:, :], s_onesrow[:, :],
                                             s_rwb[0:1, 512 * tb:512 * (tb + 1)],
                                             start=True, stop=True)
                    def evf(sc, pbt, tb=tb):
                        return sc.activation(s_rq[:, 512 * tb:512 * (tb + 1)],
                                             pbt[:, :], AF.Copy)
                    qbc.append(t_mm(mmf, [rq3], "s", evf)[1])
                qap = []
                for j in range(4):
                    qap.append(t_op("v", lambda v, j=j: v.tensor_mul(
                        s_qf[:, 1024 * j:1024 * (j + 1)],
                        s_qf[:, 1024 * j:1024 * (j + 1)], s_rq[:, :]), qbc))
                pk = []
                for j in range(4):
                    pk.append(t_op("s", lambda sc, j=j: sc.activation(
                        s_kf[:, 1024 * j:1024 * (j + 1)],
                        s_kf[:, 1024 * j:1024 * (j + 1)], AF.Silu),
                        [ld] + qap[-1:]))
                sqk = []
                for j in range(4):
                    sqk.append(t_op("s", lambda sc, j=j: sc.activation(
                        s_scr4b[:, 1024 * j:1024 * (j + 1)],
                        s_kf[:, 1024 * j:1024 * (j + 1)], AF.Square),
                        [pk[-1]] + post_d))
                krows = normrows(s_kf, sqk, (rq3,))
                rk1 = t_op("v", lambda v: v.tensor_scalar(
                    s_rw4[0:1, 0:1024], s_rw4[0:1, 0:1024], 1e-5, None, ALU.add),
                    krows + qap, force=True)
                rk2 = t_op("v", lambda v: v.reciprocal(s_rw4[0:1, 0:1024],
                                                       s_rw4[0:1, 0:1024]),
                    [rk1], force=True)
                rk3 = t_op("v", lambda v: v.tensor_copy(s_rwb[0:1, 0:1024],
                                                        s_rw4[0:1, 0:1024]), [rk2])
                kbc = []
                for tb in range(2):
                    def mmf(tensor, pbt, tb=tb):
                        return tensor.matmul(pbt[:, :], s_onesrow[:, :],
                                             s_rwb[0:1, 512 * tb:512 * (tb + 1)],
                                             start=True, stop=True)
                    def evf(sc, pbt, tb=tb):
                        return sc.activation(s_rq[:, 512 * tb:512 * (tb + 1)],
                                             pbt[:, :], AF.Copy)
                    kbc.append(t_mm(mmf, [rk3], "s", evf, tuple(qap))[1])
                kap = []
                for j in range(4):
                    kap.append(t_op("v", lambda v, j=j: v.tensor_mul(
                        s_kf[:, 1024 * j:1024 * (j + 1)],
                        s_kf[:, 1024 * j:1024 * (j + 1)], s_rq[:, :]), kbc))
                bo1 = t_dma(lambda g, ci=ci: g.dma_start(
                    out=rowscr[0:1, :], in_=s_rw4[0:1, 0:1024]), [rk2])
                bo2 = t_dma(lambda g: g.dma_start(
                    out=s_rktm[:, :],
                    in_=bass.AP(rowscr, 0, [[1, 128], [128, 8]]),
                    allow_slow_non_contiguous=True), [bo1])
                ktp = []
                for ct in range(8):
                    d1 = t_op("s", lambda sc, ct=ct: sc.activation(
                        s_ktm[:, 512 * ct:512 * (ct + 1)],
                        s_ktm[:, 512 * ct:512 * (ct + 1)], AF.Silu), [ld])
                    ktp.append(t_op("v", lambda v, ct=ct: v.tensor_scalar(
                        s_ktm[:, 512 * ct:512 * (ct + 1)],
                        s_ktm[:, 512 * ct:512 * (ct + 1)],
                        s_rktm[:, ct:ct + 1], None, ALU.mult), [d1, bo2]))
                lrd = []
                for dst, col in ((s_l0, 0), (s_l1, 1), (s_l2, 2)):
                    a1 = t_op("s", lambda sc, dst=dst, col=col: sc.activation(
                        dst[:, :], s_proj[:, col:40:5], AF.Abs,
                        bias=BLRI), [ld])
                    a2 = t_op("s", lambda sc, dst=dst: sc.activation(
                        dst[:, :], dst[:, :], AF.Exp, scale=-1.0), [a1])
                    a3 = t_op("s", lambda sc, dst=dst: sc.activation(
                        dst[:, :], dst[:, :], AF.Ln, bias=1.0), [a2])
                    a4 = t_op("v", lambda v, dst=dst, col=col: v.tensor_scalar(
                        s_rn[:, 4:5] if False else s_scr4[:, 0:8],
                        s_proj[:, col:40:5], BLRI, None, ALU.add),
                        [a3], force=True)
                    a5 = t_op("v", lambda v: v.tensor_scalar(
                        s_scr4[:, 0:8], s_scr4[:, 0:8], 0.0, None, ALU.max),
                        [a4], force=True)
                    lrd.append(t_op("v", lambda v, dst=dst: v.tensor_add(
                        dst[:, :], dst[:, :], s_scr4[:, 0:8]), [a5],
                        force=True))
                mo1 = t_op("s", lambda sc: sc.activation(
                    s_mo[:, :], s_proj[:, 3:40:5], AF.Sigmoid,
                    bias=s_bm[:, 0:1]), [ld])
                sc1 = t_op("s", lambda sc: sc.activation(
                    s_sc4[:, :], s_proj[:, 4:40:5], AF.Silu,
                    bias=s_bs[:, 0:1]), [ld])
                def mmf(tensor, pbt):
                    return tensor.matmul(pbt[0:1, 0:8], s_onescol[:, :],
                                         s_mo[:, :], start=True, stop=True)
                def evf(sc, pbt):
                    return sc.activation(s_c4[0:1, 0:8], pbt[0:1, 0:8], AF.Copy)
                mrow = t_mm(mmf, [mo1], "s", evf)[1]
                m1 = t_op("v", lambda v: v.reduce_sum(s_c4[0:1, 8:9],
                                                      s_c4[0:1, 0:8], axis=AX.X),
                          [mrow], force=True)
                m2 = t_op("v", lambda v: v.tensor_scalar(
                    s_c4b[0:1, 0:1], s_c4[0:1, 8:9], 1.0 / C, None, ALU.mult),
                    [m1])
                def mmf2(tensor, pbt):
                    return tensor.matmul(pbt[:, 0:1], s_onesrow[:, :],
                                         s_c4b[0:1, 0:1], start=True, stop=True)
                def evf2(sc, pbt):
                    return sc.activation(s_mmv[:, :], pbt[:, 0:1], AF.Copy)
                mmvd = t_mm(mmf2, [m2], "s", evf2)[1]
                sb1 = t_dma(lambda g: g.dma_start(
                    out=bass.AP(rowscr, 1024, [[1, 128], [128, 8]]),
                    in_=s_sc4[:, :], allow_slow_non_contiguous=True), [sc1])
                sb2 = t_dma(lambda g: g.dma_start(
                    out=s_srow[0:1, :], in_=rowscr[1:2, :]), [sb1])

                ghd = {}
                ghorder = []
                for m in range(4):
                    for tb in range(2):
                        pi = (m * 2 + tb) % 2
                        ghguard = [ghorder[-2]] if len(ghorder) >= 2 else []
                        def mmh(tensor, pbt, m=m, tb=tb):
                            for kd in range(4):
                                i = tensor.matmul(
                                    pbt[:, :],
                                    s_W2Tb[:, 512 * kd + 128 * m:
                                           512 * kd + 128 * (m + 1)],
                                    s_qf[:, 1024 * kd + 512 * tb:
                                         1024 * kd + 512 * (tb + 1)],
                                    start=(kd == 0), stop=(kd == 3))
                            return i
                        def evh(sc, pbt, pi=pi):
                            return sc.activation(s_t1[pi][:, :], pbt[:, :], AF.Copy)
                        hdep = t_mm(mmh, stp_done + qap, "s", evh, ghguard)[1]
                        def mmg(tensor, pbt, m=m, tb=tb):
                            for kd in range(4):
                                i = tensor.matmul(
                                    pbt[:, :],
                                    s_W0Tb[:, 512 * kd + 128 * m:
                                           512 * kd + 128 * (m + 1)],
                                    s_qf[:, 1024 * kd + 512 * tb:
                                         1024 * kd + 512 * (tb + 1)],
                                    start=(kd == 0), stop=(kd == 3))
                            return i
                        def evg(sc, pbt, pi=pi):
                            return sc.activation(s_t2[pi][:, :], pbt[:, :], AF.Silu)
                        gdep = t_mm(mmg, stp_done + qap, "s", evg, ghguard)[1]
                        ghd[(m, tb)] = t_op(
                            "v", lambda v, m=m, tb=tb, pi=pi: v.tensor_mul(
                                s_gh[:, 1024 * m + 512 * tb:
                                     1024 * m + 512 * (tb + 1)],
                                s_t1[pi][:, :], s_t2[pi][:, :]), [hdep, gdep])
                        ghorder.append(ghd[(m, tb)])
                msum_sq = {0: [], 1: []}
                for tb in range(2):
                    ghdeps = [ghd[(kh, tb)] for kh in range(4)]
                    for m in range(4):
                        def mmo(tensor, pbt, m=m, tb=tb):
                            for kh in range(4):
                                i = tensor.matmul(
                                    pbt[:, :],
                                    s_W1Tb[:, 512 * kh + 128 * m:
                                           512 * kh + 128 * (m + 1)],
                                    s_gh[:, 1024 * kh + 512 * tb:
                                         1024 * kh + 512 * (tb + 1)],
                                    start=(kh == 0), stop=(kh == 3))
                            return i
                        def evo(sc, pbt, m=m, tb=tb):
                            sc.activation(s_sq4[:, 512 * m:512 * (m + 1)],
                                          pbt[:, :], AF.Square)
                            return sc.activation(
                                s_out4[:, 2048 * tb + 512 * m:
                                       2048 * tb + 512 * (m + 1)],
                                pbt[:, :], AF.Copy)
                        msum_sq[tb].append(t_mm(mmo, ghdeps, "s", evo)[1])
                    def mmr(tensor, pbt, tb=tb):
                        for m in range(4):
                            i = tensor.matmul(pbt[0:1, 0:512], s_onescol[:, :],
                                              s_sq4[:, 512 * m:512 * (m + 1)],
                                              start=(m == 0), stop=(m == 3))
                        return i
                    def evr(sc, pbt, tb=tb):
                        sc.activation(s_rw4[0:1, 512 * tb:512 * (tb + 1)],
                                      pbt[0:1, 0:512], AF.Ln, bias=1e-5,
                                      scale=1.0 / HD)
                        sc.drain()
                        return sc.activation(s_rw4[0:1, 512 * tb:512 * (tb + 1)],
                                             s_rw4[0:1, 512 * tb:512 * (tb + 1)],
                                             AF.Exp, scale=-0.5)
                    msum_sq[tb].append(t_mm(mmr, msum_sq[tb], "s", evr,
                                            (rk3,))[1])

                kvops = []
                for ct in range(8):
                    pi = ct % 2
                    def mmgb(tensor, pbt, ct=ct):
                        for kd in range(4):
                            i = tensor.matmul(
                                pbt[:, :],
                                s_kf[:, 1024 * kd + 128 * ct:
                                     1024 * kd + 128 * (ct + 1)]
                                if False else
                                s_kf[:, 1024 * kd + 128 * ct:1024 * kd + 128 * ct + 128],
                                s_W0Tb[:, 512 * kd:512 * (kd + 1)],
                                start=(kd == 0), stop=(kd == 3))
                        return i
                    ghlast = ([ghd[(3, 1)]] if ct < 2 else
                              [kvops[ct - 2]])
                    def evgb(sc, pbt, pi=pi):
                        sc.activation(s_sg1[pi][:, :], pbt[:, :], AF.Silu)
                        sc.activation(s_dsg1[pi][:, :], pbt[:, :], AF.Sigmoid)
                        return sc.activation(s_t1[pi][:, :], pbt[:, :],
                                             AF.Copy)
                    d1 = t_mm(mmgb, kap, "s", evgb, ghlast)[1]
                    w1 = t_op("v", lambda v, pi=pi: v.tensor_mul(
                        s_t2[pi][:, :], s_dsg1[pi][:, :],
                        s_t1[pi][:, :]), [d1])
                    w3 = t_op("v", lambda v, pi=pi: v.scalar_tensor_tensor(
                        s_t1[pi][:, :], s_t1[pi][:, :], 1.0,
                        s_t2[pi][:, :], ALU.add, ALU.subtract), [w1])
                    w4 = t_op("v", lambda v, pi=pi: v.tensor_mul(
                        s_dsg1[pi][:, :], s_dsg1[pi][:, :],
                        s_t1[pi][:, :]), [w3])
                    def mmhb(tensor, pbt, ct=ct):
                        for kd in range(4):
                            i = tensor.matmul(
                                pbt[:, :],
                                s_kf[:, 1024 * kd + 128 * ct:1024 * kd + 128 * ct + 128],
                                s_W2Tb[:, 512 * kd:512 * (kd + 1)],
                                start=(kd == 0), stop=(kd == 3))
                        return i
                    def evhb(sc, pbt, pi=pi):
                        return sc.activation(s_hb1[pi][:, :], pbt[:, :], AF.Copy)
                    d2 = t_mm(mmhb, kap, "s", evhb, ghlast)[1]
                    def mmdh(tensor, pbt, ct=ct):
                        for ko in range(4):
                            i = tensor.matmul(
                                pbt[:, :],
                                s_vf[:, 1024 * ko + 128 * ct:1024 * ko + 128 * ct + 128],
                                s_W1b[:, 512 * ko:512 * (ko + 1)],
                                start=(ko == 0), stop=(ko == 3))
                        return i
                    def evdh(sc, pbt, pi=pi):
                        return sc.activation(s_dh1[pi][:, :], pbt[:, :], AF.Copy)
                    d3 = t_mm(mmdh, stp_done + pq[-1:], "s", evdh, ghlast)[1]
                    o2 = t_op("v", lambda v, ct=ct, pi=pi:
                              v.scalar_tensor_tensor(
                                  s_hl1[:, 512 * ct:512 * (ct + 1)],
                                  s_sg1[pi][:, :], s_l1[:, ct:ct + 1],
                                  s_hb1[pi][:, :], ALU.mult, ALU.mult),
                              [d1, d2, lrd[1]])
                    o3 = t_op("v", lambda v, pi=pi: v.tensor_mul(
                        s_tx[pi][:, :], s_dh1[pi][:, :], s_hb1[pi][:, :]),
                        [d3, o2])
                    o5 = t_op("v", lambda v, ct=ct, pi=pi:
                              v.scalar_tensor_tensor(
                                  s_dgl0[:, 512 * ct:512 * (ct + 1)],
                                  s_tx[pi][:, :], s_l0[:, ct:ct + 1],
                                  s_dsg1[pi][:, :], ALU.mult, ALU.mult),
                              [o3, w4, lrd[0]])
                    o7 = t_op("v", lambda v, ct=ct, pi=pi:
                              v.scalar_tensor_tensor(
                                  s_dhl2[:, 512 * ct:512 * (ct + 1)],
                                  s_dh1[pi][:, :], s_l2[:, ct:ct + 1],
                                  s_sg1[pi][:, :], ALU.mult, ALU.mult),
                              [o5, lrd[2]])
                    kvops.append(o7)

                frob_bc = []
                for c in range(3):
                    L_, R_ = ((s_dgl0, s_ktm), (s_vtm, s_hl1),
                              (s_dhl2, s_ktm))[c]
                    Ms, MTs = s_M[c], s_MT[c]
                    evs = []
                    for t_ in range(2):
                        A_, B_ = (L_, R_) if t_ == 0 else (R_, L_)
                        dst = Ms if t_ == 0 else MTs
                        for m in range(4):
                            def mmd(tensor, pbt, A_=A_, B_=B_, m=m):
                                for ct in range(8):
                                    i = tensor.matmul(
                                        pbt[:, :],
                                        A_[:, 512 * ct + 128 * m:
                                           512 * ct + 128 * (m + 1)],
                                        B_[:, 512 * ct:512 * (ct + 1)],
                                        start=(ct == 0), stop=(ct == 7))
                                return i
                            def evd(v, pbt, dst=dst, m=m):
                                return v.scalar_tensor_tensor(
                                    dst[:, 512 * m:512 * (m + 1)],
                                    dst[:, 512 * m:512 * (m + 1)],
                                    s_mmv[:, 0:1], pbt[:, :],
                                    ALU.mult, ALU.add)
                            evs.append(t_mm(mmd, kvops + [mmvd], "v", evd)[1])
                    if not NS_IT:
                        continue
                    f1 = t_op("s", lambda sc, Ms=Ms: sc.activation(
                        s_scr4b[:, 0:2048], Ms[:, :], AF.Square,
                        accum_out=s_rn[:, 4:5]), evs)
                    f2 = t_op("v", lambda v: v.tensor_copy(
                        s_rn[:, 5:6].bitcast(F32) if False else s_rn[:, 5:6],
                        s_rn[:, 4:5]), [f1])
                    f2b = t_op("v", lambda v: v.tensor_copy(
                        s_c4b[0:1, 1:2], s_c4[0:1, 1:2]), [f2]) if False else None
                    f3 = t_op("v", lambda v: v.tensor_copy(
                        s_scr4b[:, 0:1], s_rn[:, 4:5]), [f1])
                    def mmfb(tensor, pbt):
                        return tensor.matmul(pbt[0:1, 0:1], s_onescol[:, :],
                                             s_scr4b[:, 0:1], start=True,
                                             stop=True)
                    def evfb(sc, pbt):
                        return sc.activation(s_c4[0:1, 2:3], pbt[0:1, 0:1],
                                             AF.Sqrt)
                    fr = t_mm(mmfb, [f3], "s", evfb)[1]
                    g1 = t_op("v", lambda v: v.tensor_scalar(
                        s_c4[0:1, 2:3], s_c4[0:1, 2:3], 1e-7, None, ALU.add),
                        [fr], force=True)
                    g2 = t_op("v", lambda v: v.reciprocal(
                        s_c4[0:1, 2:3], s_c4[0:1, 2:3]), [g1], force=True)
                    g3 = t_op("v", lambda v: v.tensor_copy(
                        s_c4b[0:1, 2:3], s_c4[0:1, 2:3]), [g2])
                    def mmfc(tensor, pbt):
                        return tensor.matmul(pbt[:, 0:1], s_onesrow[:, :],
                                             s_c4b[0:1, 2:3], start=True,
                                             stop=True)
                    def evfc(sc, pbt):
                        return sc.activation(s_rn[:, 6:7], pbt[:, 0:1], AF.Copy)
                    fb = t_mm(mmfc, [g3], "s", evfc)[1]
                    x1 = t_op("v", lambda v, c=c, Ms=Ms: v.tensor_scalar(
                        s_X[c][:, :], Ms[:, :], s_rn[:, 6:7], None, ALU.mult),
                        [fb])
                    x2 = t_op("v", lambda v, c=c, MTs=MTs: v.tensor_scalar(
                        s_XT[c][:, :], MTs[:, :], s_rn[:, 6:7], None, ALU.mult),
                        [fb, x1])
                    frob_bc.append((x1, x2))

                lastX = [frob_bc[c][0] for c in range(3)] if NS_IT else []
                lastXT = [frob_bc[c][1] for c in range(3)] if NS_IT else []
                for it in range(NS_IT):
                    a_, b_, c_ = NSC[it]
                    Aev = [[], [], []]
                    for c in range(3):
                        for m in range(4):
                            def mmA(tensor, pbt, c=c, m=m):
                                for k in range(4):
                                    i = tensor.matmul(
                                        pbt[:, :],
                                        s_XT[c][:, 512 * k + 128 * m:
                                                512 * k + 128 * (m + 1)],
                                        s_XT[c][:, 512 * k:512 * (k + 1)],
                                        start=(k == 0), stop=(k == 3))
                                return i
                            def evA(sc, pbt, c=c, m=m):
                                return sc.activation(
                                    s_A[c][:, 512 * m:512 * (m + 1)],
                                    pbt[:, :], AF.Copy)
                            Aev[c].append(t_mm(mmA, [lastX[c], lastXT[c]],
                                               "s", evA)[1])
                    Bops = [[], [], []]
                    prevB = {}
                    for c in range(3):
                        a2ev = []
                        for m in range(4):
                            def mmA2(tensor, pbt, c=c, m=m):
                                for k in range(4):
                                    i = tensor.matmul(
                                        pbt[:, :],
                                        s_A[c][:, 512 * k + 128 * m:
                                               512 * k + 128 * (m + 1)],
                                        s_A[c][:, 512 * k:512 * (k + 1)],
                                        start=(k == 0), stop=(k == 3))
                                return i
                            def evA2(sc, pbt, c=c, m=m, c_=c_):
                                return sc.activation(
                                    s_scr4b[:, 512 * m:512 * (m + 1)],
                                    pbt[:, :], AF.Copy, scale=c_)
                            evd = [prevB[m]] if m in prevB else []
                            a2ev.append(t_mm(mmA2, Aev[c], "s", evA2, evd)[1])
                        for m in range(4):
                            bop = t_op(
                                "v", lambda v, c=c, m=m, b_=b_:
                                v.scalar_tensor_tensor(
                                    s_A[c][:, 512 * m:512 * (m + 1)],
                                    s_A[c][:, 512 * m:512 * (m + 1)], b_,
                                    s_scr4b[:, 512 * m:512 * (m + 1)],
                                    ALU.mult, ALU.add), a2ev)
                            Bops[c].append(bop)
                            prevB[m] = bop
                    newX, newXT = [], []
                    for c in range(3):
                        bx = []
                        for m in range(4):
                            def mmBX(tensor, pbt, c=c, m=m):
                                for k in range(4):
                                    i = tensor.matmul(
                                        pbt[:, :],
                                        s_A[c][:, 512 * k + 128 * m:
                                               512 * k + 128 * (m + 1)],
                                        s_X[c][:, 512 * k:512 * (k + 1)],
                                        start=(k == 0), stop=(k == 3))
                                return i
                            bx.append((mmBX, m))
                        xtb = []
                        for m in range(4):
                            def mmXTB(tensor, pbt, c=c, m=m):
                                for k in range(4):
                                    i = tensor.matmul(
                                        pbt[:, :],
                                        s_X[c][:, 512 * k + 128 * m:
                                               512 * k + 128 * (m + 1)],
                                        s_A[c][:, 512 * k:512 * (k + 1)],
                                        start=(k == 0), stop=(k == 3))
                                return i
                            xtb.append((mmXTB, m))
                        bxe = []
                        xtbseq = []
                        for mmf_, m in bx:
                            def evBX(v, pbt, c=c, m=m, a_=a_):
                                return v.scalar_tensor_tensor(
                                    s_X[c][:, 512 * m:512 * (m + 1)],
                                    s_X[c][:, 512 * m:512 * (m + 1)], a_,
                                    pbt[:, :], ALU.mult, ALU.add)
                            bxe.append((mmf_, evBX, m))
                        xe = []
                        for mmf_, m in xtb:
                            def evXTB(v, pbt, c=c, m=m, a_=a_):
                                return v.scalar_tensor_tensor(
                                    s_XT[c][:, 512 * m:512 * (m + 1)],
                                    s_XT[c][:, 512 * m:512 * (m + 1)], a_,
                                    pbt[:, :], ALU.mult, ALU.add)
                            xe.append(t_mm(mmf_, Bops[c], "v", evXTB)[1])
                        last_bx_seq = CNT["tg"] + 4
                        for mmf_, evf_, m in bxe:
                            bxe_dep = t_mm(mmf_, Bops[c],
                                           "v", evf_,
                                           [("tg", last_bx_seq)])[1]
                            xe.append(bxe_dep)
                        newX.append(xe[-1])
                        newXT.append(xe[3])
                    lastX, lastXT = newX, newXT

                wupd = []
                for c, W in enumerate((s_W0, s_W1, s_W2) if NS_IT else ()):
                    u1 = t_op("v", lambda v, W=W, c=c: v.tensor_add(
                        W[:, :], W[:, :], s_X[c][:, :]), [lastX[c], lastXT[c]])
                    sqs = []
                    for jb in range(4):
                        sqs.append(t_op("s", lambda sc, W=W, jb=jb: sc.activation(
                            s_scr4b[:, 512 * jb:512 * (jb + 1)],
                            W[:, 512 * jb:512 * (jb + 1)], AF.Square,
                            accum_out=s_rn[:, jb:jb + 1]), [u1]))
                    s1 = t_op("s", lambda sc: sc.activation(
                        s_rn[:, 0:4], s_rn[:, 0:4], AF.Sqrt), sqs)
                    v1 = t_op("v", lambda v: v.tensor_scalar(
                        s_rn[:, 0:4], s_rn[:, 0:4], 1e-5, None, ALU.add),
                        [s1], force=True)
                    v2 = t_op("v", lambda v: v.reciprocal(s_rn[:, 0:4],
                                                          s_rn[:, 0:4]),
                        [v1], force=True)
                    v3 = t_op("v", lambda v, c=c: v.tensor_mul(
                        s_rn[:, 0:4], s_rn[:, 0:4], s_n0[:, 4 * c:4 * (c + 1)]),
                        [v2], force=True)
                    for jb in range(4):
                        wupd.append(t_op("v", lambda v, W=W, jb=jb: v.tensor_scalar(
                            W[:, 512 * jb:512 * (jb + 1)],
                            W[:, 512 * jb:512 * (jb + 1)],
                            s_rn[:, jb:jb + 1], None, ALU.mult), [v3]))

                def postproc():
                    fac_ev = []
                    for tb in range(2):
                        r1 = t_op("v", lambda v, tb=tb: v.tensor_mul(
                            s_rw4[0:1, 512 * tb:512 * (tb + 1)],
                            s_rw4[0:1, 512 * tb:512 * (tb + 1)],
                            s_srow[0:1, 512 * tb:512 * (tb + 1)]),
                            [msum_sq[tb][-1], sb2])
                        r2 = t_op("v", lambda v, tb=tb: v.tensor_copy(
                            s_rwb[0:1, 512 * tb:512 * (tb + 1)],
                            s_rw4[0:1, 512 * tb:512 * (tb + 1)]), [r1])
                        def mmf3(tensor, pbt, tb=tb):
                            return tensor.matmul(pbt[:, :], s_onesrow[:, :],
                                                 s_rwb[0:1, 512 * tb:512 * (tb + 1)],
                                                 start=True, stop=True)
                        def evf3(sc, pbt, tb=tb):
                            return sc.activation(s_fac[:, 512 * tb:512 * (tb + 1)],
                                                 pbt[:, :], AF.Copy)
                        fac_ev.append(t_mm(mmf3, [r2], "s", evf3)[1])
                    tts_ops = []
                    for m in range(4):
                        for tb in range(2):
                            tts_ops.append(t_op(
                                "v", lambda v, m=m, tb=tb: v.scalar_tensor_tensor(
                                    s_tts[:, 1024 * (m % 2) + 512 * tb:
                                          1024 * (m % 2) + 512 * (tb + 1)],
                                    s_out4[:, 2048 * tb + 512 * m:
                                           2048 * tb + 512 * (m + 1)],
                                    s_ong[:, m:m + 1],
                                    s_fac[:, 512 * tb:512 * (tb + 1)],
                                    ALU.mult, ALU.mult), fac_ev +
                                ([("p4d", CNT["p4d"])] if m >= 2 else [])))
                        t_dma(lambda g, m=m, ci=ci: g.dma_start(
                            out=ta_in[HD * ci + 128 * m:HD * ci + 128 * (m + 1),
                                      0:512],
                            in_=s_tts[:, 1024 * (m % 2):1024 * (m % 2) + 512]),
                            [tts_ops[-1]])
                        t_dma(lambda g, m=m, ci=ci: g.dma_start(
                            out=ta_in[HD * ci + 128 * m:HD * ci + 128 * (m + 1),
                                      512:1024],
                            in_=s_tts[:, 1024 * (m % 2) + 512:1024 * (m % 2) + 1024]),
                            [tts_ops[-1]])
                return postproc

            for c_, W_ in enumerate((s_W0, s_W1, s_W2)):
                for jb_ in range(4):
                    t_op("s", lambda sc, W_=W_, c_=c_, jb_=jb_: sc.activation(
                        s_scr4b[:, 512 * jb_:512 * (jb_ + 1)],
                        W_[:, 512 * jb_:512 * (jb_ + 1)], AF.Square,
                        accum_out=s_n0[:, 4 * c_ + jb_:4 * c_ + jb_ + 1]),
                        [("p4n", 1)])
                t_op("s", lambda sc, c_=c_: sc.activation(
                    s_n0[:, 4 * c_:4 * (c_ + 1)],
                    s_n0[:, 4 * c_:4 * (c_ + 1)], AF.Sqrt),
                    [("ps", CNT["ps"])])

            prev_post = None
            for ci in range(N_CHUNKS):
                prev_post = emit_chunk(ci, prev_post)
            if prev_post is not None:
                prev_post()

            DRAIN_MARK = set()

            def _mark(deps, cons):
                for sem, peng in (("pv", "v"), ("ps", "s")):
                    if peng == cons:
                        continue
                    vals = [v for (s2, v) in deps if s2 == sem]
                    if vals:
                        DRAIN_MARK.add((sem, max(vals)))

            for e in SCH:
                if e[0] == "mm":
                    _mark(e[2], "t")
                    _mark(e[5], e[3])
                elif e[0] == "op":
                    _mark(e[3], e[1])
                    if e[5]:
                        DRAIN_MARK.add(("pv" if e[1] == "v" else "ps", e[4]))
                else:
                    _mark(e[2], "d")

            nc.leave_named_scope("p3", _sc_p3, False)
            _sc_p4 = nc.enter_named_scope("p4", False)[0]
            with nc.Block() as blk:
                @blk.tensor
                def _(tensor):
                    eng_enter(tensor)
                    for e in SCH:
                        if e[0] != "mm":
                            continue
                        (_, emit, deps, ee, emit_ev, evdeps, seq, bank,
                         ecnt, bdep) = e
                        for (snm, val) in deps:
                            S.wait(tensor, P4MAP[snm], val)
                        if bdep is not None:
                            S.wait(tensor, P4MAP[bdep[0]], bdep[1])
                        i = emit(tensor, pb[bank])
                        i.then_inc(S["p4g"], 1)
                    leave(tensor)

                def emit_cq(eng, enm, sem):
                    for e in SCH:
                        if e[0] == "mm" and e[3] == enm:
                            (_, emit, deps, ee, emit_ev, evdeps, seq, bank,
                             ecnt, bdep) = e
                            S.wait(eng, "p4g", seq)
                            for (snm, val) in evdeps:
                                S.wait(eng, P4MAP[snm], val)
                            i = emit_ev(eng, pb[bank])
                            if (sem, ecnt) in DRAIN_MARK:
                                i = eng.drain()
                            i.then_inc(S[sem], 1)
                        elif e[0] == "op" and e[1] == enm:
                            _, _, emit, deps, cnum, force = e
                            for (snm, val) in deps:
                                S.wait(eng, P4MAP[snm], val)
                            i = emit(eng)
                            if (sem, cnum) in DRAIN_MARK:
                                i = eng.drain()
                            i.then_inc(S[sem], 1)

                @blk.vector
                def _(vector):
                    eng_enter(vector)
                    emit_cq(vector, "v", "p4v")
                    leave(vector)

                @blk.scalar
                def _(scalar):
                    eng_enter(scalar)
                    emit_cq(scalar, "s", "p4s")
                    leave(scalar)

                @blk.gpsimd
                def _(gpsimd):
                    eng_enter(gpsimd)
                    for nm_, dst in (("w0m", s_W0), ("w1m", s_W1), ("w2m", s_W2)):
                        for jb in range(4):
                            S.inc(gpsimd.dma_start(
                                out=dst[:, 512 * jb:512 * (jb + 1)],
                                in_=wms[nm_][128 * jb:128 * (jb + 1), :]),
                                "p4i", 16)
                    for c in range(3):
                        S.inc(gpsimd.memset(s_M[c][:, :], 0), "p4im", 1)
                        S.inc(gpsimd.memset(s_MT[c][:, :], 0), "p4im", 1)
                    gpsimd.wait_ge(S["p4i"], S.val("p4i"))
                    gpsimd.wait_ge(S["p4im"], S.val("p4im"))
                    gpsimd.nop().then_inc(S["p4n"], 1)
                    S.cnt["p4n"] += 1
                    for e in SCH:
                        if e[0] != "dma":
                            continue
                        _, emit, deps, cnum = e
                        for (snm, val) in deps:
                            S.wait(gpsimd, P4MAP[snm], val)
                        i = emit(gpsimd)
                        i.then_inc(S["p4d"], 16)
                    gpsimd.wait_ge(S["p4d"], CNT["p4d"])
                    S.cnt["p4d"] = CNT["p4d"]
                    if N_CHUNKS == NCH:
                      with allow_ici_all_to_all():
                        i = gpsimd.collective_compute(
                            "AllToAll", ALU.bypass,
                            replica_groups=[list(range(8))],
                            ins=[ta_in.ap().opt()], outs=[ta_out.ap().opt()])
                      S.inc(i, "tac", 1, raw=True)
                    else:
                        S.inc(gpsimd.nop(), "tac", 1)
                    if _DEBUG:
                        gpsimd.wait_ge(S["p4v"], CNT["pv"])
                        gpsimd.wait_ge(S["p4s"], CNT["ps"])
                        if N_CHUNKS == NCH:
                            S.inc(gpsimd.dma_start(out=dbg["d_ttt"].ap(),
                                                   in_=ta_in.ap()), "dbgw", 16)
                        S.inc(gpsimd.dma_start(out=dbg["d_W0"].ap(),
                                               in_=s_W0[:, :]), "dbgw", 16)
                        S.inc(gpsimd.dma_start(out=dbg["d_M0"].ap(),
                                               in_=s_M[0][:, :]), "dbgw", 16)
                        S.inc(gpsimd.dma_start(out=dbg["d_X0"].ap(),
                                               in_=s_X[0]), "dbgw", 16)
                        S.inc(gpsimd.dma_start(out=dbg["d_A0"].ap(),
                                               in_=s_A[0]), "dbgw", 16)
                        if N_CHUNKS == 1:
                            S.inc(gpsimd.dma_start(
                                out=dbg["d_tm"][:, 0:4096],
                                in_=s_dgl0[:, :]), "dbgw", 16)
                            S.inc(gpsimd.dma_start(
                                out=dbg["d_tm"][:, 4096:8192],
                                in_=s_hl1[:, :]), "dbgw", 16)
                            for bi_, bs_ in enumerate((s_sg1[0], s_sg1[1],
                                                       s_hb1[0], s_hb1[1],
                                                       s_dh1[0], s_dh1[1])):
                                S.inc(gpsimd.dma_start(
                                    out=dbg["d_tm"][:, 8192 + 1024 * bi_:
                                                    8192 + 1024 * (bi_ + 1)]
                                    .bitcast(F32),
                                    in_=bs_[:, :]), "dbgw", 16)

                            S.inc(gpsimd.dma_start(
                                out=dbg["d_tm"][:, 14336:14352].bitcast(F32),
                                in_=s_l0[:, :]), "dbgw", 16)
                        S.inc(gpsimd.dma_start(
                            out=dbg["d_pj"][:, 0:24],
                            in_=pj_in[0:128, :]), "dbgw", 16)
                        S.inc(gpsimd.dma_start(
                            out=dbg["d_pj"][:, 24:48],
                            in_=pj_out[0:128, :]), "dbgw", 16)
                        S.inc(gpsimd.dma_start(
                            out=dbg["d_pj"][:, 48:53],
                            in_=pjsc[0:128, :]), "dbgw", 16)
                        S.inc(gpsimd.dma_start(
                            out=dbg["d_pj"][:, 53:58],
                            in_=pjsc[1024:1152, :]), "dbgw", 16)
                        for bi, bsrc in enumerate((s_qf, s_kf, s_vf, s_ktm,
                                                   s_vtm)):
                            S.inc(gpsimd.dma_start(
                                out=dbg["d_kv"][:, 4096 * bi:4096 * (bi + 1)],
                                in_=bsrc[:, :]), "dbgw", 16)
                        gpsimd.wait_ge(S["dbgw"], S.val("dbgw"))
                    leave(gpsimd)

                @blk.sync
                def _(sync):
                    eng_enter(sync)
                    leave(sync)

            barrier()
            p4.close()

        p5 = ExitStack()
        s_y = p5.enter_context(nc.sbuf_tensor("s_y", [128, 16 * C], BF16))
        s_tt = [p5.enter_context(nc.sbuf_tensor(f"s_tt{i}", [128, C], BF16))
                for i in range(2)]
        s_wo = p5.enter_context(nc.sbuf_tensor("s_wo", [128, 16 * 2048], BF16))
        s_wof = [p5.enter_context(nc.sbuf_tensor(f"s_wof{i}", [128, 2048], F32))
                 for i in range(2)]
        s_gl = [p5.enter_context(nc.sbuf_tensor(f"s_gl{i}", [128, D], BF16))
                for i in range(2)]
        s_xf = [p5.enter_context(nc.sbuf_tensor(f"s_xf{i}", [128, D], F32))
                for i in range(2)]
        s_of = [p5.enter_context(nc.sbuf_tensor(f"s_of{i}", [128, D], F32))
                for i in range(2)]

        if STAGE >= 4:
            nc.leave_named_scope("p4", _sc_p4, False)
        else:
            nc.leave_named_scope("p3", _sc_p3, False)
        _sc_p5 = nc.enter_named_scope("p5", False)[0]
        with nc.Block() as blk:
            @blk.gpsimd
            def _(gpsimd):
                eng_enter(gpsimd)
                P5TT = STAGE >= 4 and N_CHUNKS == NCH
                if STAGE >= 4:
                    S.wait(gpsimd, "tac", 1)
                for kd in range(16):
                    if STAGE >= 3:
                        S.inc(gpsimd.dma_start(
                            out=s_y[:, C * kd:C * (kd + 1)],
                            in_=attnscr[128 * kd:128 * (kd + 1), :]), "yld", 16)
                    else:
                        S.inc(gpsimd.memset(s_y[:, C * kd:C * (kd + 1)], 0),
                              "ymem", 1)
                    if P5TT:
                        hh, j = kd // 4, kd % 4
                        if kd >= 2:
                            S.wait(gpsimd, "ycb", kd - 1)
                        S.inc(gpsimd.dma_start(
                            out=s_tt[kd % 2][:, :],
                            in_=ta_out[512 * hh + 128 * j:512 * hh + 128 * (j + 1), :]),
                            "yld", 16)
                if S.val("yld"):
                    gpsimd.wait_ge(S["yld"], S.val("yld"))
                if S.val("ymem"):
                    gpsimd.wait_ge(S["ymem"], S.val("ymem"))
                gpsimd.nop().then_inc(S["ymd"], 1)
                S.cnt["ymd"] += 1
                for tt in range(8):
                    S.wait(gpsimd, "oev2", tt + 1)
                    S.inc(gpsimd.dma_start(
                        out=out_t[128 * tt:128 * (tt + 1), :],
                        in_=s_of[tt % 2][:, :]), "ost", 16)
                gpsimd.wait_ge(S["ost"], 8 * 16)
                leave(gpsimd)

            @blk.vector
            def _(vector):
                eng_enter(vector)
                if STAGE >= 4 and N_CHUNKS == NCH:
                    for kd in range(16):
                        vector.wait_ge(S["yld"], 32 * min(kd + 2, 16))
                        i = vector.tensor_add(s_y[:, C * kd:C * (kd + 1)],
                                              s_y[:, C * kd:C * (kd + 1)],
                                              s_tt[kd % 2][:, :])
                        S.inc(i, "ycb", 1)
                for kd in range(16):
                    vector.wait_ge(S["wold"], 16 * min(kd + 2, 16))
                    i = vector.tensor_copy(s_wo[:, 2048 * kd:2048 * (kd + 1)],
                                           s_wof[kd % 2][:, :])
                    S.inc(i, "wocv", 1)
                for tt in range(8):
                    vector.wait_ge(S["xrl"], 32 * min(tt + 2, 8))
                    S.wait(vector, "smm", tt + 1)
                    if tt >= 2:
                        S.wait(vector, "ost", 16 * tt)
                    g0 = (tt % 2) * 4
                    ob = s_of[tt % 2]
                    for nb in range(4):
                        i = vector.tensor_tensor(
                            ob[:, 512 * nb:512 * (nb + 1)], pb[g0 + nb][:, :],
                            s_gl[tt % 2][:, 512 * nb:512 * (nb + 1)], ALU.mult)
                    S.inc(i, "oev", 1)
                    for nb in range(4):
                        i = vector.tensor_add(
                            ob[:, 512 * nb:512 * (nb + 1)],
                            ob[:, 512 * nb:512 * (nb + 1)],
                            s_xf[tt % 2][:, 512 * nb:512 * (nb + 1)])
                    S.inc(i, "oev2", 1)
                leave(vector)

            @blk.sync
            def _(sync):
                eng_enter(sync)
                for kd in range(16):
                    if kd >= 2:
                        S.wait(sync, "wocv", kd - 1)
                    S.inc(sync.dma_start(out=s_wof[kd % 2][:, :],
                                         in_=WoI[128 * kd:128 * (kd + 1), :]),
                          "wold", 16)
                for tt in range(8):
                    if tt >= 2:
                        S.wait(sync, "oev2", tt - 1)
                    S.inc(sync.dma_start(out=s_xf[tt % 2][:, :],
                                         in_=x_in[128 * tt:128 * (tt + 1), :]),
                          "xrl", 16)
                    S.inc(sync.dma_start(out=s_gl[tt % 2][:, :],
                                         in_=gscr[128 * tt:128 * (tt + 1), :]),
                          "xrl", 16)
                leave(sync)

            @blk.tensor
            def _(tensor):
                eng_enter(tensor)
                S.wait(tensor, "ymd", 1)
                if STAGE >= 4 and N_CHUNKS == NCH:
                    S.wait(tensor, "ycb", 16)
                for tt in range(8):
                    g0 = (tt % 2) * 4
                    if tt >= 2:
                        S.wait(tensor, "oev", tt - 1)
                    for kd in range(16):
                        if tt == 0:
                            S.wait(tensor, "wocv", kd + 1)
                        lhs = s_y[:, C * kd + 128 * tt:C * kd + 128 * (tt + 1)]
                        for nb in range(4):
                            i = tensor.matmul(
                                pb[g0 + nb][:, :], lhs,
                                s_wo[:, 2048 * kd + 512 * nb:
                                     2048 * kd + 512 * (nb + 1)],
                                start=(kd == 0), stop=(kd == 15))
                    S.inc(i, "smm", 1)
                leave(tensor)

            @blk.scalar
            def _(scalar):
                eng_enter(scalar)
                leave(scalar)

        barrier()
        p5.close()
        nc.leave_named_scope("p5", _sc_p5, False)
    return nc


def _host_stage(inputs, core):
    h = core % 4
    x = np.ascontiguousarray(
        np.asarray(inputs["L0_attn_x"], dtype=np.float32).reshape(L, D)
        [C * core:C * (core + 1)])

    Wlr = np.asarray(inputs["L0_attn_Wlr"], dtype=np.float32)
    Wm = np.asarray(inputs["L0_attn_Wm"], dtype=np.float32)
    Ws = np.asarray(inputs["L0_attn_Ws"], dtype=np.float32)
    Wp = np.zeros((D, 24), np.float32)
    for hh in range(4):
        Wp[:, 6 * hh + 0:6 * hh + 3] = Wlr[:, 3 * hh:3 * hh + 3]
        Wp[:, 6 * hh + 3] = Wm[:, hh]
        Wp[:, 6 * hh + 4] = Ws[:, hh]

    def tile_gu(w):
        a = np.asarray(w).astype(np.uint8).reshape(NHB, 128, 16, 128)
        return np.ascontiguousarray(a.transpose(0, 3, 2, 1)).reshape(HID, D)

    wg = tile_gu(inputs["L0_ffn_wq_gate"])
    wu = tile_gu(inputs["L0_ffn_wq_up"])
    wd = np.ascontiguousarray(
        np.asarray(inputs["L0_ffn_wq_down"]).astype(np.uint8).T)

    sg = np.ascontiguousarray(np.asarray(inputs["L0_ffn_s_gate"],
                                         dtype=np.float32).reshape(NHB, 128).T)
    su = np.ascontiguousarray(np.asarray(inputs["L0_ffn_s_up"],
                                         dtype=np.float32).reshape(NHB, 128).T)
    sdB = np.ascontiguousarray(np.broadcast_to(
        np.asarray(inputs["L0_ffn_s_down"], dtype=np.float32)[None, :],
        (128, D)))
    ong = np.ascontiguousarray(np.asarray(inputs["L0_attn_o_norm_g"],
                                          dtype=np.float32).reshape(4, 128).T)
    qkp = np.zeros((128, 64), np.float32)
    qs = np.asarray(inputs["L0_attn_qk_scale"], dtype=np.float32)
    qo = np.asarray(inputs["L0_attn_qk_offset"], dtype=np.float32)
    qkp[:, 0:16] = qs[:, 0].reshape(16, 128).T
    qkp[:, 16:32] = qs[:, 1].reshape(16, 128).T
    qkp[:, 32:48] = qo[:, 0].reshape(16, 128).T
    qkp[:, 48:64] = qo[:, 1].reshape(16, 128).T

    w0 = np.asarray(inputs["L0_attn_w0"], dtype=np.float32)[h]
    w1 = np.asarray(inputs["L0_attn_w1"], dtype=np.float32)[h]
    w2 = np.asarray(inputs["L0_attn_w2"], dtype=np.float32)[h]

    dyn = np.zeros((1, 64), np.int32)
    dyn[0, 0] = 10240 * core
    dyn[0, 1] = 10240 * ((core + 7) % 8)
    dyn[0, 2] = 20480 * core
    dyn[0, 3] = 20480 * ((core + 7) % 8)
    dyn[0, 4] = HD * h
    dyn[0, 5] = h
    dyn[0, 6] = 6 * h

    return {
        "x": x, "Wqkv": np.asarray(inputs["L0_attn_Wqkv"], dtype=np.float32),
        "Wproj": Wp, "WgT": wg, "WuT": wu, "WdT": wd,
        "Wo": np.asarray(inputs["L0_attn_Wo"], dtype=np.float32),
        "w0m": np.ascontiguousarray(w0), "w0mT": np.ascontiguousarray(w0.T),
        "w1m": np.ascontiguousarray(w1), "w1mT": np.ascontiguousarray(w1.T),
        "w2m": np.ascontiguousarray(w2), "w2mT": np.ascontiguousarray(w2.T),
        "sg": sg, "su": su, "sdB": sdB, "ong": ong, "qkp": qkp,
        "bm1": np.full((128, 1), float(np.asarray(inputs["L0_attn_bm"])[h]),
                       np.float32),
        "bs1": np.full((128, 1), float(np.asarray(inputs["L0_attn_bs"])[h]),
                       np.float32),
        "pb1": np.full((128, 1), -200.0 if core == 0 else 0.0, np.float32),
        "dyn": dyn,
    }


def kernel(**inputs):
    nc = build()
    in_maps = [_host_stage(inputs, c) for c in range(8)]
    res = run_bass_kernel_spmd(nc, in_maps, core_ids=list(range(8)))
    out = np.concatenate([res.results[c]["out"] for c in range(8)], axis=0)
    return out.reshape(1, L, D).astype(np.float32)

